# version 07
# baseline (speedup 1.0000x reference)
import numpy as np
import ml_dtypes
from contextlib import ExitStack

import concourse.bass as bass
import concourse.mybir as mybir
from concourse.bass_utils import run_bass_kernel_spmd

F32 = mybir.dt.float32
BF16 = mybir.dt.bfloat16
I32 = mybir.dt.int32
AF = mybir.ActivationFunctionType
ALU = mybir.AluOpType

B, T, D = 2, 8192, 1024
NH, DH, E = 8, 64, 5
C, W = 64, 64
NCH = T // C
STEPS = C + W
REG = T // 4
TS1 = 128
NT1 = T // TS1
TS3 = 256
NT3 = T // TS3
EPS = 1e-5
NG = 22


class Sem:
    def __init__(self, nc, ctx, name):
        self.h = ctx.enter_context(nc.semaphore(name))
        self.n = 0

    def inc(self, ins, k=1):
        ins.then_inc(self.h, k)
        self.n += k
        return ins

    def dma(self, ins):
        return self.inc(ins, 16)


def build_nc():
    nc = bass.Bass("TRN2", target_bir_lowering=False)

    xT = nc.dram_tensor("xT", [D, T], F32, kind="ExternalInput").ap()
    xTown = nc.dram_tensor("xTown", [D, REG], F32, kind="ExternalInput").ap()
    qoff = nc.dram_tensor("qoff", [1, 1], I32, kind="ExternalInput").ap()
    WiT = nc.dram_tensor("WiT", [D, D], F32, kind="ExternalInput").ap()
    WhT = nc.dram_tensor("WhT", [D, D], F32, kind="ExternalInput").ap()
    Wq = nc.dram_tensor("Wq", [D, 512], F32, kind="ExternalInput").ap()
    Wk = nc.dram_tensor("Wk", [D, 512], F32, kind="ExternalInput").ap()
    Ws = nc.dram_tensor("Ws", [D, 40], F32, kind="ExternalInput").ap()
    Wd = nc.dram_tensor("Wd", [D, 40], F32, kind="ExternalInput").ap()
    Wvf = nc.dram_tensor("Wvf", [D, 320], F32, kind="ExternalInput").ap()
    WoS = nc.dram_tensor("WoS", [320, D], F32, kind="ExternalInput").ap()
    bi = nc.dram_tensor("bi", [128, 8], F32, kind="ExternalInput").ap()
    bh = nc.dram_tensor("bh", [128, 8], F32, kind="ExternalInput").ap()
    selc = nc.dram_tensor("selc", [40, 2560], F32, kind="ExternalInput").ap()
    idt = nc.dram_tensor("idt", [128, 128], F32, kind="ExternalInput").ap()
    outT = nc.dram_tensor("outT", [D, REG], F32, kind="ExternalOutput").ap()

    hsD = nc.dram_tensor("hsD", [128, 8 * T], BF16).ap()
    kTD = nc.dram_tensor("kTD", [512, T], BF16).ap()
    vD = nc.dram_tensor("vD", [T, 512], BF16).ap()

    hsD3 = hsD.rearrange("p (ch t) -> p ch t", ch=8)
    xT3 = xT.rearrange("(ch p) t -> p ch t", p=128)
    xTown3 = xTown.rearrange("(ch p) t -> p ch t", p=128)
    outT3 = outT.rearrange("(ch p) t -> p ch t", p=128)

    ctx = ExitStack()
    with ctx:
        SEMS = {}

        def sem(name):
            if name not in SEMS:
                SEMS[name] = Sem(nc, ctx, name)
            return SEMS[name]

        biS = ctx.enter_context(nc.sbuf_tensor("biS", [128, 8], F32))
        bb = ctx.enter_context(nc.sbuf_tensor("bb", [128, 16], F32))
        onesColB = ctx.enter_context(nc.sbuf_tensor("onesColB", [128, 1], BF16))
        onesRowF = ctx.enter_context(nc.sbuf_tensor("onesRowF", [1, 128], F32))
        ones64F = ctx.enter_context(nc.sbuf_tensor("ones64F", [1, 64], F32))
        IDb = ctx.enter_context(nc.sbuf_tensor("IDb", [128, 128], BF16))
        SELb = ctx.enter_context(nc.sbuf_tensor("SELb", [40, 2560], BF16))
        maskT = ctx.enter_context(nc.sbuf_tensor("maskT", [40, REG], BF16))

        block = ctx.enter_context(nc.Block())

        dW = sem("dW")
        cW = sem("cW")
        dOut = sem("dOut")

        with nc.sbuf_tensor("xiS", [128, 8 * 8193], BF16) as xiS:
            xiS3 = xiS[:].rearrange("p (ch t) -> p ch t", ch=8)
            p1xf, p1cast, p1sq = sem("p1xf"), sem("p1cast"), sem("p1sq")
            p1st, p1var, p1rstd = sem("p1st"), sem("p1var"), sem("p1rstd")
            p1bc, p1evb, p1tm = sem("p1bc"), sem("p1evb"), sem("p1tm")
            p1xn, p1xip, p1xie = sem("p1xn"), sem("p1xip"), sem("p1xie")

            with nc.sbuf_tensor("stage", [128, 2048], F32) as stage, \
                 nc.sbuf_tensor("WiTb", [128, 8 * 1024], BF16) as WiTb, \
                 nc.sbuf_tensor("xf0", [128, 8 * TS1], F32) as xf0, \
                 nc.sbuf_tensor("xf1", [128, 8 * TS1], F32) as xf1, \
                 nc.sbuf_tensor("xbf", [128, 2, 8 * TS1], BF16) as xbf, \
                 nc.sbuf_tensor("xsq", [128, 2, 8 * TS1], BF16) as xsq, \
                 nc.sbuf_tensor("xn1", [128, 2, 8 * TS1], BF16) as xn1, \
                 nc.sbuf_tensor("rows1", [1, 4 * TS1], F32) as rows1, \
                 nc.sbuf_tensor("bc1", [128, 2 * TS1], F32) as bc1, \
                 nc.sbuf_tensor("tmv1", [128, 2 * TS1], F32) as tmv1, \
                 nc.psum_tensor("psStA", [128, TS1], F32) as psStA, \
                 nc.psum_tensor("psStB", [128, TS1], F32) as psStB, \
                 nc.psum_tensor("psBc", [128, 2 * TS1], F32) as psBc, \
                 nc.psum_tensor("psXi0", [128, TS1], F32) as psXi0, \
                 nc.psum_tensor("psXi1", [128, TS1], F32) as psXi1:

                xf = [xf0, xf1]
                psXi = [psXi0, psXi1]
                RM, RV, RR, RMR = 0, TS1, 2 * TS1, 3 * TS1

                @block.sync
                def _(sync):
                    dW.dma(sync.dma_start(out=bb[:, 0:8], in_=bi))
                    dW.dma(sync.dma_start(out=bb[:, 8:16], in_=bh))
                    dW.dma(sync.dma_start(out=stage[:, 0:128], in_=idt))
                    dW.dma(sync.dma_start(out=stage[0:40, 128:1408], in_=selc[:, 0:1280]))
                    sync.wait_ge(cW.h, 1)
                    dW.dma(sync.dma_start(out=stage[0:40, 0:1280], in_=selc[:, 1280:2560]))
                    for r in range(8):
                        sync.wait_ge(cW.h, 2 + r)
                        dW.dma(sync.dma_start(
                            out=stage[:, 1024:2048],
                            in_=WiT.rearrange("(a p) b -> p a b", p=128)[:, r, :]))

                @block.vector
                def _(vector):
                    vector.wait_ge(dW.h, 64)
                    vector.tensor_add(out=biS[:], in0=bb[:, 0:8], in1=bb[:, 8:16])
                    vector.memset(onesColB[:], 1.0)
                    vector.memset(onesRowF[:], 1.0)
                    vector.memset(ones64F[:], 1.0)
                    vector.tensor_copy(out=IDb[:], in_=stage[:, 0:128])
                    cW.inc(vector.tensor_copy(out=SELb[:, 0:1280], in_=stage[0:40, 128:1408]))
                    vector.wait_ge(dW.h, 80)
                    cW.inc(vector.tensor_copy(out=SELb[:, 1280:2560], in_=stage[0:40, 0:1280]))
                    for r in range(8):
                        vector.wait_ge(dW.h, 80 + 16 * (r + 1))
                        cW.inc(vector.tensor_copy(
                            out=WiTb[:].rearrange("p (a b) -> p a b", a=8)[:, r, :],
                            in_=stage[:, 1024:2048]))
                CW_WI = cW.n

                @block.sync
                def _(sync):
                    for it in range(NT1):
                        if it >= 2:
                            sync.wait_ge(p1tm.h, 9 * (it - 1))
                        p1xf.dma(sync.dma_start(
                            out=xf[it % 2][:].rearrange("p (ch t) -> p ch t", ch=8),
                            in_=xT3[:, :, it * TS1:(it + 1) * TS1]))

                @block.vector
                def _(vector):
                    for it in range(NT1):
                        pb = it % 2
                        vector.wait_ge(p1xf.h, 16 * (it + 1))
                        if it >= 2:
                            vector.wait_ge(p1st.h, 2 * (it - 1))
                        p1cast.inc(vector.tensor_copy(out=xbf[:, pb, :], in_=xf[pb][:]))
                        vector.wait_ge(p1st.h, 2 * it + 1)
                        if it >= 1:
                            vector.wait_ge(p1bc.h, it)
                        vector.tensor_scalar_mul(out=rows1[:, RM:RM + TS1],
                                                 in0=psStA[0:1, :], scalar1=1.0 / 1024)
                        vector.wait_ge(p1st.h, 2 * it + 2)
                        vector.tensor_scalar(out=rows1[:, RV:RV + TS1],
                                                 in0=psStB[0:1, :], scalar1=1.0 / 1024,
                                                 scalar2=EPS, op0=ALU.mult, op1=ALU.add)
                        vector.tensor_mul(out=rows1[:, RMR:RMR + TS1],
                                          in0=rows1[:, RM:RM + TS1], in1=rows1[:, RM:RM + TS1])
                        p1var.inc(vector.tensor_sub(out=rows1[:, RV:RV + TS1],
                                                    in0=rows1[:, RV:RV + TS1],
                                                    in1=rows1[:, RMR:RMR + TS1]))
                        vector.wait_ge(p1rstd.h, it + 1)
                        p1tm.inc(vector.tensor_mul(out=rows1[:, RMR:RMR + TS1],
                                                   in0=rows1[:, RM:RM + TS1],
                                                   in1=rows1[:, RR:RR + TS1]))
                        vector.wait_ge(p1evb.h, 2 * (it + 1))
                        for chh in range(8):
                            if it >= 1 or chh >= 2:
                                vector.wait_ge(p1xn.h, 8 * it + chh - 1)
                            p1tm.inc(vector.tensor_mul(
                                out=tmv1[:, (chh % 2) * TS1:(chh % 2) * TS1 + TS1],
                                in0=xf[pb][:].rearrange("p (ch t) -> p ch t", ch=8)[:, chh, :],
                                in1=bc1[:, 0:TS1]))

                @block.scalar
                def _(scalar):
                    for it in range(NT1):
                        pb = it % 2
                        scalar.wait_ge(p1cast.h, it + 1)
                        p1sq.inc(scalar.activation(out=xsq[:, pb, :], in_=xbf[:, pb, :],
                                                   func=AF.Square))
                        scalar.wait_ge(p1var.h, it + 1)
                        scalar.activation(out=rows1[:, RR:RR + TS1],
                                          in_=rows1[:, RV:RV + TS1], func=AF.Ln)
                        p1rstd.inc(scalar.activation(out=rows1[:, RR:RR + TS1],
                                                     in_=rows1[:, RR:RR + TS1],
                                                     func=AF.Exp, scale=-0.5))
                        scalar.wait_ge(p1bc.h, it + 1)
                        if it >= 1:
                            scalar.wait_ge(p1xn.h, 8 * it)
                            scalar.wait_ge(p1tm.h, 9 * it)
                        p1evb.inc(scalar.copy(out=bc1[:, 0:TS1], in_=psBc[:, 0:TS1]))
                        p1evb.inc(scalar.copy(out=bc1[:, TS1:2 * TS1], in_=psBc[:, TS1:2 * TS1]))
                        for m in range(8):
                            scalar.wait_ge(p1xip.h, 8 * it + m + 1)
                            p1xie.inc(scalar.copy(
                                out=xiS3[:, m, 1 + it * TS1:1 + (it + 1) * TS1],
                                in_=psXi[m % 2][:, :]))

                @block.gpsimd
                def _(gpsimd):
                    for it in range(NT1):
                        pb = it % 2
                        for chh in range(8):
                            gpsimd.wait_ge(p1tm.h, 9 * it + chh + 2)
                            if it >= 2 and chh == 0:
                                gpsimd.wait_ge(p1xip.h, 8 * (it - 1))
                            p1xn.inc(gpsimd.tensor_sub(
                                out=xn1[:, pb, :].rearrange("p (ch t) -> p ch t", ch=8)[:, chh, :],
                                in0=tmv1[:, (chh % 2) * TS1:(chh % 2) * TS1 + TS1],
                                in1=bc1[:, TS1:2 * TS1]))

                @block.tensor
                def _(tensor):
                    tensor.wait_ge(cW.h, CW_WI)
                    for it in range(NT1):
                        pb = it % 2
                        tensor.wait_ge(p1cast.h, it + 1)
                        if it >= 1:
                            tensor.wait_ge(p1var.h, it)
                        for chh in range(8):
                            ins = tensor.matmul(
                                psStA[0:1, :], onesColB[:],
                                xbf[:, pb, :].rearrange("p (ch t) -> p ch t", ch=8)[:, chh, :],
                                start=(chh == 0), stop=(chh == 7))
                            if chh == 7:
                                p1st.inc(ins)
                        tensor.wait_ge(p1sq.h, it + 1)
                        for chh in range(8):
                            ins = tensor.matmul(
                                psStB[0:1, :], onesColB[:],
                                xsq[:, pb, :].rearrange("p (ch t) -> p ch t", ch=8)[:, chh, :],
                                start=(chh == 0), stop=(chh == 7))
                            if chh == 7:
                                p1st.inc(ins)
                        tensor.wait_ge(p1tm.h, 9 * it + 1)
                        if it >= 1:
                            tensor.wait_ge(p1evb.h, 2 * it)
                        tensor.matmul(psBc[:, 0:TS1], onesRowF[:],
                                      rows1[0:1, RR:RR + TS1], start=True, stop=True)
                        p1bc.inc(tensor.matmul(psBc[:, TS1:2 * TS1], onesRowF[:],
                                               rows1[0:1, RMR:RMR + TS1],
                                               start=True, stop=True))
                        tensor.wait_ge(p1xn.h, 8 * (it + 1))
                        for m in range(8):
                            if it * 8 + m >= 2:
                                tensor.wait_ge(p1xie.h, 8 * it + m - 1)
                            for dch in range(8):
                                ins = tensor.matmul(
                                    psXi[m % 2][:, :],
                                    WiTb[:].rearrange("p (a b) -> p a b", a=8)[:, dch, m * 128:(m + 1) * 128],
                                    xn1[:, pb, :].rearrange("p (ch t) -> p ch t", ch=8)[:, dch, :],
                                    start=(dch == 0), stop=(dch == 7))
                                if dch == 7:
                                    p1xip.inc(ins)
            P1_XIE = 8 * NT1

            s_ps, s_z, s_h, s_hz = sem("s_ps"), sem("s_z"), sem("s_h"), sem("s_hz")
            with nc.sbuf_tensor("stg2", [128, 2048], F32) as stg2, \
                 nc.sbuf_tensor("WhTb", [128, 64 * 128], BF16) as WhTb, \
                 nc.sbuf_tensor("wb0", [128, 8 * 128], BF16) as wb0, \
                 nc.sbuf_tensor("wb1", [128, 8 * 128], BF16) as wb1, \
                 nc.sbuf_tensor("zb", [128, 8 * 128], F32) as zb:
                p2ctx = ExitStack()
                ps2 = [p2ctx.enter_context(nc.psum_tensor(f"ps2_{j}", [128, 128], F32))
                       for j in range(8)]
                wb = [wb0, wb1]

                @block.sync
                def _(sync):
                    sync.wait_ge(p1xie.h, P1_XIE)
                    for r in range(4):
                        if r > 0:
                            sync.wait_ge(cW.h, CW_WI + r)
                        dW.dma(sync.dma_start(
                            out=stg2[:].rearrange("p (a b) -> p a b", a=2),
                            in_=WhT.rearrange("(a p) b -> p a b", p=128)[:, 2 * r:2 * r + 2, :]))

                @block.gpsimd
                def _(gpsimd):
                    for r in range(4):
                        gpsimd.wait_ge(dW.h, 208 + 16 * (r + 1))
                        for a in range(2):
                            ins = gpsimd.tensor_copy(
                                out=WhTb[:].rearrange("p (a b) -> p a b", a=64)
                                    [:, (2 * r + a) * 8:(2 * r + a) * 8 + 8, :],
                                in_=stg2[:].rearrange("p (a b) -> p a b", a=2)[:, a, :]
                                    .rearrange("p (a b) -> p a b", a=8))
                            if a == 1:
                                cW.inc(ins)
                CW_WH = cW.n

                @block.vector
                def _(vector):
                    vector.wait_ge(p1xf.h, 16 * NT1)
                    vector.wait_ge(p1xie.h, P1_XIE)
                    s_h.inc(vector.memset(wb0[:, 0:512], 0.0))
                    s_h.inc(vector.memset(wb0[:, 512:1024], 0.0))
                    vector.memset(xiS3[:, :, 0:1], 0.0)
                    for s in range(STEPS):
                        xio = s * 128 if s < W else 1 + (s - W) * 128
                        for j in range(8):
                            vector.wait_ge(s_ps.h, s * 8 + j + 1)
                            if s >= 1:
                                vector.wait_ge(s_h.h, 2 * s + (1 if j < 4 else 2))
                            s_z.inc(vector.scalar_tensor_tensor(
                                out=zb[:, j * 128:(j + 1) * 128],
                                in0=ps2[j][:, :], scalar=biS[:, j:j + 1],
                                in1=xiS3[:, j, xio:xio + 128],
                                op0=ALU.add, op1=ALU.add))
                        if s == W - 1:
                            vector.wait_ge(s_h.h, 2 + 2 * W)
                            s_hz.inc(vector.memset(
                                wb[0][:].rearrange("p (ch c) -> p ch c", ch=8)[:, :, 0:1], 0.0))

                @block.scalar
                def _(scalar):
                    for s in range(STEPS):
                        dst = wb[(s + 1) % 2]
                        for half in range(2):
                            scalar.wait_ge(s_z.h, s * 8 + 4 * (half + 1))
                            if half == 0 and s >= 66:
                                scalar.wait_ge(dOut.h, 16 * (s - 65))
                            s_h.inc(scalar.activation(
                                out=dst[:, half * 512:(half + 1) * 512],
                                in_=zb[:, half * 512:(half + 1) * 512], func=AF.Tanh))

                @block.tensor
                def _(tensor):
                    tensor.wait_ge(cW.h, CW_WH)
                    for s in range(STEPS):
                        src = wb[s % 2]
                        for j in range(8):
                            if s > 0:
                                tensor.wait_ge(s_z.h, (s - 1) * 8 + j + 1)
                            if j == 0:
                                tensor.wait_ge(s_h.h, 2 * s + 1)
                                if s == W:
                                    tensor.wait_ge(s_hz.h, 1)
                            for i in range(8):
                                if j == 0 and i == 4:
                                    tensor.wait_ge(s_h.h, 2 * s + 2)
                                ins = tensor.matmul(
                                    ps2[j][:, :],
                                    WhTb[:, (i * 8 + j) * 128:(i * 8 + j) * 128 + 128],
                                    src[:, i * 128:(i + 1) * 128],
                                    start=(i == 0), stop=(i == 7))
                                if i == 7:
                                    s_ps.inc(ins)

                @block.gpsimd
                def _(gpsimd):
                    for s in range(W, STEPS):
                        gpsimd.wait_ge(s_h.h, 2 + 2 * s + 2)
                        dOut.dma(gpsimd.dma_start(
                            out=hsD3[:, :, (s - W) * 128:(s - W + 1) * 128],
                            in_=wb[(s + 1) % 2][:].rearrange("p (ch c) -> p ch c", ch=8)))
                p2ctx.close()
        SC_Z = STEPS * 8
        SC_H = 2 + 2 * STEPS
        DOUT_P2 = dOut.n

        dK, dV, dF = sem("dK"), sem("dV"), sem("dF")
        p3w = sem("p3w")
        with nc.sbuf_tensor("x2own", [128, 8 * REG], F32) as x2own, \
             nc.sbuf_tensor("qT", [128, 4 * REG], BF16) as qT_s, \
             nc.sbuf_tensor("Wo3", [128, 3 * 1024], BF16) as Wo3:

            p3x2 = sem("p3x2")
            with nc.sbuf_tensor("stW", [128, 4096], F32) as stW, \
                 nc.sbuf_tensor("Wqk", [128, 8 * 1024], BF16) as Wqk, \
                 nc.sbuf_tensor("Wvsd", [128, 8 * 400], BF16) as Wvsd:

                p3s = ExitStack()
                _S3 = lambda n, s, d: p3s.enter_context(nc.sbuf_tensor(n, s, d))
                _P3 = lambda n, s, d: p3s.enter_context(nc.psum_tensor(n, s, d))
                if True:
                    hsob = _S3("hsob", [128, 8 * 512], BF16)
                    xf20 = _S3("xf20", [128, 8 * TS3], F32)
                    xf21 = _S3("xf21", [128, 8 * TS3], F32)
                    hsf = _S3("hsf", [128, 8 * TS3], BF16)
                    x2bf = _S3("x2bf", [128, 8 * TS3], BF16)
                    x2sq = _S3("x2sq", [128, 8 * TS3], BF16)
                    xn2 = _S3("xn2", [128, 2, 8 * TS3], BF16)
                    rows3 = _S3("rows3", [1, 4 * TS3], F32)
                    bc3 = _S3("bc3", [128, 2 * TS3], F32)
                    tmv3 = _S3("tmv3", [128, 2 * TS3], F32)
                    kst = _S3("kst", [128, 2 * TS3], BF16)
                    xvb = _S3("xvb", [128, 8 * 320], BF16)
                    glf = _S3("glf", [128, 8 * 40], F32)
                    gsc = _S3("gsc", [128, 12 * 64], F32)
                    wvS = _S3("wvS", [128, 8 * 40], F32)
                    wvT = _S3("wvT", [128, 8 * 40], F32)
                    vst = _S3("vst", [128, 8 * 512], BF16)
                    psSt3A = _P3("psSt3A", [128, TS3], F32)
                    psSt3B = _P3("psSt3B", [128, TS3], F32)
                    psBc3 = _P3("psBc3", [128, 2 * TS3], F32)
                    psK0 = _P3("psK0", [128, TS3], F32)
                    psK1 = _P3("psK1", [128, TS3], F32)
                    psVG0 = _P3("psVG0", [128, 368], F32)
                    psVG1 = _P3("psVG1", [128, 368], F32)

                    psK = [psK0, psK1]
                    psVG = [psVG0, psVG1]
                    xf2 = [xf20, xf21]
                    p3xf, p3hs, p3cast, p3sq = sem("p3xf"), sem("p3hs"), sem("p3cast"), sem("p3sq")
                    p3st, p3var, p3rstd = sem("p3st"), sem("p3var"), sem("p3rstd")
                    p3bc, p3evb, p3tm, p3xn = sem("p3bc"), sem("p3evb"), sem("p3tm"), sem("p3xn")
                    p3kp, p3ke = sem("p3kp"), sem("p3ke")
                    p3vg, p3vgc, p3xv = sem("p3vg"), sem("p3vgc"), sem("p3xv")
                    p3sig, p3wv, p3vm = sem("p3sig"), sem("p3wv"), sem("p3vm")
                    p3own = sem("p3own")
                    RM, RV, RR, RMR = 0, TS3, 2 * TS3, 3 * TS3

                    @block.sync
                    def _(sync):
                        sync.wait_ge(s_z.h, SC_Z)
                        sync.wait_ge(s_h.h, SC_H)
                        sync.wait_ge(dOut.h, DOUT_P2)
                        p3w.dma(sync.dma_start(out=stW[:].rearrange("p (a b) -> p a b", a=8),
                                               in_=Wq.rearrange("(a p) b -> p a b", p=128)))
                        sync.wait_ge(cW.h, CW_WH + 1)
                        p3w.dma(sync.dma_start(out=stW[:].rearrange("p (a b) -> p a b", a=8),
                                               in_=Wk.rearrange("(a p) b -> p a b", p=128)))
                        sync.wait_ge(cW.h, CW_WH + 2)
                        p3w.dma(sync.dma_start(
                            out=stW[:].rearrange("p (a b) -> p a b", a=8)[:, :, 0:320],
                            in_=Wvf.rearrange("(a p) b -> p a b", p=128)))
                        sync.wait_ge(cW.h, CW_WH + 3)
                        p3w.dma(sync.dma_start(
                            out=stW[:].rearrange("p (a b) -> p a b", a=8)[:, :, 0:40],
                            in_=Ws.rearrange("(a p) b -> p a b", p=128)))
                        sync.wait_ge(cW.h, CW_WH + 4)
                        p3w.dma(sync.dma_start(
                            out=stW[:].rearrange("p (a b) -> p a b", a=8)[:, :, 0:40],
                            in_=Wd.rearrange("(a p) b -> p a b", p=128)))
                        sync.wait_ge(cW.h, CW_WH + 5)
                        for kc in range(3):
                            rows = 128 if kc < 2 else 64
                            p3w.dma(sync.dma_start(
                                out=stW[0:rows, kc * 1024:(kc + 1) * 1024],
                                in_=WoS[kc * 128:kc * 128 + rows, :]))
                        p3own.dma(sync.dma_start(
                            out=x2own[:].rearrange("p (ch t) -> p ch t", ch=8),
                            in_=xTown3))
                        with sync.register("rq") as rq:
                            sync.reg_load(rq, qoff)
                            q0 = sync.snap(rq, min_val=0, max_val=3 * REG)
                            for qc in range(4):
                                with sync.register(f"rq2_{qc}") as rq2:
                                    sync.reg_add(rq2, q0, qc * 512)
                                    q2 = sync.snap(rq2, min_val=0, max_val=4 * REG - 512)
                                    if qc >= 1:
                                        sync.wait_ge(p3x2.h, 8 * qc)
                                    p3own.dma(sync.dma_start(
                                        out=hsob[:].rearrange("p (ch t) -> p ch t", ch=8),
                                        in_=hsD3[:, :, bass.ds(q2, 512)]))
                        for it in range(NT3):
                            if it >= 2:
                                sync.wait_ge(p3tm.h, 9 * (it - 1))
                            p3xf.dma(sync.dma_start(
                                out=xf2[it % 2][:].rearrange("p (ch t) -> p ch t", ch=8),
                                in_=xT3[:, :, it * TS3:(it + 1) * TS3]))
                            if it >= 1:
                                sync.wait_ge(p3cast.h, 2 * it - 1)
                            p3hs.dma(sync.dma_start(
                                out=hsf[:].rearrange("p (ch t) -> p ch t", ch=8),
                                in_=hsD3[:, :, it * TS3:(it + 1) * TS3]))

                    @block.gpsimd
                    def _(gpsimd):
                        gpsimd.wait_ge(p3w.h, 16)
                        cW.inc(gpsimd.tensor_copy(
                            out=Wqk[:].rearrange("p (a b) -> p a b", a=8)[:, :, 0:512],
                            in_=stW[:].rearrange("p (a b) -> p a b", a=8)))
                        gpsimd.wait_ge(p3w.h, 32)
                        cW.inc(gpsimd.tensor_copy(
                            out=Wqk[:].rearrange("p (a b) -> p a b", a=8)[:, :, 512:1024],
                            in_=stW[:].rearrange("p (a b) -> p a b", a=8)))
                        gpsimd.wait_ge(p3w.h, 48)
                        cW.inc(gpsimd.tensor_copy(
                            out=Wvsd[:].rearrange("p (a b) -> p a b", a=8)[:, :, 0:320],
                            in_=stW[:].rearrange("p (a b) -> p a b", a=8)[:, :, 0:320]))
                        gpsimd.wait_ge(p3w.h, 64)
                        cW.inc(gpsimd.tensor_copy(
                            out=Wvsd[:].rearrange("p (a b) -> p a b", a=8)[:, :, 320:360],
                            in_=stW[:].rearrange("p (a b) -> p a b", a=8)[:, :, 0:40]))
                        gpsimd.wait_ge(p3w.h, 80)
                        cW.inc(gpsimd.tensor_copy(
                            out=Wvsd[:].rearrange("p (a b) -> p a b", a=8)[:, :, 360:400],
                            in_=stW[:].rearrange("p (a b) -> p a b", a=8)[:, :, 0:40]))
                        gpsimd.wait_ge(p3w.h, 128)
                        gpsimd.memset(Wo3[64:128, 2 * 1024:3 * 1024], 0.0)
                        for kc in range(3):
                            rows = 128 if kc < 2 else 64
                            cW.inc(gpsimd.tensor_copy(
                                out=Wo3[0:rows, kc * 1024:(kc + 1) * 1024],
                                in_=stW[0:rows, kc * 1024:(kc + 1) * 1024]))
                        for it in range(NT3):
                            pb = it % 2
                            for chh in range(8):
                                gpsimd.wait_ge(p3tm.h, 9 * it + chh + 2)
                                if it >= 2 and chh == 0:
                                    gpsimd.wait_ge(p3kp.h, 4 * (it - 1))
                                    gpsimd.wait_ge(p3vg.h, 2 * (it - 1))
                                p3xn.inc(gpsimd.tensor_sub(
                                    out=xn2[:, pb, :].rearrange("p (ch t) -> p ch t", ch=8)[:, chh, :],
                                    in0=tmv3[:, (chh % 2) * TS3:(chh % 2) * TS3 + TS3],
                                    in1=bc3[:, TS3:2 * TS3]))
                            for mt in range(4):
                                gpsimd.wait_ge(p3ke.h, 4 * it + mt + 1)
                                dK.dma(gpsimd.dma_start(
                                    out=kTD[mt * 128:(mt + 1) * 128, it * TS3:(it + 1) * TS3],
                                    in_=kst[:, (mt % 2) * TS3:(mt % 2) * TS3 + TS3]))
                            if it % 4 == 3:
                                idx0 = (it // 4) * 8
                                for sl in range(8):
                                    gpsimd.wait_ge(p3vm.h, idx0 + sl + 1)
                                    dV.dma(gpsimd.dma_start(
                                        out=vD[(idx0 + sl) * 128:(idx0 + sl + 1) * 128, :],
                                        in_=vst[:, sl * 512:(sl + 1) * 512]))
                    CW_ATT = cW.n

                    @block.vector
                    def _(vector):
                        for qc in range(4):
                            vector.wait_ge(p3own.h, 16 + 16 * (qc + 1))
                            for chh in range(8):
                                p3x2.inc(vector.tensor_add(
                                    out=x2own[:].rearrange("p (ch t) -> p ch t", ch=8)
                                        [:, chh, qc * 512:(qc + 1) * 512],
                                    in0=x2own[:].rearrange("p (ch t) -> p ch t", ch=8)
                                        [:, chh, qc * 512:(qc + 1) * 512],
                                    in1=hsob[:].rearrange("p (ch t) -> p ch t", ch=8)[:, chh, :]))
                        for it in range(NT3):
                            pb = it % 2
                            vector.wait_ge(p3xf.h, 16 * (it + 1))
                            vector.wait_ge(p3hs.h, 16 * (it + 1))
                            if it >= 1:
                                vector.wait_ge(p3st.h, 2 * it)
                            p3cast.inc(vector.tensor_add(out=xf2[pb][:], in0=xf2[pb][:],
                                                         in1=hsf[:]))
                            p3cast.inc(vector.tensor_copy(out=x2bf[:], in_=xf2[pb][:]))
                            vector.wait_ge(p3st.h, 2 * it + 1)
                            if it >= 1:
                                vector.wait_ge(p3bc.h, it)
                            vector.tensor_scalar_mul(out=rows3[:, RM:RM + TS3],
                                                     in0=psSt3A[0:1, :], scalar1=1.0 / 1024)
                            vector.wait_ge(p3st.h, 2 * it + 2)
                            vector.tensor_scalar(out=rows3[:, RV:RV + TS3],
                                                     in0=psSt3B[0:1, :], scalar1=1.0 / 1024,
                                                     scalar2=EPS, op0=ALU.mult, op1=ALU.add)
                            vector.tensor_mul(out=rows3[:, RMR:RMR + TS3],
                                              in0=rows3[:, RM:RM + TS3], in1=rows3[:, RM:RM + TS3])
                            p3var.inc(vector.tensor_sub(out=rows3[:, RV:RV + TS3],
                                                        in0=rows3[:, RV:RV + TS3],
                                                        in1=rows3[:, RMR:RMR + TS3]))
                            vector.wait_ge(p3rstd.h, it + 1)
                            p3tm.inc(vector.tensor_mul(out=rows3[:, RMR:RMR + TS3],
                                                       in0=rows3[:, RM:RM + TS3],
                                                       in1=rows3[:, RR:RR + TS3]))
                            vector.wait_ge(p3evb.h, 2 * (it + 1))
                            for chh in range(8):
                                if it >= 1 or chh >= 2:
                                    vector.wait_ge(p3xn.h, 8 * it + chh - 1)
                                p3tm.inc(vector.tensor_mul(
                                    out=tmv3[:, (chh % 2) * TS3:(chh % 2) * TS3 + TS3],
                                    in0=xf2[pb][:].rearrange("p (ch t) -> p ch t", ch=8)[:, chh, :],
                                    in1=bc3[:, 0:TS3]))
                            for sub in range(2):
                                idx = 2 * it + sub
                                sl = idx % 8
                                vector.wait_ge(p3vg.h, idx + 1)
                                if idx >= 8:
                                    vector.wait_ge(p3vm.h, idx - 7)
                                p3vgc.inc(vector.tensor_copy(out=glf[:, sl * 40:(sl + 1) * 40],
                                                             in_=psVG[sub][:, 320:360]))
                            if it % 4 == 3:
                                g4 = it // 4
                                idx0 = g4 * 8
                                vector.wait_ge(p3sig.h, idx0 + 8)
                                vector.wait_ge(p3xv.h, idx0 + 8)
                                if g4 >= 1:
                                    vector.wait_ge(dV.h, 16 * idx0)
                                vector.tensor_scalar(out=wvS[:], in0=wvT[:],
                                                     scalar1=0.5, scalar2=0.5,
                                                     op0=ALU.mult, op1=ALU.add)
                                gv = glf[:].rearrange("p (s h e) -> p e s h", e=5, h=8)
                                wv_ = wvS[:].rearrange("p (s h e) -> p e s h", e=5, h=8)
                                t_ = gsc[:].rearrange("p (a sh) -> p a sh", a=12)
                                g = [gv[:, e, :, :].rearrange("p s h -> p (s h)") for e in range(5)]
                                def _tt3(o, a, b, op):
                                    vector.tensor_tensor(out=o, in0=a, in1=b, op=op)
                                _tt3(t_[:, 0, :], g[0], g[1], ALU.max)
                                _tt3(t_[:, 3, :], g[4], g[3], ALU.max)
                                _tt3(t_[:, 1, :], t_[:, 0, :], g[2], ALU.max)
                                _tt3(t_[:, 4, :], t_[:, 3, :], g[2], ALU.max)
                                _tt3(t_[:, 2, :], t_[:, 1, :], g[3], ALU.max)
                                _tt3(t_[:, 5, :], t_[:, 4, :], g[1], ALU.max)
                                _tt3(t_[:, 6, :], g[0], t_[:, 4, :], ALU.max)
                                _tt3(t_[:, 7, :], t_[:, 0, :], t_[:, 3, :], ALU.max)
                                _tt3(t_[:, 8, :], t_[:, 1, :], g[4], ALU.max)
                                _tt3(t_[:, 5, :], g[0], t_[:, 5, :], ALU.min)
                                _tt3(t_[:, 6, :], g[1], t_[:, 6, :], ALU.min)
                                _tt3(t_[:, 7, :], g[2], t_[:, 7, :], ALU.min)
                                _tt3(t_[:, 8, :], g[3], t_[:, 8, :], ALU.min)
                                _tt3(t_[:, 9, :], g[4], t_[:, 2, :], ALU.min)
                                _tt3(t_[:, 5, :], t_[:, 5, :], t_[:, 6, :], ALU.max)
                                _tt3(t_[:, 7, :], t_[:, 7, :], t_[:, 8, :], ALU.max)
                                _tt3(t_[:, 5, :], t_[:, 5, :], t_[:, 7, :], ALU.max)
                                _tt3(t_[:, 5, :], t_[:, 5, :], t_[:, 9, :], ALU.max)
                                for e in range(5):
                                    _tt3(t_[:, 10 + (e % 2), :], g[e], t_[:, 5, :], ALU.is_ge)
                                    if e >= 1:
                                        we = e - 1
                                        vector.tensor_mul(
                                            out=wv_[:, we, :, :].rearrange("p s h -> p (s h)"),
                                            in0=t_[:, 10 + (we % 2), :],
                                            in1=wv_[:, we, :, :].rearrange("p s h -> p (s h)"))
                                vector.tensor_mul(
                                    out=wv_[:, 4, :, :].rearrange("p s h -> p (s h)"),
                                    in0=t_[:, 10, :],
                                    in1=wv_[:, 4, :, :].rearrange("p s h -> p (s h)"))
                                for sl in range(8):
                                    for hh in range(8):
                                        for e in range(5):
                                            if e == 0:
                                                ins = vector.tensor_scalar_mul(
                                                    out=vst[:, sl * 512 + hh * 64:sl * 512 + (hh + 1) * 64],
                                                    in0=xvb[:, sl * 320 + e * 64:sl * 320 + (e + 1) * 64],
                                                    scalar1=wvS[:, sl * 40 + hh * 5 + e:sl * 40 + hh * 5 + e + 1])
                                            else:
                                                ins = vector.scalar_tensor_tensor(
                                                    out=vst[:, sl * 512 + hh * 64:sl * 512 + (hh + 1) * 64],
                                                    in0=xvb[:, sl * 320 + e * 64:sl * 320 + (e + 1) * 64],
                                                    scalar=wvS[:, sl * 40 + hh * 5 + e:sl * 40 + hh * 5 + e + 1],
                                                    in1=vst[:, sl * 512 + hh * 64:sl * 512 + (hh + 1) * 64],
                                                    op0=ALU.mult, op1=ALU.add)
                                    p3vm.inc(ins)

                    @block.scalar
                    def _(scalar):
                        for it in range(NT3):
                            pb = it % 2
                            scalar.wait_ge(p3cast.h, 2 * (it + 1))
                            p3sq.inc(scalar.activation(out=x2sq[:], in_=x2bf[:],
                                                       func=AF.Square))
                            scalar.wait_ge(p3var.h, it + 1)
                            scalar.activation(out=rows3[:, RR:RR + TS3],
                                              in_=rows3[:, RV:RV + TS3], func=AF.Ln)
                            p3rstd.inc(scalar.activation(out=rows3[:, RR:RR + TS3],
                                                         in_=rows3[:, RR:RR + TS3],
                                                         func=AF.Exp, scale=-0.5))
                            scalar.wait_ge(p3bc.h, it + 1)
                            if it >= 1:
                                scalar.wait_ge(p3xn.h, 8 * it)
                                scalar.wait_ge(p3tm.h, 9 * it)
                            p3evb.inc(scalar.copy(out=bc3[:, 0:TS3], in_=psBc3[:, 0:TS3]))
                            p3evb.inc(scalar.copy(out=bc3[:, TS3:2 * TS3],
                                                  in_=psBc3[:, TS3:2 * TS3]))
                            for mt in range(4):
                                scalar.wait_ge(p3kp.h, 4 * it + mt + 1)
                                if 4 * it + mt >= 2:
                                    scalar.wait_ge(dK.h, 16 * (4 * it + mt - 1))
                                p3ke.inc(scalar.copy(
                                    out=kst[:, (mt % 2) * TS3:(mt % 2) * TS3 + TS3],
                                    in_=psK[mt % 2][:, :]))
                            for sub in range(2):
                                idx = 2 * it + sub
                                sl = idx % 8
                                scalar.wait_ge(p3vg.h, idx + 1)
                                if idx >= 8:
                                    scalar.wait_ge(p3vm.h, idx - 7)
                                p3xv.inc(scalar.copy(out=xvb[:, sl * 320:(sl + 1) * 320],
                                                     in_=psVG[sub][:, 0:320]))
                                ins = scalar.activation(out=wvT[:, sl * 40:(sl + 1) * 40],
                                                        in_=psVG[sub][:, 320:360],
                                                        func=AF.Tanh, scale=0.5)
                                p3sig.inc(ins)
                                p3vgc.inc(scalar.nop())

                    @block.tensor
                    def _(tensor):
                        tensor.wait_ge(cW.h, CW_ATT)
                        for it in range(NT3):
                            pb = it % 2
                            tensor.wait_ge(p3cast.h, 2 * (it + 1))
                            if it >= 1:
                                tensor.wait_ge(p3var.h, it)
                            for chh in range(8):
                                ins = tensor.matmul(
                                    psSt3A[0:1, :], onesColB[:],
                                    x2bf[:].rearrange("p (ch t) -> p ch t", ch=8)[:, chh, :],
                                    start=(chh == 0), stop=(chh == 7))
                                if chh == 7:
                                    p3st.inc(ins)
                            tensor.wait_ge(p3sq.h, it + 1)
                            for chh in range(8):
                                ins = tensor.matmul(
                                    psSt3B[0:1, :], onesColB[:],
                                    x2sq[:].rearrange("p (ch t) -> p ch t", ch=8)[:, chh, :],
                                    start=(chh == 0), stop=(chh == 7))
                                if chh == 7:
                                    p3st.inc(ins)
                            tensor.wait_ge(p3tm.h, 9 * it + 1)
                            if it >= 1:
                                tensor.wait_ge(p3evb.h, 2 * it)
                            tensor.matmul(psBc3[:, 0:TS3], onesRowF[:],
                                          rows3[0:1, RR:RR + TS3], start=True, stop=True)
                            p3bc.inc(tensor.matmul(psBc3[:, TS3:2 * TS3], onesRowF[:],
                                                   rows3[0:1, RMR:RMR + TS3],
                                                   start=True, stop=True))
                            tensor.wait_ge(p3xn.h, 8 * (it + 1))
                            for mt in range(4):
                                if it * 4 + mt >= 2:
                                    tensor.wait_ge(p3ke.h, 4 * it + mt - 1)
                                for dch in range(8):
                                    ins = tensor.matmul(
                                        psK[mt % 2][:, :],
                                        Wqk[:].rearrange("p (a b) -> p a b", a=8)
                                            [:, dch, 512 + mt * 128:512 + (mt + 1) * 128],
                                        xn2[:, pb, :].rearrange("p (ch t) -> p ch t", ch=8)[:, dch, :],
                                        start=(dch == 0), stop=(dch == 7))
                                    if dch == 7:
                                        p3kp.inc(ins)
                            for sub in range(2):
                                idx = 2 * it + sub
                                if idx >= 2:
                                    tensor.wait_ge(p3vgc.h, 2 * (idx - 1))
                                for dch in range(8):
                                    ins = tensor.matmul(
                                        psVG[sub][:, :],
                                        xn2[:, pb, :].rearrange("p (ch t) -> p ch t", ch=8)
                                            [:, dch, sub * 128:(sub + 1) * 128],
                                        Wvsd[:].rearrange("p (a b) -> p a b", a=8)[:, dch, 0:368],
                                        start=(dch == 0), stop=(dch == 7))
                                    if dch == 7:
                                        p3vg.inc(ins)
                    P3T = dict(ke=p3ke.n, xn=p3xn.n, vm=p3vm.n, vg=p3vg.n,
                               dK=dK.n, dV=dV.n, tm=p3tm.n, st=p3st.n)
                p3s.close()

                p4cast, p4sq, p4st = sem("p4cast"), sem("p4sq"), sem("p4st")
                p4var, p4rstd, p4bc = sem("p4var"), sem("p4rstd"), sem("p4bc")
                p4evb, p4tm, p4xn = sem("p4evb"), sem("p4tm"), sem("p4xn")
                p4qp, p4qe = sem("p4qp"), sem("p4qe")
                p4gp, p4gv, p4mk, p4mt, p4mte = sem("p4gp"), sem("p4gv"), sem("p4mk"), sem("p4mt"), sem("p4mte")
                NIC4 = REG // TS3

                p4s = ExitStack()
                _S4 = lambda n, s, d: p4s.enter_context(nc.sbuf_tensor(n, s, d))
                _P4 = lambda n, s, d: p4s.enter_context(nc.psum_tensor(n, s, d))
                if True:
                    xn2o = _S4("xn2o", [128, 8 * REG], BF16)
                    xo_bf = _S4("xo_bf", [128, 2, 8 * TS3], BF16)
                    xo_sq = _S4("xo_sq", [128, 8 * TS3], BF16)
                    rows4 = _S4("rows4", [1, 4 * TS3], F32)
                    bc4 = _S4("bc4", [128, 2 * TS3], F32)
                    tmv4 = _S4("tmv4", [128, 2 * TS3], F32)
                    glo4 = _S4("glo4", [128, 16 * 40], F32)
                    gsc4 = _S4("gsc4", [128, 12 * 128], F32)
                    mko = _S4("mko", [128, 16 * 40], BF16)
                    psSt4A = _P4("psSt4A", [128, TS3], F32)
                    psSt4B = _P4("psSt4B", [128, TS3], F32)
                    psBc4 = _P4("psBc4", [128, 2 * TS3], F32)
                    psQ0 = _P4("psQ0", [128, TS3], F32)
                    psQ1 = _P4("psQ1", [128, TS3], F32)
                    psGO = _P4("psGO", [128, 40], F32)
                    psMT = _P4("psMT", [128, 128], BF16)

                    psQ = [psQ0, psQ1]
                    RM, RV, RR, RMR = 0, TS3, 2 * TS3, 3 * TS3
                    x2o3 = x2own[:].rearrange("p (ch t) -> p ch t", ch=8)
                    xn2o3 = xn2o[:].rearrange("p (ch t) -> p ch t", ch=8)

                    @block.gpsimd
                    def _(gpsimd):
                        gpsimd.wait_ge(p3ke.h, P3T["ke"])
                        gpsimd.wait_ge(p3vm.h, P3T["vm"])
                        gpsimd.wait_ge(p3sig.h, 2 * NT3)
                        gpsimd.wait_ge(dK.h, P3T["dK"])
                        gpsimd.wait_ge(dV.h, P3T["dV"])
                        gpsimd.wait_ge(p3x2.h, 32)
                        for ic in range(NIC4):
                            if ic >= 2:
                                gpsimd.wait_ge(p4st.h, 2 * (ic - 1))
                            for chh in range(8):
                                ins = gpsimd.tensor_copy(
                                    out=xo_bf[:, ic % 2, :].rearrange("p (ch t) -> p ch t", ch=8)[:, chh, :],
                                    in_=x2o3[:, chh, ic * TS3:(ic + 1) * TS3])
                                if chh == 7:
                                    p4cast.inc(ins)
                            for chh in range(8):
                                gpsimd.wait_ge(p4tm.h, 9 * ic + chh + 2)
                                p4xn.inc(gpsimd.tensor_sub(
                                    out=xn2o3[:, chh, ic * TS3:(ic + 1) * TS3],
                                    in0=tmv4[:, (chh % 2) * TS3:(chh % 2) * TS3 + TS3],
                                    in1=bc4[:, TS3:2 * TS3]))

                    @block.vector
                    def _(vector):
                        vector.wait_ge(p3xn.h, P3T["xn"])
                        vector.wait_ge(p3vg.h, P3T["vg"])
                        for ic in range(NIC4):
                            vector.wait_ge(p4st.h, 2 * ic + 1)
                            if ic >= 1:
                                vector.wait_ge(p4bc.h, ic)
                            vector.tensor_scalar_mul(out=rows4[:, RM:RM + TS3],
                                                     in0=psSt4A[0:1, :], scalar1=1.0 / 1024)
                            vector.wait_ge(p4st.h, 2 * ic + 2)
                            vector.tensor_scalar(out=rows4[:, RV:RV + TS3],
                                                     in0=psSt4B[0:1, :], scalar1=1.0 / 1024,
                                                     scalar2=EPS, op0=ALU.mult, op1=ALU.add)
                            vector.tensor_mul(out=rows4[:, RMR:RMR + TS3],
                                              in0=rows4[:, RM:RM + TS3], in1=rows4[:, RM:RM + TS3])
                            p4var.inc(vector.tensor_sub(out=rows4[:, RV:RV + TS3],
                                                        in0=rows4[:, RV:RV + TS3],
                                                        in1=rows4[:, RMR:RMR + TS3]))
                            vector.wait_ge(p4rstd.h, ic + 1)
                            p4tm.inc(vector.tensor_mul(out=rows4[:, RMR:RMR + TS3],
                                                       in0=rows4[:, RM:RM + TS3],
                                                       in1=rows4[:, RR:RR + TS3]))
                            vector.wait_ge(p4evb.h, 2 * (ic + 1))
                            for chh in range(8):
                                if ic >= 1 or chh >= 2:
                                    vector.wait_ge(p4xn.h, 8 * ic + chh - 1)
                                p4tm.inc(vector.tensor_mul(
                                    out=tmv4[:, (chh % 2) * TS3:(chh % 2) * TS3 + TS3],
                                    in0=x2o3[:, chh, ic * TS3:(ic + 1) * TS3],
                                    in1=bc4[:, 0:TS3]))
                        for st4 in range(16):
                            vector.wait_ge(p4gp.h, st4 + 1)
                            p4gv.inc(vector.tensor_copy(
                                out=glo4[:, st4 * 40:(st4 + 1) * 40], in_=psGO[:, 0:40]))
                        gv = glo4[:].rearrange("p (s h e) -> p e s h", e=5, h=8)
                        mk_ = mko[:].rearrange("p (s h e) -> p e s h", e=5, h=8)
                        t_ = gsc4[:].rearrange("p (a sh) -> p a sh", a=12)
                        g = [gv[:, e, :, :].rearrange("p s h -> p (s h)") for e in range(5)]
                        def _tt4(o, a, b, op):
                            vector.tensor_tensor(out=o, in0=a, in1=b, op=op)
                        _tt4(t_[:, 0, :], g[0], g[1], ALU.max)
                        _tt4(t_[:, 3, :], g[4], g[3], ALU.max)
                        _tt4(t_[:, 1, :], t_[:, 0, :], g[2], ALU.max)
                        _tt4(t_[:, 4, :], t_[:, 3, :], g[2], ALU.max)
                        _tt4(t_[:, 2, :], t_[:, 1, :], g[3], ALU.max)
                        _tt4(t_[:, 5, :], t_[:, 4, :], g[1], ALU.max)
                        _tt4(t_[:, 6, :], g[0], t_[:, 4, :], ALU.max)
                        _tt4(t_[:, 7, :], t_[:, 0, :], t_[:, 3, :], ALU.max)
                        _tt4(t_[:, 8, :], t_[:, 1, :], g[4], ALU.max)
                        _tt4(t_[:, 5, :], g[0], t_[:, 5, :], ALU.min)
                        _tt4(t_[:, 6, :], g[1], t_[:, 6, :], ALU.min)
                        _tt4(t_[:, 7, :], g[2], t_[:, 7, :], ALU.min)
                        _tt4(t_[:, 8, :], g[3], t_[:, 8, :], ALU.min)
                        _tt4(t_[:, 9, :], g[4], t_[:, 2, :], ALU.min)
                        _tt4(t_[:, 5, :], t_[:, 5, :], t_[:, 6, :], ALU.max)
                        _tt4(t_[:, 7, :], t_[:, 7, :], t_[:, 8, :], ALU.max)
                        _tt4(t_[:, 5, :], t_[:, 5, :], t_[:, 7, :], ALU.max)
                        _tt4(t_[:, 5, :], t_[:, 5, :], t_[:, 9, :], ALU.max)
                        for e in range(5):
                            ins = vector.tensor_tensor(
                                out=mk_[:, e, :, :].rearrange("p s h -> p (s h)"),
                                in0=g[e], in1=t_[:, 5, :], op=ALU.is_ge)
                        p4mk.inc(ins)
                        for st4 in range(16):
                            vector.wait_ge(p4mt.h, st4 + 1)
                            p4mte.inc(vector.tensor_copy(
                                out=maskT[:, st4 * 128:(st4 + 1) * 128], in_=psMT[0:40, :]))

                    @block.scalar
                    def _(scalar):
                        scalar.wait_ge(p3sig.h, 2 * NT3)
                        for ic in range(NIC4):
                            scalar.wait_ge(p4cast.h, ic + 1)
                            if ic >= 1:
                                scalar.wait_ge(p4st.h, 2 * ic)
                            p4sq.inc(scalar.activation(out=xo_sq[:], in_=xo_bf[:, ic % 2, :],
                                                       func=AF.Square))
                            scalar.wait_ge(p4var.h, ic + 1)
                            scalar.activation(out=rows4[:, RR:RR + TS3],
                                              in_=rows4[:, RV:RV + TS3], func=AF.Ln)
                            p4rstd.inc(scalar.activation(out=rows4[:, RR:RR + TS3],
                                                         in_=rows4[:, RR:RR + TS3],
                                                         func=AF.Exp, scale=-0.5))
                            scalar.wait_ge(p4bc.h, ic + 1)
                            if ic >= 1:
                                scalar.wait_ge(p4xn.h, 8 * ic)
                                scalar.wait_ge(p4tm.h, 9 * ic)
                            p4evb.inc(scalar.copy(out=bc4[:, 0:TS3], in_=psBc4[:, 0:TS3]))
                            p4evb.inc(scalar.copy(out=bc4[:, TS3:2 * TS3],
                                                  in_=psBc4[:, TS3:2 * TS3]))
                            for mt in range(4):
                                scalar.wait_ge(p4qp.h, 4 * ic + mt + 1)
                                p4qe.inc(scalar.activation(
                                    out=qT_s[:].rearrange("p (a t) -> p a t", a=4)
                                        [:, mt, ic * TS3:(ic + 1) * TS3],
                                    in_=psQ[mt % 2][:, :], func=AF.Copy, scale=0.125))

                    @block.tensor
                    def _(tensor):
                        tensor.wait_ge(p3st.h, P3T["st"])
                        for ic in range(NIC4):
                            tensor.wait_ge(p4cast.h, ic + 1)
                            if ic >= 1:
                                tensor.wait_ge(p4var.h, ic)
                            for chh in range(8):
                                ins = tensor.matmul(
                                    psSt4A[0:1, :], onesColB[:],
                                    xo_bf[:, ic % 2, :].rearrange("p (ch t) -> p ch t", ch=8)[:, chh, :],
                                    start=(chh == 0), stop=(chh == 7))
                                if chh == 7:
                                    p4st.inc(ins)
                            tensor.wait_ge(p4sq.h, ic + 1)
                            for chh in range(8):
                                ins = tensor.matmul(
                                    psSt4B[0:1, :], onesColB[:],
                                    xo_sq[:].rearrange("p (ch t) -> p ch t", ch=8)[:, chh, :],
                                    start=(chh == 0), stop=(chh == 7))
                                if chh == 7:
                                    p4st.inc(ins)
                            tensor.wait_ge(p4tm.h, 9 * ic + 1)
                            if ic >= 1:
                                tensor.wait_ge(p4evb.h, 2 * ic)
                            tensor.matmul(psBc4[:, 0:TS3], onesRowF[:],
                                          rows4[0:1, RR:RR + TS3], start=True, stop=True)
                            p4bc.inc(tensor.matmul(psBc4[:, TS3:2 * TS3], onesRowF[:],
                                                   rows4[0:1, RMR:RMR + TS3],
                                                   start=True, stop=True))
                            tensor.wait_ge(p4xn.h, 8 * (ic + 1))
                            for mt in range(4):
                                if ic * 4 + mt >= 2:
                                    tensor.wait_ge(p4qe.h, 4 * ic + mt - 1)
                                for dch in range(8):
                                    ins = tensor.matmul(
                                        psQ[mt % 2][:, :],
                                        Wqk[:].rearrange("p (a b) -> p a b", a=8)
                                            [:, dch, mt * 128:(mt + 1) * 128],
                                        xn2o3[:, dch, ic * TS3:(ic + 1) * TS3],
                                        start=(dch == 0), stop=(dch == 7))
                                    if dch == 7:
                                        p4qp.inc(ins)
                        for st4 in range(16):
                            if st4 >= 1:
                                tensor.wait_ge(p4gv.h, st4)
                            tensor.wait_ge(p4xn.h, 8 * (st4 // 2 + 1))
                            for dch in range(8):
                                ins = tensor.matmul(
                                    psGO[:, 0:40],
                                    xn2o3[:, dch, st4 * 128:(st4 + 1) * 128],
                                    Wvsd[:].rearrange("p (a b) -> p a b", a=8)[:, dch, 360:400],
                                    start=(dch == 0), stop=(dch == 7))
                                if dch == 7:
                                    p4gp.inc(ins)
                        tensor.wait_ge(p4mk.h, 1)
                        for st4 in range(16):
                            if st4 >= 1:
                                tensor.wait_ge(p4mte.h, st4)
                            p4mt.inc(tensor.transpose(psMT[0:40, :],
                                                      mko[:, st4 * 40:(st4 + 1) * 40], IDb[:]))
                P4_QE, P4_MTE, P4_XN = p4qe.n, p4mte.n, p4xn.n
                p4s.close()

            a_kk, a_v1, a_qq = sem("a_kk"), sem("a_v1"), sem("a_qq")
            a_qk, a_exp, a_pv = sem("a_qk"), sem("a_exp"), sem("a_pv")
            a_rec, a_bcd, a_anf, a_ann = sem("a_rec"), sem("a_bcd"), sem("a_anf"), sem("a_ann")
            a_sel, a_re, a_rs = sem("a_sel"), sem("a_re"), sem("a_rs")
            a_rsb, a_wo, a_fin = sem("a_rsb"), sem("a_wo"), sem("a_fin")

            p5s = ExitStack()
            _S5 = lambda n, s, d: p5s.enter_context(nc.sbuf_tensor(n, s, d))
            _P5 = lambda n, s, d: p5s.enter_context(nc.psum_tensor(n, s, d))
            if True:
                kk0 = _S5("kk0", [128, T], BF16)
                kk1 = _S5("kk1", [128, T], BF16)
                V650 = _S5("V650", [128, 64 * 66], BF16)
                V651 = _S5("V651", [128, 64 * 66], BF16)
                qq = _S5("qq", [128, REG], BF16)
                pT0 = _S5("pT0", [128, 1536], BF16)
                pT1 = _S5("pT1", [128, 1536], BF16)
                recrow = _S5("recrow", [1, 512], F32)
                attf = _S5("attf", [64, 512], F32)
                attn_n = _S5("attn_n", [64, 512], BF16)
                Rst = _S5("Rst", [128, 2 * 512], BF16)
                RSum = _S5("RSum", [128, 3 * REG], F32)
                RSb = _S5("RSb", [128, 3 * 512], BF16)
                ost = _S5("ost", [128, 2 * 512], F32)
                psS0 = _P5("psS0", [128, 1536], F32)
                psS1 = _P5("psS1", [128, 1536], F32)
                psATT = _P5("psATT", [128, 512], F32)
                psBR = _P5("psBR", [128, 512], F32)

                kk = [kk0, kk1]
                V65 = [V650, V651]
                pT = [pT0, pT1]
                psS = [psS0, psS1]
                RSum3 = RSum[:].rearrange("p (kc t) -> p kc t", kc=3)

                def grp_sizes():
                    gs = []
                    jp = 0
                    while jp < 64:
                        ng = min(3, 64 - jp)
                        gs.append((jp, ng))
                        jp += ng
                    return gs
                GS = grp_sizes()

                @block.sync
                def _(sync):
                    sync.wait_ge(p4qe.h, P4_QE)
                    sync.wait_ge(p4mte.h, P4_MTE)
                    for h in range(8):
                        if h >= 2:
                            sync.wait_ge(a_qk.h, 256 * (h - 1))
                        a_kk.dma(sync.dma_start(out=kk[h % 2][0:64, :],
                                                in_=kTD[h * 64:(h + 1) * 64, :]))
                        a_kk.dma(sync.dma_start(out=kk[h % 2][64:128, :],
                                                in_=kTD[h * 64:(h + 1) * 64, :]))
                        if h >= 2:
                            sync.wait_ge(a_pv.h, NG * 4 * (h - 1))
                        a_kk.dma(sync.dma_start(
                            out=V65[h % 2][:].rearrange("p (t c) -> p t c", c=66)[:, :, 0:64],
                            in_=vD.rearrange("(t p) d -> p t d", p=128)[:, :, h * 64:(h + 1) * 64]))

                @block.vector
                def _(vector):
                    vector.wait_ge(p4mte.h, P4_MTE)
                    vector.wait_ge(p4qe.h, P4_QE)
                    a_v1.inc(vector.memset(
                        V650[:].rearrange("p (t c) -> p t c", c=66)[:, :, 64:65], 1.0))
                    a_v1.inc(vector.memset(
                        V651[:].rearrange("p (t c) -> p t c", c=66)[:, :, 64:65], 1.0))
                    for h in range(8):
                        vector.wait_ge(a_qk.h, 256 * h)
                        a_qq.inc(vector.tensor_copy(
                            out=qq[0:64, :],
                            in_=qT_s[(h % 2) * 64:(h % 2) * 64 + 64,
                                     (h // 2) * REG:(h // 2 + 1) * REG]))
                        a_qq.inc(vector.tensor_copy(
                            out=qq[64:128, :],
                            in_=qT_s[(h % 2) * 64:(h % 2) * 64 + 64,
                                     (h // 2) * REG:(h // 2 + 1) * REG]))
                        for ic in range(4):
                            hic = h * 4 + ic
                            vector.wait_ge(a_pv.h, NG * (hic + 1))
                            if hic >= 1:
                                vector.wait_ge(a_bcd.h, hic)
                            a_rec.inc(vector.reciprocal(out=recrow[:], in_=psATT[64:65, :]))
                            vector.wait_ge(a_anf.h, hic + 1)
                            vector.wait_ge(a_bcd.h, hic + 1)
                            a_ann.inc(vector.tensor_mul(out=attn_n[:], in0=psBR[0:64, :],
                                                        in1=attf[:]))
                            for e in range(5):
                                vector.wait_ge(a_sel.h, 5 * hic + e + 1)
                                gp = 3 * hic + e // 2
                                if gp >= 2:
                                    vector.wait_ge(a_rs.h, gp + 2)
                                sl = ((e // 2) % 2) * 512
                                rb = 64 * (e % 2)
                                a_re.inc(vector.tensor_mul(
                                    out=Rst[rb:rb + 64, sl:sl + 512],
                                    in0=psBR[64:128, :], in1=attn_n[:]))

                @block.scalar
                def _(scalar):
                    for h in range(8):
                        for ic in range(4):
                            hic = h * 4 + ic
                            for g, (jp0, ng) in enumerate(GS):
                                gidx = hic * NG + g
                                scalar.wait_ge(a_qk.h, hic * 64 + jp0 + ng)
                                if gidx >= 2:
                                    scalar.wait_ge(a_pv.h, gidx - 1)
                                a_exp.inc(scalar.activation(
                                    out=pT[gidx % 2][:, 0:ng * 512],
                                    in_=psS[gidx % 2][:, 0:ng * 512], func=AF.Exp))
                            scalar.wait_ge(a_pv.h, NG * (hic + 1))
                            if hic >= 1:
                                scalar.wait_ge(a_ann.h, hic)
                            a_anf.inc(scalar.copy(out=attf[:], in_=psATT[0:64, :]))

                @block.tensor
                def _(tensor):
                    tensor.wait_ge(a_v1.h, 2)
                    for h in range(8):
                        tensor.wait_ge(a_kk.h, 48 * h + 32)
                        tensor.wait_ge(a_qq.h, 2 * (h + 1))
                        for ic in range(4):
                            hic = h * 4 + ic
                            for g, (jp0, ng) in enumerate(GS):
                                gidx = hic * NG + g
                                if gidx >= 2:
                                    tensor.wait_ge(a_exp.h, gidx - 1)
                                for jj in range(ng):
                                    jp = jp0 + jj
                                    jb = jp * 128
                                    tensor.matmul(
                                        psS[gidx % 2][0:64, jj * 512:(jj + 1) * 512],
                                        kk[h % 2][0:64, jb:jb + 64],
                                        qq[0:64, ic * 512:(ic + 1) * 512],
                                        start=True, stop=True, tile_position=(0, 0))
                                    ins = tensor.matmul(
                                        psS[gidx % 2][64:128, jj * 512:(jj + 1) * 512],
                                        kk[h % 2][64:128, jb + 64:jb + 128],
                                        qq[64:128, ic * 512:(ic + 1) * 512],
                                        start=True, stop=True, tile_position=(64, 64))
                                    a_qk.inc(ins)
                                if g >= 1:
                                    gprev = gidx - 1
                                    jpp, ngp = GS[g - 1]
                                    tensor.wait_ge(a_exp.h, gprev + 1)
                                    if g == 1:
                                        tensor.wait_ge(a_kk.h, 48 * (h + 1))
                                        if hic >= 1:
                                            tensor.wait_ge(a_rec.h, hic)
                                            tensor.wait_ge(a_anf.h, hic)
                                    for jj in range(ngp):
                                        jp = jpp + jj
                                        ins = tensor.matmul(
                                            psATT[0:65, :],
                                            V65[h % 2][:, jp * 66:jp * 66 + 65],
                                            pT[gprev % 2][:, jj * 512:(jj + 1) * 512],
                                            start=(jp == 0), stop=(jp == 63))
                                        if jj == ngp - 1:
                                            a_pv.inc(ins)
                            gidx = hic * NG + NG - 1
                            jpp, ngp = GS[NG - 1]
                            tensor.wait_ge(a_exp.h, gidx + 1)
                            for jj in range(ngp):
                                jp = jpp + jj
                                ins = tensor.matmul(
                                    psATT[0:65, :],
                                    V65[h % 2][:, jp * 66:jp * 66 + 65],
                                    pT[gidx % 2][:, jj * 512:(jj + 1) * 512],
                                    start=(jp == 0), stop=(jp == 63))
                                if jj == ngp - 1:
                                    a_pv.inc(ins)
                            tensor.wait_ge(a_rec.h, hic + 1)
                            if hic >= 1:
                                tensor.wait_ge(a_ann.h, hic)
                            a_bcd.inc(tensor.matmul(psBR[0:64, :], ones64F[:],
                                                    recrow[:], start=True, stop=True))
                            for e in range(5):
                                tensor.wait_ge(a_re.h, 5 * hic + e)
                                if e == 0:
                                    tensor.wait_ge(a_ann.h, hic + 1)
                                a_sel.inc(tensor.matmul(
                                    psBR[64:128, :],
                                    SELb[0:40, (h * 5 + e) * 64:(h * 5 + e) * 64 + 64],
                                    maskT[0:40, ic * 512:(ic + 1) * 512],
                                    start=True, stop=True, tile_position=(0, 64)))
                    for ic in range(4):
                        tensor.wait_ge(a_rsb.h, 3 * (ic + 1))
                        for dch in range(8):
                            gi = ic * 8 + dch
                            tensor.wait_ge(a_fin.h, gi)
                            for kc in range(3):
                                ins = tensor.matmul(
                                    psATT[:, :],
                                    Wo3[:, kc * 1024 + dch * 128:kc * 1024 + (dch + 1) * 128],
                                    RSb[:, kc * 512:(kc + 1) * 512],
                                    start=(kc == 0), stop=(kc == 2))
                                if kc == 2:
                                    a_wo.inc(ins)

                @block.gpsimd
                def _(gpsimd):
                    gpsimd.wait_ge(p4mte.h, P4_MTE)
                    for kc in range(3):
                        ins = gpsimd.memset(RSum3[:, kc, :], 0.0)
                        a_rs.inc(ins)
                    for h in range(8):
                        for ic in range(4):
                            hic = h * 4 + ic
                            for pi in range(3):
                                last_e = (pi * 2 + 1) if pi < 2 else 4
                                rows = 128 if pi < 2 else 64
                                gpsimd.wait_ge(a_re.h, 5 * hic + last_e + 1)
                                sl = (pi % 2) * 512
                                a_rs.inc(gpsimd.tensor_add(
                                    out=RSum3[0:rows, pi, ic * 512:(ic + 1) * 512],
                                    in0=RSum3[0:rows, pi, ic * 512:(ic + 1) * 512],
                                    in1=Rst[0:rows, sl:sl + 512]))
                    for ic in range(4):
                        for dch in range(8):
                            gi = ic * 8 + dch
                            gpsimd.wait_ge(a_fin.h, gi + 1)
                            dF.dma(gpsimd.dma_start(
                                out=outT3[:, dch, ic * 512:(ic + 1) * 512],
                                in_=ost[:, (gi % 2) * 512:(gi % 2) * 512 + 512]))
                    gpsimd.wait_ge(dF.h, 16 * 32)

                @block.scalar
                def _(scalar):
                    scalar.wait_ge(a_rs.h, 3 + 96)
                    for ic in range(4):
                        if ic >= 1:
                            scalar.wait_ge(a_wo.h, 8 * ic)
                        for kc in range(3):
                            a_rsb.inc(scalar.copy(out=RSb[:, kc * 512:(kc + 1) * 512],
                                                  in_=RSum3[:, kc, ic * 512:(ic + 1) * 512]))

                @block.vector
                def _(vector):
                    for ic in range(4):
                        for dch in range(8):
                            gi = ic * 8 + dch
                            vector.wait_ge(a_wo.h, gi + 1)
                            if gi >= 2:
                                vector.wait_ge(dF.h, 16 * (gi - 1))
                            a_fin.inc(vector.tensor_add(
                                out=ost[:, (gi % 2) * 512:(gi % 2) * 512 + 512],
                                in0=psATT[:, :],
                                in1=x2own[:].rearrange("p (ch t) -> p ch t", ch=8)
                                    [:, dch, ic * 512:(ic + 1) * 512]))
            p5s.close()
    return nc


_PCOL_TOK = None


def _tok_of_col():
    global _PCOL_TOK
    if _PCOL_TOK is None:
        j = np.arange(T)
        sp, c = j // 128, j % 128
        _PCOL_TOK = c * C + sp
    return _PCOL_TOK


def kernel(L0_attn_x, L0_attn_Wi, L0_attn_bi, L0_attn_Wh, L0_attn_bh,
           L0_ffn_Wq, L0_ffn_Wk, L0_ffn_Ws, L0_ffn_Wd, L0_ffn_Wv, L0_ffn_Wo):
    x = np.asarray(L0_attn_x, dtype=np.float32)
    Wi = np.asarray(L0_attn_Wi, dtype=np.float32)
    Wh = np.asarray(L0_attn_Wh, dtype=np.float32)
    biv = np.ascontiguousarray(np.asarray(L0_attn_bi, dtype=np.float32).reshape(8, 128).T)
    bhv = np.ascontiguousarray(np.asarray(L0_attn_bh, dtype=np.float32).reshape(8, 128).T)
    Wq = np.ascontiguousarray(np.asarray(L0_ffn_Wq, dtype=np.float32))
    Wk = np.ascontiguousarray(np.asarray(L0_ffn_Wk, dtype=np.float32))
    Ws = np.ascontiguousarray(np.asarray(L0_ffn_Ws, dtype=np.float32))
    Wd = np.ascontiguousarray(np.asarray(L0_ffn_Wd, dtype=np.float32))
    Wv = np.asarray(L0_ffn_Wv, dtype=np.float32)
    Wo = np.asarray(L0_ffn_Wo, dtype=np.float32)

    tok = _tok_of_col()
    xP = [np.ascontiguousarray(x[b].T[:, tok]) for b in range(B)]
    WiT = np.ascontiguousarray(Wi.T)
    WhT = np.ascontiguousarray(Wh.T)
    Wvf = np.ascontiguousarray(Wv.transpose(1, 0, 2).reshape(D, E * DH))
    WoS = np.ascontiguousarray(Wo.reshape(E * DH, D))
    selc = np.zeros((40, 2560), np.float32)
    for r in range(40):
        selc[r, r * 64:(r + 1) * 64] = 1.0
    idt = np.eye(128, dtype=np.float32)

    nc = build_nc()
    in_maps = []
    for k in range(8):
        b, q = k // 4, k % 4
        in_maps.append({
            "xT": xP[b],
            "xTown": np.ascontiguousarray(xP[b][:, q * REG:(q + 1) * REG]),
            "qoff": np.array([[q * REG]], np.int32),
            "WiT": WiT, "WhT": WhT, "Wq": Wq, "Wk": Wk, "Ws": Ws, "Wd": Wd,
            "Wvf": Wvf, "WoS": WoS, "bi": biv, "bh": bhv,
            "selc": selc, "idt": idt,
        })
    res = run_bass_kernel_spmd(nc, in_maps, core_ids=list(range(8)))
    out = np.empty((B, T, D), np.float32)
    for k in range(8):
        b, q = k // 4, k % 4
        cols = tok[q * REG:(q + 1) * REG]
        out[b, cols, :] = res.results[k]["outT"].T
    return out


# version 08
# speedup vs baseline: 1.0235x; 1.0235x over previous
import numpy as np
import ml_dtypes
from contextlib import ExitStack

import concourse.bass as bass
import concourse.mybir as mybir
from concourse.bass_utils import run_bass_kernel_spmd

F32 = mybir.dt.float32
BF16 = mybir.dt.bfloat16
I32 = mybir.dt.int32
AF = mybir.ActivationFunctionType
ALU = mybir.AluOpType

B, T, D = 2, 8192, 1024
NH, DH, E = 8, 64, 5
C, W = 64, 64
NCH = T // C
STEPS = C + W
REG = T // 4
TS1 = 256
NT1 = T // TS1
TS3 = 256
NT3 = T // TS3
EPS = 1e-5
NG = 32




class Sem:
    def __init__(self, nc, ctx, name):
        self.h = ctx.enter_context(nc.semaphore(name))
        self.n = 0

    def inc(self, ins, k=1):
        ins.then_inc(self.h, k)
        self.n += k
        return ins

    def dma(self, ins):
        return self.inc(ins, 16)


def build_nc():
    nc = bass.Bass("TRN2", target_bir_lowering=False)

    xT = nc.dram_tensor("xT", [D, T], F32, kind="ExternalInput").ap()
    xTown = nc.dram_tensor("xTown", [D, REG], F32, kind="ExternalInput").ap()
    qoff = nc.dram_tensor("qoff", [1, 1], I32, kind="ExternalInput").ap()
    WiT = nc.dram_tensor("WiT", [D, D], F32, kind="ExternalInput").ap()
    WhT = nc.dram_tensor("WhT", [D, D], F32, kind="ExternalInput").ap()
    Wq = nc.dram_tensor("Wq", [D, 512], F32, kind="ExternalInput").ap()
    Wk = nc.dram_tensor("Wk", [D, 512], F32, kind="ExternalInput").ap()
    Ws = nc.dram_tensor("Ws", [D, 40], F32, kind="ExternalInput").ap()
    Wd = nc.dram_tensor("Wd", [D, 40], F32, kind="ExternalInput").ap()
    Wvf = nc.dram_tensor("Wvf", [D, 320], F32, kind="ExternalInput").ap()
    WoS = nc.dram_tensor("WoS", [320, D], F32, kind="ExternalInput").ap()
    bi = nc.dram_tensor("bi", [128, 8], F32, kind="ExternalInput").ap()
    bh = nc.dram_tensor("bh", [128, 8], F32, kind="ExternalInput").ap()
    selc = nc.dram_tensor("selc", [40, 2560], F32, kind="ExternalInput").ap()
    idt = nc.dram_tensor("idt", [128, 128], F32, kind="ExternalInput").ap()
    outT = nc.dram_tensor("outT", [D, REG], F32, kind="ExternalOutput").ap()

    hsD = nc.dram_tensor("hsD", [128, 8 * T], BF16).ap()
    kTD = nc.dram_tensor("kTD", [512, T], BF16).ap()
    vD = nc.dram_tensor("vD", [T, 512], BF16).ap()

    hsD3 = hsD.rearrange("p (ch t) -> p ch t", ch=8)
    xT3 = xT.rearrange("(ch p) t -> p ch t", p=128)
    xTown3 = xTown.rearrange("(ch p) t -> p ch t", p=128)
    outT3 = outT.rearrange("(ch p) t -> p ch t", p=128)

    ctx = ExitStack()
    with ctx:
        SEMS = {}

        def sem(name):
            if name not in SEMS:
                SEMS[name] = Sem(nc, ctx, name)
            return SEMS[name]

        biS = ctx.enter_context(nc.sbuf_tensor("biS", [128, 8], F32))
        bb = ctx.enter_context(nc.sbuf_tensor("bb", [128, 16], F32))
        onesColB = ctx.enter_context(nc.sbuf_tensor("onesColB", [128, 1], BF16))
        onesRowF = ctx.enter_context(nc.sbuf_tensor("onesRowF", [1, 128], F32))
        ones64F = ctx.enter_context(nc.sbuf_tensor("ones64F", [1, 64], F32))
        IDb = ctx.enter_context(nc.sbuf_tensor("IDb", [128, 128], BF16))
        SELb = ctx.enter_context(nc.sbuf_tensor("SELb", [40, 2560], BF16))
        maskT = ctx.enter_context(nc.sbuf_tensor("maskT", [40, REG], BF16))

        block = ctx.enter_context(nc.Block())

        dW = sem("dW")
        cW = sem("cW")
        dOut = sem("dOut")

        with nc.sbuf_tensor("xiS", [128, 8 * 8193], BF16) as xiS:
            xiS3 = xiS[:].rearrange("p (ch t) -> p ch t", ch=8)
            p1xf, p1cast, p1sq = sem("p1xf"), sem("p1cast"), sem("p1sq")
            p1st, p1var, p1rstd = sem("p1st"), sem("p1var"), sem("p1rstd")
            p1bc, p1evb, p1tm = sem("p1bc"), sem("p1evb"), sem("p1tm")
            p1xn, p1xip, p1xie = sem("p1xn"), sem("p1xip"), sem("p1xie")

            with nc.sbuf_tensor("stage", [128, 1024], F32) as stage, \
                 nc.sbuf_tensor("WiTb", [128, 8 * 1024], BF16) as WiTb, \
                 nc.sbuf_tensor("xf0", [128, 8 * TS1], F32) as xf0, \
                 nc.sbuf_tensor("xf1", [128, 8 * TS1], F32) as xf1, \
                 nc.sbuf_tensor("xbf", [128, 8 * TS1], BF16) as xbf, \
                 nc.sbuf_tensor("xsq", [128, 8 * TS1], BF16) as xsq, \
                 nc.sbuf_tensor("xn1", [128, 2, 8 * TS1], BF16) as xn1, \
                 nc.sbuf_tensor("rows1", [1, 2, 4 * TS1], F32) as rows1, \
                 nc.sbuf_tensor("bc1", [128, 2, 2 * TS1], F32) as bc1, \
                 nc.sbuf_tensor("tmv1", [128, 2 * TS1], F32) as tmv1, \
                 nc.psum_tensor("psStA0", [128, TS1], F32) as psStA0, \
                 nc.psum_tensor("psStA1", [128, TS1], F32) as psStA1, \
                 nc.psum_tensor("psStB0", [128, TS1], F32) as psStB0, \
                 nc.psum_tensor("psStB1", [128, TS1], F32) as psStB1, \
                 nc.psum_tensor("psBc0", [128, 2 * TS1], F32) as psBc0, \
                 nc.psum_tensor("psBc1", [128, 2 * TS1], F32) as psBc1, \
                 nc.psum_tensor("psXi0", [128, TS1], F32) as psXi0, \
                 nc.psum_tensor("psXi1", [128, TS1], F32) as psXi1:

                xf = [xf0, xf1]
                psXi = [psXi0, psXi1]
                psStA = [psStA0, psStA1]
                psStB = [psStB0, psStB1]
                psBc = [psBc0, psBc1]
                RM, RV, RR, RMR = 0, TS1, 2 * TS1, 3 * TS1

                @block.sync
                def _(sync):
                    dW.dma(sync.dma_start(out=bb[:, 0:8], in_=bi))
                    dW.dma(sync.dma_start(out=bb[:, 8:16], in_=bh))
                    dW.dma(sync.dma_start(out=stage[:, 0:128], in_=idt))
                    dW.dma(sync.dma_start(out=stage[0:40, 128:1024], in_=selc[:, 0:896]))
                    sync.wait_ge(cW.h, 1)
                    dW.dma(sync.dma_start(out=stage[0:40, 0:896], in_=selc[:, 896:1792]))
                    sync.wait_ge(cW.h, 2)
                    dW.dma(sync.dma_start(out=stage[0:40, 0:768], in_=selc[:, 1792:2560]))
                    for r in range(8):
                        sync.wait_ge(cW.h, 3 + r)
                        dW.dma(sync.dma_start(
                            out=stage[:, 0:1024],
                            in_=WiT.rearrange("(a p) b -> p a b", p=128)[:, r, :]))

                @block.vector
                def _(vector):
                    vector.wait_ge(dW.h, 64)
                    vector.tensor_add(out=biS[:], in0=bb[:, 0:8], in1=bb[:, 8:16])
                    vector.memset(onesColB[:], 1.0)
                    vector.memset(onesRowF[:], 1.0)
                    vector.memset(ones64F[:], 1.0)
                    vector.tensor_copy(out=IDb[:], in_=stage[:, 0:128])
                    cW.inc(vector.tensor_copy(out=SELb[:, 0:896], in_=stage[0:40, 128:1024]))
                    vector.wait_ge(dW.h, 80)
                    cW.inc(vector.tensor_copy(out=SELb[:, 896:1792], in_=stage[0:40, 0:896]))
                    vector.wait_ge(dW.h, 96)
                    cW.inc(vector.tensor_copy(out=SELb[:, 1792:2560], in_=stage[0:40, 0:768]))
                    for r in range(8):
                        vector.wait_ge(dW.h, 96 + 16 * (r + 1))
                        cW.inc(vector.tensor_copy(
                            out=WiTb[:].rearrange("p (a b) -> p a b", a=8)[:, r, :],
                            in_=stage[:, 0:1024]))
                CW_WI = cW.n

                @block.sync
                def _(sync):
                    for it in range(NT1):
                        if it >= 2:
                            sync.wait_ge(p1tm.h, 9 * (it - 1))
                        p1xf.dma(sync.dma_start(
                            out=xf[it % 2][:].rearrange("p (ch t) -> p ch t", ch=8),
                            in_=xT3[:, :, it * TS1:(it + 1) * TS1]))

                @block.vector
                def _(vector):
                    for it in range(NT1):
                        pb = it % 2
                        vector.wait_ge(p1xf.h, 16 * (it + 1))
                        if it >= 1:
                            vector.wait_ge(p1st.h, 2 * it - 1)
                            vector.wait_ge(p1sq.h, it)
                        p1cast.inc(vector.tensor_copy(out=xbf[:], in_=xf[pb][:]))
                        r_ = rows1[:, pb, :]
                        vector.wait_ge(p1st.h, 2 * it + 1)
                        if it >= 2:
                            vector.wait_ge(p1bc.h, it - 1)
                        vector.tensor_scalar_mul(out=r_[:, RM:RM + TS1],
                                                 in0=psStA[pb][0:1, :], scalar1=1.0 / 1024)
                        vector.wait_ge(p1st.h, 2 * it + 2)
                        vector.tensor_scalar(out=r_[:, RV:RV + TS1],
                                                 in0=psStB[pb][0:1, :], scalar1=1.0 / 1024,
                                                 scalar2=EPS, op0=ALU.mult, op1=ALU.add)
                        vector.tensor_mul(out=r_[:, RMR:RMR + TS1],
                                          in0=r_[:, RM:RM + TS1], in1=r_[:, RM:RM + TS1])
                        p1var.inc(vector.tensor_sub(out=r_[:, RV:RV + TS1],
                                                    in0=r_[:, RV:RV + TS1],
                                                    in1=r_[:, RMR:RMR + TS1]))
                        vector.wait_ge(p1rstd.h, it + 1)
                        p1tm.inc(vector.tensor_mul(out=r_[:, RMR:RMR + TS1],
                                                   in0=r_[:, RM:RM + TS1],
                                                   in1=r_[:, RR:RR + TS1]))
                        vector.wait_ge(p1evb.h, 2 * (it + 1))
                        for chh in range(8):
                            if it >= 1 or chh >= 2:
                                vector.wait_ge(p1xn.h, 8 * it + chh - 1)
                            p1tm.inc(vector.tensor_mul(
                                out=tmv1[:, (chh % 2) * TS1:(chh % 2) * TS1 + TS1],
                                in0=xf[pb][:].rearrange("p (ch t) -> p ch t", ch=8)[:, chh, :],
                                in1=bc1[:, pb, 0:TS1]))

                @block.scalar
                def _(scalar):
                    for it in range(NT1):
                        pb = it % 2
                        scalar.wait_ge(p1cast.h, it + 1)
                        if it >= 1:
                            scalar.wait_ge(p1st.h, 2 * it)
                        p1sq.inc(scalar.activation(out=xsq[:], in_=xbf[:],
                                                   func=AF.Square))
                        r_ = rows1[:, pb, :]
                        scalar.wait_ge(p1var.h, it + 1)
                        scalar.activation(out=r_[:, RR:RR + TS1],
                                          in_=r_[:, RV:RV + TS1], func=AF.Ln)
                        p1rstd.inc(scalar.activation(out=r_[:, RR:RR + TS1],
                                                     in_=r_[:, RR:RR + TS1],
                                                     func=AF.Exp, scale=-0.5))
                        scalar.wait_ge(p1bc.h, it + 1)
                        if it >= 2:
                            scalar.wait_ge(p1xn.h, 8 * (it - 1))
                            scalar.wait_ge(p1tm.h, 9 * (it - 1))
                        p1evb.inc(scalar.copy(out=bc1[:, pb, 0:TS1], in_=psBc[pb][:, 0:TS1]))
                        p1evb.inc(scalar.copy(out=bc1[:, pb, TS1:2 * TS1], in_=psBc[pb][:, TS1:2 * TS1]))
                        for m in range(8):
                            scalar.wait_ge(p1xip.h, 8 * it + m + 1)
                            p1xie.inc(scalar.copy(
                                out=xiS3[:, m, 1 + it * TS1:1 + (it + 1) * TS1],
                                in_=psXi[m % 2][:, :]))

                @block.gpsimd
                def _(gpsimd):
                    for it in range(NT1):
                        pb = it % 2
                        for chh in range(8):
                            gpsimd.wait_ge(p1tm.h, 9 * it + chh + 2)
                            if it >= 2 and chh == 0:
                                gpsimd.wait_ge(p1xip.h, 8 * (it - 1))
                            p1xn.inc(gpsimd.tensor_sub(
                                out=xn1[:, pb, :].rearrange("p (ch t) -> p ch t", ch=8)[:, chh, :],
                                in0=tmv1[:, (chh % 2) * TS1:(chh % 2) * TS1 + TS1],
                                in1=bc1[:, pb, TS1:2 * TS1]))

                @block.tensor
                def _(tensor):
                    tensor.wait_ge(cW.h, CW_WI)
                    for it in range(NT1):
                        pb = it % 2
                        tensor.wait_ge(p1cast.h, it + 1)
                        if it >= 2:
                            tensor.wait_ge(p1var.h, it - 1)
                        for chh in range(8):
                            ins = tensor.matmul(
                                psStA[pb][0:1, :], onesColB[:],
                                xbf[:].rearrange("p (ch t) -> p ch t", ch=8)[:, chh, :],
                                start=(chh == 0), stop=(chh == 7))
                            if chh == 7:
                                p1st.inc(ins)
                        tensor.wait_ge(p1sq.h, it + 1)
                        for chh in range(8):
                            ins = tensor.matmul(
                                psStB[pb][0:1, :], onesColB[:],
                                xsq[:].rearrange("p (ch t) -> p ch t", ch=8)[:, chh, :],
                                start=(chh == 0), stop=(chh == 7))
                            if chh == 7:
                                p1st.inc(ins)
                        tensor.wait_ge(p1tm.h, 9 * it + 1)
                        if it >= 2:
                            tensor.wait_ge(p1evb.h, 2 * (it - 1))
                        tensor.matmul(psBc[pb][:, 0:TS1], onesRowF[:],
                                      rows1[0:1, pb, RR:RR + TS1], start=True, stop=True)
                        p1bc.inc(tensor.matmul(psBc[pb][:, TS1:2 * TS1], onesRowF[:],
                                               rows1[0:1, pb, RMR:RMR + TS1],
                                               start=True, stop=True))
                        tensor.wait_ge(p1xn.h, 8 * (it + 1))
                        for m in range(8):
                            if it * 8 + m >= 2:
                                tensor.wait_ge(p1xie.h, 8 * it + m - 1)
                            for dch in range(8):
                                ins = tensor.matmul(
                                    psXi[m % 2][:, :],
                                    WiTb[:].rearrange("p (a b) -> p a b", a=8)[:, dch, m * 128:(m + 1) * 128],
                                    xn1[:, pb, :].rearrange("p (ch t) -> p ch t", ch=8)[:, dch, :],
                                    start=(dch == 0), stop=(dch == 7))
                                if dch == 7:
                                    p1xip.inc(ins)
            P1_XIE = 8 * NT1

            s_ps, s_z, s_h, s_hz = sem("s_ps"), sem("s_z"), sem("s_h"), sem("s_hz")
            with nc.sbuf_tensor("stg2", [128, 2048], F32) as stg2, \
                 nc.sbuf_tensor("WhTb", [128, 64 * 128], BF16) as WhTb, \
                 nc.sbuf_tensor("wb0", [128, 8 * 128], BF16) as wb0, \
                 nc.sbuf_tensor("wb1", [128, 8 * 128], BF16) as wb1, \
                 nc.sbuf_tensor("zb", [128, 8 * 128], F32) as zb:
                p2ctx = ExitStack()
                ps2 = [p2ctx.enter_context(nc.psum_tensor(f"ps2_{j}", [128, 128], F32))
                       for j in range(8)]
                wb = [wb0, wb1]

                @block.sync
                def _(sync):
                    sync.wait_ge(p1xie.h, P1_XIE)
                    for r in range(4):
                        if r > 0:
                            sync.wait_ge(cW.h, CW_WI + r)
                        dW.dma(sync.dma_start(
                            out=stg2[:].rearrange("p (a b) -> p a b", a=2),
                            in_=WhT.rearrange("(a p) b -> p a b", p=128)[:, 2 * r:2 * r + 2, :]))

                @block.gpsimd
                def _(gpsimd):
                    for r in range(4):
                        gpsimd.wait_ge(dW.h, 224 + 16 * (r + 1))
                        for a in range(2):
                            ins = gpsimd.tensor_copy(
                                out=WhTb[:].rearrange("p (a b) -> p a b", a=64)
                                    [:, (2 * r + a) * 8:(2 * r + a) * 8 + 8, :],
                                in_=stg2[:].rearrange("p (a b) -> p a b", a=2)[:, a, :]
                                    .rearrange("p (a b) -> p a b", a=8))
                            if a == 1:
                                cW.inc(ins)
                CW_WH = cW.n

                @block.vector
                def _(vector):
                    vector.wait_ge(p1xf.h, 16 * NT1)
                    vector.wait_ge(p1xie.h, P1_XIE)
                    s_h.inc(vector.memset(wb0[:, 0:512], 0.0))
                    s_h.inc(vector.memset(wb0[:, 512:1024], 0.0))
                    vector.memset(xiS3[:, :, 0:1], 0.0)
                    for s in range(STEPS):
                        xio = s * 128 if s < W else 1 + (s - W) * 128
                        for j in range(8):
                            vector.wait_ge(s_ps.h, s * 8 + j + 1)
                            if s >= 1:
                                vector.wait_ge(s_h.h, 2 * s + (1 if j < 4 else 2))
                            s_z.inc(vector.scalar_tensor_tensor(
                                out=zb[:, j * 128:(j + 1) * 128],
                                in0=ps2[j][:, :], scalar=biS[:, j:j + 1],
                                in1=xiS3[:, j, xio:xio + 128],
                                op0=ALU.add, op1=ALU.add))
                        if s == W - 1:
                            vector.wait_ge(s_h.h, 2 + 2 * W)
                            s_hz.inc(vector.memset(
                                wb[0][:].rearrange("p (ch c) -> p ch c", ch=8)[:, :, 0:1], 0.0))

                @block.scalar
                def _(scalar):
                    for s in range(STEPS):
                        dst = wb[(s + 1) % 2]
                        for half in range(2):
                            scalar.wait_ge(s_z.h, s * 8 + 4 * (half + 1))
                            if half == 0 and s >= 66:
                                scalar.wait_ge(dOut.h, 16 * (s - 65))
                            s_h.inc(scalar.activation(
                                out=dst[:, half * 512:(half + 1) * 512],
                                in_=zb[:, half * 512:(half + 1) * 512], func=AF.Tanh))

                @block.tensor
                def _(tensor):
                    tensor.wait_ge(cW.h, CW_WH)
                    for s in range(STEPS):
                        src = wb[s % 2]
                        for j in range(8):
                            if s > 0:
                                tensor.wait_ge(s_z.h, (s - 1) * 8 + j + 1)
                            if j == 0:
                                tensor.wait_ge(s_h.h, 2 * s + 1)
                                if s == W:
                                    tensor.wait_ge(s_hz.h, 1)
                            for i in range(8):
                                if j == 0 and i == 4:
                                    tensor.wait_ge(s_h.h, 2 * s + 2)
                                ins = tensor.matmul(
                                    ps2[j][:, :],
                                    WhTb[:, (i * 8 + j) * 128:(i * 8 + j) * 128 + 128],
                                    src[:, i * 128:(i + 1) * 128],
                                    start=(i == 0), stop=(i == 7))
                                if i == 7:
                                    s_ps.inc(ins)

                @block.gpsimd
                def _(gpsimd):
                    for s in range(W, STEPS):
                        gpsimd.wait_ge(s_h.h, 2 + 2 * s + 2)
                        dOut.dma(gpsimd.dma_start(
                            out=hsD3[:, :, (s - W) * 128:(s - W + 1) * 128],
                            in_=wb[(s + 1) % 2][:].rearrange("p (ch c) -> p ch c", ch=8)))
                p2ctx.close()
        SC_Z = STEPS * 8
        SC_H = 2 + 2 * STEPS
        DOUT_P2 = dOut.n

        dK, dV, dF = sem("dK"), sem("dV"), sem("dF")
        p3w = sem("p3w")
        with nc.sbuf_tensor("x2own", [128, 8 * REG], F32) as x2own, \
             nc.sbuf_tensor("qT", [128, 4 * REG], BF16) as qT_s, \
             nc.sbuf_tensor("Wo3", [128, 3 * 1024], BF16) as Wo3:

            p3x2 = sem("p3x2")
            with nc.sbuf_tensor("stW", [128, 4096], F32) as stW, \
                 nc.sbuf_tensor("Wqk", [128, 8 * 1024], BF16) as Wqk, \
                 nc.sbuf_tensor("Wvsd", [128, 8 * 400], BF16) as Wvsd:

                p3s = ExitStack()
                _S3 = lambda n, s, d: p3s.enter_context(nc.sbuf_tensor(n, s, d))
                _P3 = lambda n, s, d: p3s.enter_context(nc.psum_tensor(n, s, d))
                if True:
                    hsob = _S3("hsob", [128, 8 * 512], BF16)
                    xf20 = _S3("xf20", [128, 8 * TS3], F32)
                    xf21 = _S3("xf21", [128, 8 * TS3], F32)
                    hsf = _S3("hsf", [128, 8 * TS3], BF16)
                    x2bf = _S3("x2bf", [128, 8 * TS3], BF16)
                    x2sq = _S3("x2sq", [128, 8 * TS3], BF16)
                    xn2 = _S3("xn2", [128, 2, 8 * TS3], BF16)
                    rows3 = _S3("rows3", [1, 4 * TS3], F32)
                    bc3 = _S3("bc3", [128, 2 * TS3], F32)
                    tmv3 = _S3("tmv3", [128, 2 * TS3], F32)
                    kst = _S3("kst", [128, 2 * TS3], BF16)
                    xvb = _S3("xvb", [128, 8 * 320], BF16)
                    glf = _S3("glf", [128, 8 * 40], F32)
                    gsc = _S3("gsc", [128, 12 * 64], F32)
                    wvS = _S3("wvS", [128, 8 * 40], F32)
                    wvT = _S3("wvT", [128, 8 * 40], F32)
                    vst = _S3("vst", [128, 8 * 512], BF16)
                    psSt3A = _P3("psSt3A", [128, TS3], F32)
                    psSt3B = _P3("psSt3B", [128, TS3], F32)
                    psBc3 = _P3("psBc3", [128, 2 * TS3], F32)
                    psK0 = _P3("psK0", [128, TS3], F32)
                    psK1 = _P3("psK1", [128, TS3], F32)
                    psVG0 = _P3("psVG0", [128, 368], F32)
                    psVG1 = _P3("psVG1", [128, 368], F32)

                    psK = [psK0, psK1]
                    psVG = [psVG0, psVG1]
                    xf2 = [xf20, xf21]
                    p3xf, p3hs, p3cast, p3sq = sem("p3xf"), sem("p3hs"), sem("p3cast"), sem("p3sq")
                    p3st, p3var, p3rstd = sem("p3st"), sem("p3var"), sem("p3rstd")
                    p3bc, p3evb, p3tm, p3xn = sem("p3bc"), sem("p3evb"), sem("p3tm"), sem("p3xn")
                    p3kp, p3ke = sem("p3kp"), sem("p3ke")
                    p3vg, p3vgc, p3xv = sem("p3vg"), sem("p3vgc"), sem("p3xv")
                    p3sig, p3wv, p3vm = sem("p3sig"), sem("p3wv"), sem("p3vm")
                    p3own = sem("p3own")
                    RM, RV, RR, RMR = 0, TS3, 2 * TS3, 3 * TS3

                    @block.sync
                    def _(sync):
                        sync.wait_ge(s_z.h, SC_Z)
                        sync.wait_ge(s_h.h, SC_H)
                        sync.wait_ge(dOut.h, DOUT_P2)
                        p3w.dma(sync.dma_start(out=stW[:].rearrange("p (a b) -> p a b", a=8),
                                               in_=Wq.rearrange("(a p) b -> p a b", p=128)))
                        sync.wait_ge(cW.h, CW_WH + 1)
                        p3w.dma(sync.dma_start(out=stW[:].rearrange("p (a b) -> p a b", a=8),
                                               in_=Wk.rearrange("(a p) b -> p a b", p=128)))
                        sync.wait_ge(cW.h, CW_WH + 2)
                        p3w.dma(sync.dma_start(
                            out=stW[:].rearrange("p (a b) -> p a b", a=8)[:, :, 0:320],
                            in_=Wvf.rearrange("(a p) b -> p a b", p=128)))
                        sync.wait_ge(cW.h, CW_WH + 3)
                        p3w.dma(sync.dma_start(
                            out=stW[:].rearrange("p (a b) -> p a b", a=8)[:, :, 0:40],
                            in_=Ws.rearrange("(a p) b -> p a b", p=128)))
                        sync.wait_ge(cW.h, CW_WH + 4)
                        p3w.dma(sync.dma_start(
                            out=stW[:].rearrange("p (a b) -> p a b", a=8)[:, :, 0:40],
                            in_=Wd.rearrange("(a p) b -> p a b", p=128)))
                        sync.wait_ge(cW.h, CW_WH + 5)
                        for kc in range(3):
                            rows = 128 if kc < 2 else 64
                            p3w.dma(sync.dma_start(
                                out=stW[0:rows, kc * 1024:(kc + 1) * 1024],
                                in_=WoS[kc * 128:kc * 128 + rows, :]))
                        p3own.dma(sync.dma_start(
                            out=x2own[:].rearrange("p (ch t) -> p ch t", ch=8),
                            in_=xTown3))
                        with sync.register("rq") as rq:
                            sync.reg_load(rq, qoff)
                            q0 = sync.snap(rq, min_val=0, max_val=3 * REG)
                            for qc in range(4):
                                with sync.register(f"rq2_{qc}") as rq2:
                                    sync.reg_add(rq2, q0, qc * 512)
                                    q2 = sync.snap(rq2, min_val=0, max_val=4 * REG - 512)
                                    if qc >= 1:
                                        sync.wait_ge(p3x2.h, 8 * qc)
                                    p3own.dma(sync.dma_start(
                                        out=hsob[:].rearrange("p (ch t) -> p ch t", ch=8),
                                        in_=hsD3[:, :, bass.ds(q2, 512)]))
                        for it in range(NT3):
                            if it >= 2:
                                sync.wait_ge(p3tm.h, 9 * (it - 1))
                            p3xf.dma(sync.dma_start(
                                out=xf2[it % 2][:].rearrange("p (ch t) -> p ch t", ch=8),
                                in_=xT3[:, :, it * TS3:(it + 1) * TS3]))
                            if it >= 1:
                                sync.wait_ge(p3cast.h, 2 * it - 1)
                            p3hs.dma(sync.dma_start(
                                out=hsf[:].rearrange("p (ch t) -> p ch t", ch=8),
                                in_=hsD3[:, :, it * TS3:(it + 1) * TS3]))

                    @block.gpsimd
                    def _(gpsimd):
                        gpsimd.wait_ge(p3w.h, 16)
                        cW.inc(gpsimd.tensor_copy(
                            out=Wqk[:].rearrange("p (a b) -> p a b", a=8)[:, :, 0:512],
                            in_=stW[:].rearrange("p (a b) -> p a b", a=8)))
                        gpsimd.wait_ge(p3w.h, 32)
                        cW.inc(gpsimd.tensor_copy(
                            out=Wqk[:].rearrange("p (a b) -> p a b", a=8)[:, :, 512:1024],
                            in_=stW[:].rearrange("p (a b) -> p a b", a=8)))
                        gpsimd.wait_ge(p3w.h, 48)
                        cW.inc(gpsimd.tensor_copy(
                            out=Wvsd[:].rearrange("p (a b) -> p a b", a=8)[:, :, 0:320],
                            in_=stW[:].rearrange("p (a b) -> p a b", a=8)[:, :, 0:320]))
                        gpsimd.wait_ge(p3w.h, 64)
                        cW.inc(gpsimd.tensor_copy(
                            out=Wvsd[:].rearrange("p (a b) -> p a b", a=8)[:, :, 320:360],
                            in_=stW[:].rearrange("p (a b) -> p a b", a=8)[:, :, 0:40]))
                        gpsimd.wait_ge(p3w.h, 80)
                        cW.inc(gpsimd.tensor_copy(
                            out=Wvsd[:].rearrange("p (a b) -> p a b", a=8)[:, :, 360:400],
                            in_=stW[:].rearrange("p (a b) -> p a b", a=8)[:, :, 0:40]))
                        gpsimd.wait_ge(p3w.h, 128)
                        gpsimd.memset(Wo3[64:128, 2 * 1024:3 * 1024], 0.0)
                        for kc in range(3):
                            rows = 128 if kc < 2 else 64
                            cW.inc(gpsimd.tensor_copy(
                                out=Wo3[0:rows, kc * 1024:(kc + 1) * 1024],
                                in_=stW[0:rows, kc * 1024:(kc + 1) * 1024]))
                        for it in range(NT3):
                            pb = it % 2
                            for chh in range(8):
                                gpsimd.wait_ge(p3tm.h, 9 * it + chh + 2)
                                if it >= 2 and chh == 0:
                                    gpsimd.wait_ge(p3kp.h, 4 * (it - 1))
                                    gpsimd.wait_ge(p3vg.h, 2 * (it - 1))
                                p3xn.inc(gpsimd.tensor_sub(
                                    out=xn2[:, pb, :].rearrange("p (ch t) -> p ch t", ch=8)[:, chh, :],
                                    in0=tmv3[:, (chh % 2) * TS3:(chh % 2) * TS3 + TS3],
                                    in1=bc3[:, TS3:2 * TS3]))
                            for mt in range(4):
                                gpsimd.wait_ge(p3ke.h, 4 * it + mt + 1)
                                dK.dma(gpsimd.dma_start(
                                    out=kTD[mt * 128:(mt + 1) * 128, it * TS3:(it + 1) * TS3],
                                    in_=kst[:, (mt % 2) * TS3:(mt % 2) * TS3 + TS3]))
                            if it % 4 == 3:
                                idx0 = (it // 4) * 8
                                for sl in range(8):
                                    gpsimd.wait_ge(p3vm.h, idx0 + sl + 1)
                                    dV.dma(gpsimd.dma_start(
                                        out=vD[(idx0 + sl) * 128:(idx0 + sl + 1) * 128, :],
                                        in_=vst[:, sl * 512:(sl + 1) * 512]))
                    CW_ATT = cW.n

                    @block.vector
                    def _(vector):
                        for qc in range(4):
                            vector.wait_ge(p3own.h, 16 + 16 * (qc + 1))
                            for chh in range(8):
                                p3x2.inc(vector.tensor_add(
                                    out=x2own[:].rearrange("p (ch t) -> p ch t", ch=8)
                                        [:, chh, qc * 512:(qc + 1) * 512],
                                    in0=x2own[:].rearrange("p (ch t) -> p ch t", ch=8)
                                        [:, chh, qc * 512:(qc + 1) * 512],
                                    in1=hsob[:].rearrange("p (ch t) -> p ch t", ch=8)[:, chh, :]))
                        for it in range(NT3):
                            pb = it % 2
                            vector.wait_ge(p3xf.h, 16 * (it + 1))
                            vector.wait_ge(p3hs.h, 16 * (it + 1))
                            if it >= 1:
                                vector.wait_ge(p3st.h, 2 * it)
                            p3cast.inc(vector.tensor_add(out=xf2[pb][:], in0=xf2[pb][:],
                                                         in1=hsf[:]))
                            p3cast.inc(vector.tensor_copy(out=x2bf[:], in_=xf2[pb][:]))
                            vector.wait_ge(p3st.h, 2 * it + 1)
                            if it >= 1:
                                vector.wait_ge(p3bc.h, it)
                            vector.tensor_scalar_mul(out=rows3[:, RM:RM + TS3],
                                                     in0=psSt3A[0:1, :], scalar1=1.0 / 1024)
                            vector.wait_ge(p3st.h, 2 * it + 2)
                            vector.tensor_scalar(out=rows3[:, RV:RV + TS3],
                                                     in0=psSt3B[0:1, :], scalar1=1.0 / 1024,
                                                     scalar2=EPS, op0=ALU.mult, op1=ALU.add)
                            vector.tensor_mul(out=rows3[:, RMR:RMR + TS3],
                                              in0=rows3[:, RM:RM + TS3], in1=rows3[:, RM:RM + TS3])
                            p3var.inc(vector.tensor_sub(out=rows3[:, RV:RV + TS3],
                                                        in0=rows3[:, RV:RV + TS3],
                                                        in1=rows3[:, RMR:RMR + TS3]))
                            vector.wait_ge(p3rstd.h, it + 1)
                            p3tm.inc(vector.tensor_mul(out=rows3[:, RMR:RMR + TS3],
                                                       in0=rows3[:, RM:RM + TS3],
                                                       in1=rows3[:, RR:RR + TS3]))
                            vector.wait_ge(p3evb.h, 2 * (it + 1))
                            for chh in range(8):
                                if it >= 1 or chh >= 2:
                                    vector.wait_ge(p3xn.h, 8 * it + chh - 1)
                                p3tm.inc(vector.tensor_mul(
                                    out=tmv3[:, (chh % 2) * TS3:(chh % 2) * TS3 + TS3],
                                    in0=xf2[pb][:].rearrange("p (ch t) -> p ch t", ch=8)[:, chh, :],
                                    in1=bc3[:, 0:TS3]))
                            for sub in range(2):
                                idx = 2 * it + sub
                                sl = idx % 8
                                vector.wait_ge(p3vg.h, idx + 1)
                                if idx >= 8:
                                    vector.wait_ge(p3vm.h, idx - 7)
                                p3vgc.inc(vector.tensor_copy(out=glf[:, sl * 40:(sl + 1) * 40],
                                                             in_=psVG[sub][:, 320:360]))
                            if it % 4 == 3:
                                g4 = it // 4
                                idx0 = g4 * 8
                                vector.wait_ge(p3sig.h, idx0 + 8)
                                vector.wait_ge(p3xv.h, idx0 + 8)
                                if g4 >= 1:
                                    vector.wait_ge(dV.h, 16 * idx0)
                                vector.tensor_scalar(out=wvS[:], in0=wvT[:],
                                                     scalar1=0.5, scalar2=0.5,
                                                     op0=ALU.mult, op1=ALU.add)
                                gv = glf[:].rearrange("p (s h e) -> p e s h", e=5, h=8)
                                wv_ = wvS[:].rearrange("p (s h e) -> p e s h", e=5, h=8)
                                t_ = gsc[:].rearrange("p (a sh) -> p a sh", a=12)
                                g = [gv[:, e, :, :].rearrange("p s h -> p (s h)") for e in range(5)]
                                def _tt3(o, a, b, op):
                                    vector.tensor_tensor(out=o, in0=a, in1=b, op=op)
                                _tt3(t_[:, 0, :], g[0], g[1], ALU.max)
                                _tt3(t_[:, 3, :], g[4], g[3], ALU.max)
                                _tt3(t_[:, 1, :], t_[:, 0, :], g[2], ALU.max)
                                _tt3(t_[:, 4, :], t_[:, 3, :], g[2], ALU.max)
                                _tt3(t_[:, 2, :], t_[:, 1, :], g[3], ALU.max)
                                _tt3(t_[:, 5, :], t_[:, 4, :], g[1], ALU.max)
                                _tt3(t_[:, 6, :], g[0], t_[:, 4, :], ALU.max)
                                _tt3(t_[:, 7, :], t_[:, 0, :], t_[:, 3, :], ALU.max)
                                _tt3(t_[:, 8, :], t_[:, 1, :], g[4], ALU.max)
                                _tt3(t_[:, 5, :], g[0], t_[:, 5, :], ALU.min)
                                _tt3(t_[:, 6, :], g[1], t_[:, 6, :], ALU.min)
                                _tt3(t_[:, 7, :], g[2], t_[:, 7, :], ALU.min)
                                _tt3(t_[:, 8, :], g[3], t_[:, 8, :], ALU.min)
                                _tt3(t_[:, 9, :], g[4], t_[:, 2, :], ALU.min)
                                _tt3(t_[:, 5, :], t_[:, 5, :], t_[:, 6, :], ALU.max)
                                _tt3(t_[:, 7, :], t_[:, 7, :], t_[:, 8, :], ALU.max)
                                _tt3(t_[:, 5, :], t_[:, 5, :], t_[:, 7, :], ALU.max)
                                _tt3(t_[:, 5, :], t_[:, 5, :], t_[:, 9, :], ALU.max)
                                for e in range(5):
                                    _tt3(t_[:, 10 + (e % 2), :], g[e], t_[:, 5, :], ALU.is_ge)
                                    if e >= 1:
                                        we = e - 1
                                        vector.tensor_mul(
                                            out=wv_[:, we, :, :].rearrange("p s h -> p (s h)"),
                                            in0=t_[:, 10 + (we % 2), :],
                                            in1=wv_[:, we, :, :].rearrange("p s h -> p (s h)"))
                                vector.tensor_mul(
                                    out=wv_[:, 4, :, :].rearrange("p s h -> p (s h)"),
                                    in0=t_[:, 10, :],
                                    in1=wv_[:, 4, :, :].rearrange("p s h -> p (s h)"))
                                for sl in range(8):
                                    for hh in range(8):
                                        for e in range(5):
                                            if e == 0:
                                                ins = vector.tensor_scalar_mul(
                                                    out=vst[:, sl * 512 + hh * 64:sl * 512 + (hh + 1) * 64],
                                                    in0=xvb[:, sl * 320 + e * 64:sl * 320 + (e + 1) * 64],
                                                    scalar1=wvS[:, sl * 40 + hh * 5 + e:sl * 40 + hh * 5 + e + 1])
                                            else:
                                                ins = vector.scalar_tensor_tensor(
                                                    out=vst[:, sl * 512 + hh * 64:sl * 512 + (hh + 1) * 64],
                                                    in0=xvb[:, sl * 320 + e * 64:sl * 320 + (e + 1) * 64],
                                                    scalar=wvS[:, sl * 40 + hh * 5 + e:sl * 40 + hh * 5 + e + 1],
                                                    in1=vst[:, sl * 512 + hh * 64:sl * 512 + (hh + 1) * 64],
                                                    op0=ALU.mult, op1=ALU.add)
                                    p3vm.inc(ins)

                    @block.scalar
                    def _(scalar):
                        for it in range(NT3):
                            pb = it % 2
                            scalar.wait_ge(p3cast.h, 2 * (it + 1))
                            p3sq.inc(scalar.activation(out=x2sq[:], in_=x2bf[:],
                                                       func=AF.Square))
                            scalar.wait_ge(p3var.h, it + 1)
                            scalar.activation(out=rows3[:, RR:RR + TS3],
                                              in_=rows3[:, RV:RV + TS3], func=AF.Ln)
                            p3rstd.inc(scalar.activation(out=rows3[:, RR:RR + TS3],
                                                         in_=rows3[:, RR:RR + TS3],
                                                         func=AF.Exp, scale=-0.5))
                            scalar.wait_ge(p3bc.h, it + 1)
                            if it >= 1:
                                scalar.wait_ge(p3xn.h, 8 * it)
                                scalar.wait_ge(p3tm.h, 9 * it)
                            p3evb.inc(scalar.copy(out=bc3[:, 0:TS3], in_=psBc3[:, 0:TS3]))
                            p3evb.inc(scalar.copy(out=bc3[:, TS3:2 * TS3],
                                                  in_=psBc3[:, TS3:2 * TS3]))
                            for mt in range(4):
                                scalar.wait_ge(p3kp.h, 4 * it + mt + 1)
                                if 4 * it + mt >= 2:
                                    scalar.wait_ge(dK.h, 16 * (4 * it + mt - 1))
                                p3ke.inc(scalar.copy(
                                    out=kst[:, (mt % 2) * TS3:(mt % 2) * TS3 + TS3],
                                    in_=psK[mt % 2][:, :]))
                            for sub in range(2):
                                idx = 2 * it + sub
                                sl = idx % 8
                                scalar.wait_ge(p3vg.h, idx + 1)
                                if idx >= 8:
                                    scalar.wait_ge(p3vm.h, idx - 7)
                                p3xv.inc(scalar.copy(out=xvb[:, sl * 320:(sl + 1) * 320],
                                                     in_=psVG[sub][:, 0:320]))
                                ins = scalar.activation(out=wvT[:, sl * 40:(sl + 1) * 40],
                                                        in_=psVG[sub][:, 320:360],
                                                        func=AF.Tanh, scale=0.5)
                                p3sig.inc(ins)
                                p3vgc.inc(scalar.nop())

                    @block.tensor
                    def _(tensor):
                        tensor.wait_ge(cW.h, CW_ATT)
                        for it in range(NT3):
                            pb = it % 2
                            tensor.wait_ge(p3cast.h, 2 * (it + 1))
                            if it >= 1:
                                tensor.wait_ge(p3var.h, it)
                            for chh in range(8):
                                ins = tensor.matmul(
                                    psSt3A[0:1, :], onesColB[:],
                                    x2bf[:].rearrange("p (ch t) -> p ch t", ch=8)[:, chh, :],
                                    start=(chh == 0), stop=(chh == 7))
                                if chh == 7:
                                    p3st.inc(ins)
                            tensor.wait_ge(p3sq.h, it + 1)
                            for chh in range(8):
                                ins = tensor.matmul(
                                    psSt3B[0:1, :], onesColB[:],
                                    x2sq[:].rearrange("p (ch t) -> p ch t", ch=8)[:, chh, :],
                                    start=(chh == 0), stop=(chh == 7))
                                if chh == 7:
                                    p3st.inc(ins)
                            tensor.wait_ge(p3tm.h, 9 * it + 1)
                            if it >= 1:
                                tensor.wait_ge(p3evb.h, 2 * it)
                            tensor.matmul(psBc3[:, 0:TS3], onesRowF[:],
                                          rows3[0:1, RR:RR + TS3], start=True, stop=True)
                            p3bc.inc(tensor.matmul(psBc3[:, TS3:2 * TS3], onesRowF[:],
                                                   rows3[0:1, RMR:RMR + TS3],
                                                   start=True, stop=True))
                            tensor.wait_ge(p3xn.h, 8 * (it + 1))
                            for mt in range(4):
                                if it * 4 + mt >= 2:
                                    tensor.wait_ge(p3ke.h, 4 * it + mt - 1)
                                for dch in range(8):
                                    ins = tensor.matmul(
                                        psK[mt % 2][:, :],
                                        Wqk[:].rearrange("p (a b) -> p a b", a=8)
                                            [:, dch, 512 + mt * 128:512 + (mt + 1) * 128],
                                        xn2[:, pb, :].rearrange("p (ch t) -> p ch t", ch=8)[:, dch, :],
                                        start=(dch == 0), stop=(dch == 7))
                                    if dch == 7:
                                        p3kp.inc(ins)
                            for sub in range(2):
                                idx = 2 * it + sub
                                if idx >= 2:
                                    tensor.wait_ge(p3vgc.h, 2 * (idx - 1))
                                for dch in range(8):
                                    ins = tensor.matmul(
                                        psVG[sub][:, :],
                                        xn2[:, pb, :].rearrange("p (ch t) -> p ch t", ch=8)
                                            [:, dch, sub * 128:(sub + 1) * 128],
                                        Wvsd[:].rearrange("p (a b) -> p a b", a=8)[:, dch, 0:368],
                                        start=(dch == 0), stop=(dch == 7))
                                    if dch == 7:
                                        p3vg.inc(ins)
                    P3T = dict(ke=p3ke.n, xn=p3xn.n, vm=p3vm.n, vg=p3vg.n,
                               dK=dK.n, dV=dV.n, tm=p3tm.n, st=p3st.n)
                p3s.close()

                p4cast, p4sq, p4st = sem("p4cast"), sem("p4sq"), sem("p4st")
                p4var, p4rstd, p4bc = sem("p4var"), sem("p4rstd"), sem("p4bc")
                p4evb, p4tm, p4xn = sem("p4evb"), sem("p4tm"), sem("p4xn")
                p4qp, p4qe = sem("p4qp"), sem("p4qe")
                p4gp, p4gv, p4mk, p4mt, p4mte = sem("p4gp"), sem("p4gv"), sem("p4mk"), sem("p4mt"), sem("p4mte")
                NIC4 = REG // TS3

                p4s = ExitStack()
                _S4 = lambda n, s, d: p4s.enter_context(nc.sbuf_tensor(n, s, d))
                _P4 = lambda n, s, d: p4s.enter_context(nc.psum_tensor(n, s, d))
                if True:
                    xn2o = _S4("xn2o", [128, 8 * REG], BF16)
                    xo_bf = _S4("xo_bf", [128, 2, 8 * TS3], BF16)
                    xo_sq = _S4("xo_sq", [128, 8 * TS3], BF16)
                    rows4 = _S4("rows4", [1, 4 * TS3], F32)
                    bc4 = _S4("bc4", [128, 2 * TS3], F32)
                    tmv4 = _S4("tmv4", [128, 2 * TS3], F32)
                    glo4 = _S4("glo4", [128, 16 * 40], F32)
                    gsc4 = _S4("gsc4", [128, 12 * 128], F32)
                    mko = _S4("mko", [128, 16 * 40], BF16)
                    psSt4A = _P4("psSt4A", [128, TS3], F32)
                    psSt4B = _P4("psSt4B", [128, TS3], F32)
                    psBc4 = _P4("psBc4", [128, 2 * TS3], F32)
                    psQ0 = _P4("psQ0", [128, TS3], F32)
                    psQ1 = _P4("psQ1", [128, TS3], F32)
                    psGO = _P4("psGO", [128, 40], F32)
                    psMT = _P4("psMT", [128, 128], BF16)

                    psQ = [psQ0, psQ1]
                    RM, RV, RR, RMR = 0, TS3, 2 * TS3, 3 * TS3
                    x2o3 = x2own[:].rearrange("p (ch t) -> p ch t", ch=8)
                    xn2o3 = xn2o[:].rearrange("p (ch t) -> p ch t", ch=8)

                    @block.gpsimd
                    def _(gpsimd):
                        gpsimd.wait_ge(p3ke.h, P3T["ke"])
                        gpsimd.wait_ge(p3vm.h, P3T["vm"])
                        gpsimd.wait_ge(p3sig.h, 2 * NT3)
                        gpsimd.wait_ge(dK.h, P3T["dK"])
                        gpsimd.wait_ge(dV.h, P3T["dV"])
                        gpsimd.wait_ge(p3x2.h, 32)
                        for ic in range(NIC4):
                            if ic >= 2:
                                gpsimd.wait_ge(p4st.h, 2 * (ic - 1))
                            for chh in range(8):
                                ins = gpsimd.tensor_copy(
                                    out=xo_bf[:, ic % 2, :].rearrange("p (ch t) -> p ch t", ch=8)[:, chh, :],
                                    in_=x2o3[:, chh, ic * TS3:(ic + 1) * TS3])
                                if chh == 7:
                                    p4cast.inc(ins)
                            for chh in range(8):
                                gpsimd.wait_ge(p4tm.h, 9 * ic + chh + 2)
                                p4xn.inc(gpsimd.tensor_sub(
                                    out=xn2o3[:, chh, ic * TS3:(ic + 1) * TS3],
                                    in0=tmv4[:, (chh % 2) * TS3:(chh % 2) * TS3 + TS3],
                                    in1=bc4[:, TS3:2 * TS3]))

                    @block.vector
                    def _(vector):
                        vector.wait_ge(p3xn.h, P3T["xn"])
                        vector.wait_ge(p3vg.h, P3T["vg"])
                        for ic in range(NIC4):
                            vector.wait_ge(p4st.h, 2 * ic + 1)
                            if ic >= 1:
                                vector.wait_ge(p4bc.h, ic)
                            vector.tensor_scalar_mul(out=rows4[:, RM:RM + TS3],
                                                     in0=psSt4A[0:1, :], scalar1=1.0 / 1024)
                            vector.wait_ge(p4st.h, 2 * ic + 2)
                            vector.tensor_scalar(out=rows4[:, RV:RV + TS3],
                                                     in0=psSt4B[0:1, :], scalar1=1.0 / 1024,
                                                     scalar2=EPS, op0=ALU.mult, op1=ALU.add)
                            vector.tensor_mul(out=rows4[:, RMR:RMR + TS3],
                                              in0=rows4[:, RM:RM + TS3], in1=rows4[:, RM:RM + TS3])
                            p4var.inc(vector.tensor_sub(out=rows4[:, RV:RV + TS3],
                                                        in0=rows4[:, RV:RV + TS3],
                                                        in1=rows4[:, RMR:RMR + TS3]))
                            vector.wait_ge(p4rstd.h, ic + 1)
                            p4tm.inc(vector.tensor_mul(out=rows4[:, RMR:RMR + TS3],
                                                       in0=rows4[:, RM:RM + TS3],
                                                       in1=rows4[:, RR:RR + TS3]))
                            vector.wait_ge(p4evb.h, 2 * (ic + 1))
                            for chh in range(8):
                                if ic >= 1 or chh >= 2:
                                    vector.wait_ge(p4xn.h, 8 * ic + chh - 1)
                                p4tm.inc(vector.tensor_mul(
                                    out=tmv4[:, (chh % 2) * TS3:(chh % 2) * TS3 + TS3],
                                    in0=x2o3[:, chh, ic * TS3:(ic + 1) * TS3],
                                    in1=bc4[:, 0:TS3]))
                        for st4 in range(16):
                            vector.wait_ge(p4gp.h, st4 + 1)
                            p4gv.inc(vector.tensor_copy(
                                out=glo4[:, st4 * 40:(st4 + 1) * 40], in_=psGO[:, 0:40]))
                        gv = glo4[:].rearrange("p (s h e) -> p e s h", e=5, h=8)
                        mk_ = mko[:].rearrange("p (s h e) -> p e s h", e=5, h=8)
                        t_ = gsc4[:].rearrange("p (a sh) -> p a sh", a=12)
                        g = [gv[:, e, :, :].rearrange("p s h -> p (s h)") for e in range(5)]
                        def _tt4(o, a, b, op):
                            vector.tensor_tensor(out=o, in0=a, in1=b, op=op)
                        _tt4(t_[:, 0, :], g[0], g[1], ALU.max)
                        _tt4(t_[:, 3, :], g[4], g[3], ALU.max)
                        _tt4(t_[:, 1, :], t_[:, 0, :], g[2], ALU.max)
                        _tt4(t_[:, 4, :], t_[:, 3, :], g[2], ALU.max)
                        _tt4(t_[:, 2, :], t_[:, 1, :], g[3], ALU.max)
                        _tt4(t_[:, 5, :], t_[:, 4, :], g[1], ALU.max)
                        _tt4(t_[:, 6, :], g[0], t_[:, 4, :], ALU.max)
                        _tt4(t_[:, 7, :], t_[:, 0, :], t_[:, 3, :], ALU.max)
                        _tt4(t_[:, 8, :], t_[:, 1, :], g[4], ALU.max)
                        _tt4(t_[:, 5, :], g[0], t_[:, 5, :], ALU.min)
                        _tt4(t_[:, 6, :], g[1], t_[:, 6, :], ALU.min)
                        _tt4(t_[:, 7, :], g[2], t_[:, 7, :], ALU.min)
                        _tt4(t_[:, 8, :], g[3], t_[:, 8, :], ALU.min)
                        _tt4(t_[:, 9, :], g[4], t_[:, 2, :], ALU.min)
                        _tt4(t_[:, 5, :], t_[:, 5, :], t_[:, 6, :], ALU.max)
                        _tt4(t_[:, 7, :], t_[:, 7, :], t_[:, 8, :], ALU.max)
                        _tt4(t_[:, 5, :], t_[:, 5, :], t_[:, 7, :], ALU.max)
                        _tt4(t_[:, 5, :], t_[:, 5, :], t_[:, 9, :], ALU.max)
                        for e in range(5):
                            ins = vector.tensor_tensor(
                                out=mk_[:, e, :, :].rearrange("p s h -> p (s h)"),
                                in0=g[e], in1=t_[:, 5, :], op=ALU.is_ge)
                        p4mk.inc(ins)
                        for st4 in range(16):
                            vector.wait_ge(p4mt.h, st4 + 1)
                            p4mte.inc(vector.tensor_copy(
                                out=maskT[:, st4 * 128:(st4 + 1) * 128], in_=psMT[0:40, :]))

                    @block.scalar
                    def _(scalar):
                        scalar.wait_ge(p3sig.h, 2 * NT3)
                        for ic in range(NIC4):
                            scalar.wait_ge(p4cast.h, ic + 1)
                            if ic >= 1:
                                scalar.wait_ge(p4st.h, 2 * ic)
                            p4sq.inc(scalar.activation(out=xo_sq[:], in_=xo_bf[:, ic % 2, :],
                                                       func=AF.Square))
                            scalar.wait_ge(p4var.h, ic + 1)
                            scalar.activation(out=rows4[:, RR:RR + TS3],
                                              in_=rows4[:, RV:RV + TS3], func=AF.Ln)
                            p4rstd.inc(scalar.activation(out=rows4[:, RR:RR + TS3],
                                                         in_=rows4[:, RR:RR + TS3],
                                                         func=AF.Exp, scale=-0.5))
                            scalar.wait_ge(p4bc.h, ic + 1)
                            if ic >= 1:
                                scalar.wait_ge(p4xn.h, 8 * ic)
                                scalar.wait_ge(p4tm.h, 9 * ic)
                            p4evb.inc(scalar.copy(out=bc4[:, 0:TS3], in_=psBc4[:, 0:TS3]))
                            p4evb.inc(scalar.copy(out=bc4[:, TS3:2 * TS3],
                                                  in_=psBc4[:, TS3:2 * TS3]))
                            for mt in range(4):
                                scalar.wait_ge(p4qp.h, 4 * ic + mt + 1)
                                p4qe.inc(scalar.activation(
                                    out=qT_s[:].rearrange("p (a t) -> p a t", a=4)
                                        [:, mt, ic * TS3:(ic + 1) * TS3],
                                    in_=psQ[mt % 2][:, :], func=AF.Copy, scale=0.125))

                    @block.tensor
                    def _(tensor):
                        tensor.wait_ge(p3st.h, P3T["st"])
                        for ic in range(NIC4):
                            tensor.wait_ge(p4cast.h, ic + 1)
                            if ic >= 1:
                                tensor.wait_ge(p4var.h, ic)
                            for chh in range(8):
                                ins = tensor.matmul(
                                    psSt4A[0:1, :], onesColB[:],
                                    xo_bf[:, ic % 2, :].rearrange("p (ch t) -> p ch t", ch=8)[:, chh, :],
                                    start=(chh == 0), stop=(chh == 7))
                                if chh == 7:
                                    p4st.inc(ins)
                            tensor.wait_ge(p4sq.h, ic + 1)
                            for chh in range(8):
                                ins = tensor.matmul(
                                    psSt4B[0:1, :], onesColB[:],
                                    xo_sq[:].rearrange("p (ch t) -> p ch t", ch=8)[:, chh, :],
                                    start=(chh == 0), stop=(chh == 7))
                                if chh == 7:
                                    p4st.inc(ins)
                            tensor.wait_ge(p4tm.h, 9 * ic + 1)
                            if ic >= 1:
                                tensor.wait_ge(p4evb.h, 2 * ic)
                            tensor.matmul(psBc4[:, 0:TS3], onesRowF[:],
                                          rows4[0:1, RR:RR + TS3], start=True, stop=True)
                            p4bc.inc(tensor.matmul(psBc4[:, TS3:2 * TS3], onesRowF[:],
                                                   rows4[0:1, RMR:RMR + TS3],
                                                   start=True, stop=True))
                            tensor.wait_ge(p4xn.h, 8 * (ic + 1))
                            for mt in range(4):
                                if ic * 4 + mt >= 2:
                                    tensor.wait_ge(p4qe.h, 4 * ic + mt - 1)
                                for dch in range(8):
                                    ins = tensor.matmul(
                                        psQ[mt % 2][:, :],
                                        Wqk[:].rearrange("p (a b) -> p a b", a=8)
                                            [:, dch, mt * 128:(mt + 1) * 128],
                                        xn2o3[:, dch, ic * TS3:(ic + 1) * TS3],
                                        start=(dch == 0), stop=(dch == 7))
                                    if dch == 7:
                                        p4qp.inc(ins)
                        for st4 in range(16):
                            if st4 >= 1:
                                tensor.wait_ge(p4gv.h, st4)
                            tensor.wait_ge(p4xn.h, 8 * (st4 // 2 + 1))
                            for dch in range(8):
                                ins = tensor.matmul(
                                    psGO[:, 0:40],
                                    xn2o3[:, dch, st4 * 128:(st4 + 1) * 128],
                                    Wvsd[:].rearrange("p (a b) -> p a b", a=8)[:, dch, 360:400],
                                    start=(dch == 0), stop=(dch == 7))
                                if dch == 7:
                                    p4gp.inc(ins)
                        tensor.wait_ge(p4mk.h, 1)
                        for st4 in range(16):
                            if st4 >= 1:
                                tensor.wait_ge(p4mte.h, st4)
                            p4mt.inc(tensor.transpose(psMT[0:40, :],
                                                      mko[:, st4 * 40:(st4 + 1) * 40], IDb[:]))
                P4_QE, P4_MTE, P4_XN = p4qe.n, p4mte.n, p4xn.n
                p4s.close()

            a_kk, a_v1, a_qq = sem("a_kk"), sem("a_v1"), sem("a_qq")
            a_qk, a_exp, a_pv = sem("a_qk"), sem("a_exp"), sem("a_pv")
            a_rec, a_bcd, a_anf, a_ann = sem("a_rec"), sem("a_bcd"), sem("a_anf"), sem("a_ann")
            a_sel, a_re, a_rs = sem("a_sel"), sem("a_re"), sem("a_rs")
            a_rsb, a_wo, a_fin = sem("a_rsb"), sem("a_wo"), sem("a_fin")

            p5s = ExitStack()
            _S5 = lambda n, s, d: p5s.enter_context(nc.sbuf_tensor(n, s, d))
            _P5 = lambda n, s, d: p5s.enter_context(nc.psum_tensor(n, s, d))
            if True:
                kk0 = _S5("kk0", [128, T], BF16)
                kk1 = _S5("kk1", [128, T], BF16)
                V650 = _S5("V650", [128, 64 * 66], BF16)
                V651 = _S5("V651", [128, 64 * 66], BF16)
                qq = _S5("qq", [128, REG], BF16)
                pT0 = _S5("pT0", [128, 1024], BF16)
                pT1 = _S5("pT1", [128, 1024], BF16)
                recrow = _S5("recrow", [1, 512], F32)
                attf = _S5("attf", [64, 512], F32)
                attn_n = _S5("attn_n", [64, 512], BF16)
                Rst = _S5("Rst", [128, 2 * 512], BF16)
                RSum = _S5("RSum", [128, 3 * REG], F32)
                RSb = _S5("RSb", [128, 3 * 512], BF16)
                ost = _S5("ost", [128, 2 * 512], F32)
                psS0 = _P5("psS0", [128, 1024], F32)
                psS1 = _P5("psS1", [128, 1024], F32)
                psATT0 = _P5("psATT0", [128, 512], F32)
                psATT1 = _P5("psATT1", [128, 512], F32)
                psBR = _P5("psBR", [128, 512], F32)

                kk = [kk0, kk1]
                V65 = [V650, V651]
                pT = [pT0, pT1]
                psS = [psS0, psS1]
                psATT = [psATT0, psATT1]
                RSum3 = RSum[:].rearrange("p (kc t) -> p kc t", kc=3)

                GS = [(2 * g, 2) for g in range(32)]

                @block.sync
                def _(sync):
                    sync.wait_ge(p4qe.h, P4_QE)
                    sync.wait_ge(p4mte.h, P4_MTE)
                    for h in range(8):
                        if h >= 2:
                            sync.wait_ge(a_qk.h, 256 * (h - 1))
                        a_kk.dma(sync.dma_start(out=kk[h % 2][0:64, :],
                                                in_=kTD[h * 64:(h + 1) * 64, :]))
                        a_kk.dma(sync.dma_start(out=kk[h % 2][64:128, :],
                                                in_=kTD[h * 64:(h + 1) * 64, :]))
                        if h >= 2:
                            sync.wait_ge(a_pv.h, NG * 4 * (h - 1))
                        a_kk.dma(sync.dma_start(
                            out=V65[h % 2][:].rearrange("p (t c) -> p t c", c=66)[:, :, 0:64],
                            in_=vD.rearrange("(t p) d -> p t d", p=128)[:, :, h * 64:(h + 1) * 64]))

                @block.vector
                def _(vector):
                    vector.wait_ge(p4mte.h, P4_MTE)
                    vector.wait_ge(p4qe.h, P4_QE)
                    a_v1.inc(vector.memset(
                        V650[:].rearrange("p (t c) -> p t c", c=66)[:, :, 64:65], 1.0))
                    a_v1.inc(vector.memset(
                        V651[:].rearrange("p (t c) -> p t c", c=66)[:, :, 64:65], 1.0))
                    for h in range(8):
                        vector.wait_ge(a_qk.h, 256 * h)
                        a_qq.inc(vector.tensor_copy(
                            out=qq[0:64, :],
                            in_=qT_s[(h % 2) * 64:(h % 2) * 64 + 64,
                                     (h // 2) * REG:(h // 2 + 1) * REG]))
                        a_qq.inc(vector.tensor_copy(
                            out=qq[64:128, :],
                            in_=qT_s[(h % 2) * 64:(h % 2) * 64 + 64,
                                     (h // 2) * REG:(h // 2 + 1) * REG]))
                        for ic in range(4):
                            hic = h * 4 + ic
                            vector.wait_ge(a_pv.h, NG * (hic + 1))
                            if hic >= 1:
                                vector.wait_ge(a_bcd.h, hic)
                            a_rec.inc(vector.reciprocal(out=recrow[:], in_=psATT[hic % 2][64:65, :]))
                            vector.wait_ge(a_anf.h, hic + 1)
                            vector.wait_ge(a_bcd.h, hic + 1)
                            a_ann.inc(vector.tensor_mul(out=attn_n[:], in0=psBR[0:64, :],
                                                        in1=attf[:]))
                            for e in range(5):
                                vector.wait_ge(a_sel.h, 5 * hic + e + 1)
                                gp = 3 * hic + e // 2
                                if gp >= 2:
                                    vector.wait_ge(a_rs.h, gp + 2)
                                sl = ((e // 2) % 2) * 512
                                rb = 64 * (e % 2)
                                a_re.inc(vector.tensor_mul(
                                    out=Rst[rb:rb + 64, sl:sl + 512],
                                    in0=psBR[64:128, :], in1=attn_n[:]))

                @block.scalar
                def _(scalar):
                    for h in range(8):
                        for ic in range(4):
                            hic = h * 4 + ic
                            for g, (jp0, ng) in enumerate(GS):
                                gidx = hic * NG + g
                                scalar.wait_ge(a_qk.h, hic * 64 + jp0 + ng)
                                if gidx >= 2:
                                    scalar.wait_ge(a_pv.h, gidx - 1)
                                a_exp.inc(scalar.activation(
                                    out=pT[gidx % 2][:, 0:ng * 512],
                                    in_=psS[gidx % 2][:, 0:ng * 512], func=AF.Exp))
                            scalar.wait_ge(a_pv.h, NG * (hic + 1))
                            if hic >= 1:
                                scalar.wait_ge(a_ann.h, hic)
                            a_anf.inc(scalar.copy(out=attf[:], in_=psATT[hic % 2][0:64, :]))

                @block.tensor
                def _(tensor):
                    tensor.wait_ge(a_v1.h, 2)
                    for h in range(8):
                        tensor.wait_ge(a_kk.h, 48 * h + 32)
                        tensor.wait_ge(a_qq.h, 2 * (h + 1))
                        for ic in range(4):
                            hic = h * 4 + ic
                            for g, (jp0, ng) in enumerate(GS):
                                gidx = hic * NG + g
                                if gidx >= 2:
                                    tensor.wait_ge(a_exp.h, gidx - 1)
                                for jj in range(ng):
                                    jp = jp0 + jj
                                    jb = jp * 128
                                    tensor.matmul(
                                        psS[gidx % 2][0:64, jj * 512:(jj + 1) * 512],
                                        kk[h % 2][0:64, jb:jb + 64],
                                        qq[0:64, ic * 512:(ic + 1) * 512],
                                        start=True, stop=True, tile_position=(0, 0))
                                    ins = tensor.matmul(
                                        psS[gidx % 2][64:128, jj * 512:(jj + 1) * 512],
                                        kk[h % 2][64:128, jb + 64:jb + 128],
                                        qq[64:128, ic * 512:(ic + 1) * 512],
                                        start=True, stop=True, tile_position=(64, 64))
                                    a_qk.inc(ins)
                                if g >= 1:
                                    gprev = gidx - 1
                                    jpp, ngp = GS[g - 1]
                                    tensor.wait_ge(a_exp.h, gprev + 1)
                                    if g == 1:
                                        tensor.wait_ge(a_kk.h, 48 * (h + 1))
                                        if hic >= 2:
                                            tensor.wait_ge(a_rec.h, hic - 1)
                                            tensor.wait_ge(a_anf.h, hic - 1)
                                    for jj in range(ngp):
                                        jp = jpp + jj
                                        ins = tensor.matmul(
                                            psATT[hic % 2][0:65, :],
                                            V65[h % 2][:, jp * 66:jp * 66 + 65],
                                            pT[gprev % 2][:, jj * 512:(jj + 1) * 512],
                                            start=(jp == 0), stop=(jp == 63))
                                        if jj == ngp - 1:
                                            a_pv.inc(ins)
                            gidx = hic * NG + NG - 1
                            jpp, ngp = GS[NG - 1]
                            tensor.wait_ge(a_exp.h, gidx + 1)
                            for jj in range(ngp):
                                jp = jpp + jj
                                ins = tensor.matmul(
                                    psATT[hic % 2][0:65, :],
                                    V65[h % 2][:, jp * 66:jp * 66 + 65],
                                    pT[gidx % 2][:, jj * 512:(jj + 1) * 512],
                                    start=(jp == 0), stop=(jp == 63))
                                if jj == ngp - 1:
                                    a_pv.inc(ins)
                            tensor.wait_ge(a_rec.h, hic + 1)
                            if hic >= 1:
                                tensor.wait_ge(a_ann.h, hic)
                            a_bcd.inc(tensor.matmul(psBR[0:64, :], ones64F[:],
                                                    recrow[:], start=True, stop=True))
                            for e in range(5):
                                tensor.wait_ge(a_re.h, 5 * hic + e)
                                if e == 0:
                                    tensor.wait_ge(a_ann.h, hic + 1)
                                a_sel.inc(tensor.matmul(
                                    psBR[64:128, :],
                                    SELb[0:40, (h * 5 + e) * 64:(h * 5 + e) * 64 + 64],
                                    maskT[0:40, ic * 512:(ic + 1) * 512],
                                    start=True, stop=True, tile_position=(0, 64)))
                    for ic in range(4):
                        tensor.wait_ge(a_rsb.h, 3 * (ic + 1))
                        for dch in range(8):
                            gi = ic * 8 + dch
                            tensor.wait_ge(a_fin.h, gi)
                            for kc in range(3):
                                ins = tensor.matmul(
                                    psATT[0][:, :],
                                    Wo3[:, kc * 1024 + dch * 128:kc * 1024 + (dch + 1) * 128],
                                    RSb[:, kc * 512:(kc + 1) * 512],
                                    start=(kc == 0), stop=(kc == 2))
                                if kc == 2:
                                    a_wo.inc(ins)

                @block.gpsimd
                def _(gpsimd):
                    gpsimd.wait_ge(p4mte.h, P4_MTE)
                    for kc in range(3):
                        ins = gpsimd.memset(RSum3[:, kc, :], 0.0)
                        a_rs.inc(ins)
                    for h in range(8):
                        for ic in range(4):
                            hic = h * 4 + ic
                            for pi in range(3):
                                last_e = (pi * 2 + 1) if pi < 2 else 4
                                rows = 128 if pi < 2 else 64
                                gpsimd.wait_ge(a_re.h, 5 * hic + last_e + 1)
                                sl = (pi % 2) * 512
                                a_rs.inc(gpsimd.tensor_add(
                                    out=RSum3[0:rows, pi, ic * 512:(ic + 1) * 512],
                                    in0=RSum3[0:rows, pi, ic * 512:(ic + 1) * 512],
                                    in1=Rst[0:rows, sl:sl + 512]))
                    for ic in range(4):
                        for dch in range(8):
                            gi = ic * 8 + dch
                            gpsimd.wait_ge(a_fin.h, gi + 1)
                            dF.dma(gpsimd.dma_start(
                                out=outT3[:, dch, ic * 512:(ic + 1) * 512],
                                in_=ost[:, (gi % 2) * 512:(gi % 2) * 512 + 512]))
                    gpsimd.wait_ge(dF.h, 16 * 32)

                @block.scalar
                def _(scalar):
                    scalar.wait_ge(a_rs.h, 3 + 96)
                    for ic in range(4):
                        if ic >= 1:
                            scalar.wait_ge(a_wo.h, 8 * ic)
                        for kc in range(3):
                            a_rsb.inc(scalar.copy(out=RSb[:, kc * 512:(kc + 1) * 512],
                                                  in_=RSum3[:, kc, ic * 512:(ic + 1) * 512]))

                @block.vector
                def _(vector):
                    for ic in range(4):
                        for dch in range(8):
                            gi = ic * 8 + dch
                            vector.wait_ge(a_wo.h, gi + 1)
                            if gi >= 2:
                                vector.wait_ge(dF.h, 16 * (gi - 1))
                            a_fin.inc(vector.tensor_add(
                                out=ost[:, (gi % 2) * 512:(gi % 2) * 512 + 512],
                                in0=psATT[0][:, :],
                                in1=x2own[:].rearrange("p (ch t) -> p ch t", ch=8)
                                    [:, dch, ic * 512:(ic + 1) * 512]))
            p5s.close()
    return nc


_PCOL_TOK = None


def _tok_of_col():
    global _PCOL_TOK
    if _PCOL_TOK is None:
        j = np.arange(T)
        sp, c = j // 128, j % 128
        _PCOL_TOK = c * C + sp
    return _PCOL_TOK


def kernel(L0_attn_x, L0_attn_Wi, L0_attn_bi, L0_attn_Wh, L0_attn_bh,
           L0_ffn_Wq, L0_ffn_Wk, L0_ffn_Ws, L0_ffn_Wd, L0_ffn_Wv, L0_ffn_Wo):
    x = np.asarray(L0_attn_x, dtype=np.float32)
    Wi = np.asarray(L0_attn_Wi, dtype=np.float32)
    Wh = np.asarray(L0_attn_Wh, dtype=np.float32)
    biv = np.ascontiguousarray(np.asarray(L0_attn_bi, dtype=np.float32).reshape(8, 128).T)
    bhv = np.ascontiguousarray(np.asarray(L0_attn_bh, dtype=np.float32).reshape(8, 128).T)
    Wq = np.ascontiguousarray(np.asarray(L0_ffn_Wq, dtype=np.float32))
    Wk = np.ascontiguousarray(np.asarray(L0_ffn_Wk, dtype=np.float32))
    Ws = np.ascontiguousarray(np.asarray(L0_ffn_Ws, dtype=np.float32))
    Wd = np.ascontiguousarray(np.asarray(L0_ffn_Wd, dtype=np.float32))
    Wv = np.asarray(L0_ffn_Wv, dtype=np.float32)
    Wo = np.asarray(L0_ffn_Wo, dtype=np.float32)

    tok = _tok_of_col()
    xP = [np.ascontiguousarray(x[b].T[:, tok]) for b in range(B)]
    WiT = np.ascontiguousarray(Wi.T)
    WhT = np.ascontiguousarray(Wh.T)
    Wvf = np.ascontiguousarray(Wv.transpose(1, 0, 2).reshape(D, E * DH))
    WoS = np.ascontiguousarray(Wo.reshape(E * DH, D))
    selc = np.zeros((40, 2560), np.float32)
    for r in range(40):
        selc[r, r * 64:(r + 1) * 64] = 1.0
    idt = np.eye(128, dtype=np.float32)

    nc = build_nc()
    in_maps = []
    for k in range(8):
        b, q = k // 4, k % 4
        in_maps.append({
            "xT": xP[b],
            "xTown": np.ascontiguousarray(xP[b][:, q * REG:(q + 1) * REG]),
            "qoff": np.array([[q * REG]], np.int32),
            "WiT": WiT, "WhT": WhT, "Wq": Wq, "Wk": Wk, "Ws": Ws, "Wd": Wd,
            "Wvf": Wvf, "WoS": WoS, "bi": biv, "bh": bhv,
            "selc": selc, "idt": idt,
        })
    res = run_bass_kernel_spmd(nc, in_maps, core_ids=list(range(8)))
    out = np.empty((B, T, D), np.float32)
    for k in range(8):
        b, q = k // 4, k % 4
        cols = tok[q * REG:(q + 1) * REG]
        out[b, cols, :] = res.results[k]["outT"].T
    return out


# version 11 (final)
# speedup vs baseline: 1.0283x; 1.0047x over previous
import numpy as np
import ml_dtypes
from contextlib import ExitStack

import concourse.bass as bass
import concourse.mybir as mybir
from concourse.bass_utils import run_bass_kernel_spmd

F32 = mybir.dt.float32
BF16 = mybir.dt.bfloat16
I32 = mybir.dt.int32
AF = mybir.ActivationFunctionType
ALU = mybir.AluOpType

B, T, D = 2, 8192, 1024
NH, DH, E = 8, 64, 5
C, W = 64, 48
NCH = T // C
STEPS = C + W
REG = T // 4
TS1 = 256
NT1 = T // TS1
TS3 = 256
NT3 = T // TS3
EPS = 1e-5
NG = 32




class Sem:
    def __init__(self, nc, ctx, name):
        self.h = ctx.enter_context(nc.semaphore(name))
        self.n = 0

    def inc(self, ins, k=1):
        ins.then_inc(self.h, k)
        self.n += k
        return ins

    def dma(self, ins):
        return self.inc(ins, 16)


def build_nc():
    nc = bass.Bass("TRN2", target_bir_lowering=False)

    xT = nc.dram_tensor("xT", [D, T], F32, kind="ExternalInput").ap()
    xTown = nc.dram_tensor("xTown", [D, REG], F32, kind="ExternalInput").ap()
    qoff = nc.dram_tensor("qoff", [1, 1], I32, kind="ExternalInput").ap()
    WiT = nc.dram_tensor("WiT", [D, D], F32, kind="ExternalInput").ap()
    WhT = nc.dram_tensor("WhT", [D, D], F32, kind="ExternalInput").ap()
    Wq = nc.dram_tensor("Wq", [D, 512], F32, kind="ExternalInput").ap()
    Wk = nc.dram_tensor("Wk", [D, 512], F32, kind="ExternalInput").ap()
    Ws = nc.dram_tensor("Ws", [D, 40], F32, kind="ExternalInput").ap()
    Wd = nc.dram_tensor("Wd", [D, 40], F32, kind="ExternalInput").ap()
    Wvf = nc.dram_tensor("Wvf", [D, 320], F32, kind="ExternalInput").ap()
    WoS = nc.dram_tensor("WoS", [320, D], F32, kind="ExternalInput").ap()
    bi = nc.dram_tensor("bi", [128, 8], F32, kind="ExternalInput").ap()
    bh = nc.dram_tensor("bh", [128, 8], F32, kind="ExternalInput").ap()
    selc = nc.dram_tensor("selc", [40, 2560], F32, kind="ExternalInput").ap()
    idt = nc.dram_tensor("idt", [128, 128], F32, kind="ExternalInput").ap()
    outT = nc.dram_tensor("outT", [D, REG], F32, kind="ExternalOutput").ap()

    hsD = nc.dram_tensor("hsD", [128, 8 * T], BF16).ap()
    kTD = nc.dram_tensor("kTD", [512, T], BF16).ap()
    vD = nc.dram_tensor("vD", [T, 512], BF16).ap()

    hsD3 = hsD.rearrange("p (ch t) -> p ch t", ch=8)
    xT3 = xT.rearrange("(ch p) t -> p ch t", p=128)
    xTown3 = xTown.rearrange("(ch p) t -> p ch t", p=128)
    outT3 = outT.rearrange("(ch p) t -> p ch t", p=128)

    ctx = ExitStack()
    with ctx:
        SEMS = {}

        def sem(name):
            if name not in SEMS:
                SEMS[name] = Sem(nc, ctx, name)
            return SEMS[name]

        biS = ctx.enter_context(nc.sbuf_tensor("biS", [128, 8], F32))
        bb = ctx.enter_context(nc.sbuf_tensor("bb", [128, 16], F32))
        onesColB = ctx.enter_context(nc.sbuf_tensor("onesColB", [128, 1], BF16))
        onesRowF = ctx.enter_context(nc.sbuf_tensor("onesRowF", [1, 128], F32))
        ones64F = ctx.enter_context(nc.sbuf_tensor("ones64F", [1, 64], F32))
        IDb = ctx.enter_context(nc.sbuf_tensor("IDb", [128, 128], BF16))
        SELb = ctx.enter_context(nc.sbuf_tensor("SELb", [40, 2560], BF16))
        maskT = ctx.enter_context(nc.sbuf_tensor("maskT", [40, REG], BF16))

        block = ctx.enter_context(nc.Block())

        dW = sem("dW")
        cW = sem("cW")
        dOut = sem("dOut")

        with nc.sbuf_tensor("xiS", [128, 8 * 8193], BF16) as xiS:
            xiS3 = xiS[:].rearrange("p (ch t) -> p ch t", ch=8)
            p1xf, p1cast, p1sq = sem("p1xf"), sem("p1cast"), sem("p1sq")
            p1st, p1var, p1rstd = sem("p1st"), sem("p1var"), sem("p1rstd")
            p1bc, p1evb, p1tm = sem("p1bc"), sem("p1evb"), sem("p1tm")
            p1xn, p1xip, p1xie = sem("p1xn"), sem("p1xip"), sem("p1xie")

            with nc.sbuf_tensor("stage", [128, 1024], F32) as stage, \
                 nc.sbuf_tensor("WiTb", [128, 8 * 1024], BF16) as WiTb, \
                 nc.sbuf_tensor("xf0", [128, 8 * TS1], F32) as xf0, \
                 nc.sbuf_tensor("xf1", [128, 8 * TS1], F32) as xf1, \
                 nc.sbuf_tensor("xbf", [128, 8 * TS1], BF16) as xbf, \
                 nc.sbuf_tensor("xsq", [128, 8 * TS1], BF16) as xsq, \
                 nc.sbuf_tensor("xn1", [128, 2, 8 * TS1], BF16) as xn1, \
                 nc.sbuf_tensor("rows1", [1, 2, 4 * TS1], F32) as rows1, \
                 nc.sbuf_tensor("bc1", [128, 2, 2 * TS1], F32) as bc1, \
                 nc.sbuf_tensor("tmv1", [128, 2 * TS1], F32) as tmv1, \
                 nc.psum_tensor("psStA0", [128, TS1], F32) as psStA0, \
                 nc.psum_tensor("psStA1", [128, TS1], F32) as psStA1, \
                 nc.psum_tensor("psStB0", [128, TS1], F32) as psStB0, \
                 nc.psum_tensor("psStB1", [128, TS1], F32) as psStB1, \
                 nc.psum_tensor("psBc0", [128, 2 * TS1], F32) as psBc0, \
                 nc.psum_tensor("psBc1", [128, 2 * TS1], F32) as psBc1, \
                 nc.psum_tensor("psXi0", [128, TS1], F32) as psXi0, \
                 nc.psum_tensor("psXi1", [128, TS1], F32) as psXi1:

                xf = [xf0, xf1]
                psXi = [psXi0, psXi1]
                psStA = [psStA0, psStA1]
                psStB = [psStB0, psStB1]
                psBc = [psBc0, psBc1]
                RM, RV, RR, RMR = 0, TS1, 2 * TS1, 3 * TS1

                @block.sync
                def _(sync):
                    dW.dma(sync.dma_start(out=bb[:, 0:8], in_=bi))
                    dW.dma(sync.dma_start(out=bb[:, 8:16], in_=bh))
                    dW.dma(sync.dma_start(out=stage[:, 0:128], in_=idt))
                    dW.dma(sync.dma_start(out=stage[0:40, 128:1024], in_=selc[:, 0:896]))
                    sync.wait_ge(cW.h, 1)
                    dW.dma(sync.dma_start(out=stage[0:40, 0:896], in_=selc[:, 896:1792]))
                    sync.wait_ge(cW.h, 2)
                    dW.dma(sync.dma_start(out=stage[0:40, 0:768], in_=selc[:, 1792:2560]))
                    for r in range(8):
                        sync.wait_ge(cW.h, 3 + r)
                        dW.dma(sync.dma_start(
                            out=stage[:, 0:1024],
                            in_=WiT.rearrange("(a p) b -> p a b", p=128)[:, r, :]))

                @block.vector
                def _(vector):
                    vector.wait_ge(dW.h, 64)
                    vector.tensor_add(out=biS[:], in0=bb[:, 0:8], in1=bb[:, 8:16])
                    vector.memset(onesColB[:], 1.0)
                    vector.memset(onesRowF[:], 1.0)
                    vector.memset(ones64F[:], 1.0)
                    vector.tensor_copy(out=IDb[:], in_=stage[:, 0:128])
                    cW.inc(vector.tensor_copy(out=SELb[:, 0:896], in_=stage[0:40, 128:1024]))
                    vector.wait_ge(dW.h, 80)
                    cW.inc(vector.tensor_copy(out=SELb[:, 896:1792], in_=stage[0:40, 0:896]))
                    vector.wait_ge(dW.h, 96)
                    cW.inc(vector.tensor_copy(out=SELb[:, 1792:2560], in_=stage[0:40, 0:768]))
                    for r in range(8):
                        vector.wait_ge(dW.h, 96 + 16 * (r + 1))
                        cW.inc(vector.tensor_copy(
                            out=WiTb[:].rearrange("p (a b) -> p a b", a=8)[:, r, :],
                            in_=stage[:, 0:1024]))
                CW_WI = cW.n

                @block.sync
                def _(sync):
                    for it in range(NT1):
                        if it >= 2:
                            sync.wait_ge(p1tm.h, 9 * (it - 1))
                        p1xf.dma(sync.dma_start(
                            out=xf[it % 2][:].rearrange("p (ch t) -> p ch t", ch=8),
                            in_=xT3[:, :, it * TS1:(it + 1) * TS1]))

                @block.vector
                def _(vector):
                    for it in range(NT1):
                        pb = it % 2
                        vector.wait_ge(p1xf.h, 16 * (it + 1))
                        if it >= 1:
                            vector.wait_ge(p1st.h, 2 * it - 1)
                            vector.wait_ge(p1sq.h, it)
                        p1cast.inc(vector.tensor_copy(out=xbf[:], in_=xf[pb][:]))
                        r_ = rows1[:, pb, :]
                        vector.wait_ge(p1st.h, 2 * it + 1)
                        if it >= 2:
                            vector.wait_ge(p1bc.h, it - 1)
                        vector.tensor_scalar_mul(out=r_[:, RM:RM + TS1],
                                                 in0=psStA[pb][0:1, :], scalar1=1.0 / 1024)
                        vector.wait_ge(p1st.h, 2 * it + 2)
                        vector.tensor_scalar(out=r_[:, RV:RV + TS1],
                                                 in0=psStB[pb][0:1, :], scalar1=1.0 / 1024,
                                                 scalar2=EPS, op0=ALU.mult, op1=ALU.add)
                        vector.tensor_mul(out=r_[:, RMR:RMR + TS1],
                                          in0=r_[:, RM:RM + TS1], in1=r_[:, RM:RM + TS1])
                        p1var.inc(vector.tensor_sub(out=r_[:, RV:RV + TS1],
                                                    in0=r_[:, RV:RV + TS1],
                                                    in1=r_[:, RMR:RMR + TS1]))
                        vector.wait_ge(p1rstd.h, it + 1)
                        p1tm.inc(vector.tensor_mul(out=r_[:, RMR:RMR + TS1],
                                                   in0=r_[:, RM:RM + TS1],
                                                   in1=r_[:, RR:RR + TS1]))
                        vector.wait_ge(p1evb.h, 2 * (it + 1))
                        for chh in range(8):
                            if it >= 1 or chh >= 2:
                                vector.wait_ge(p1xn.h, 8 * it + chh - 1)
                            p1tm.inc(vector.tensor_mul(
                                out=tmv1[:, (chh % 2) * TS1:(chh % 2) * TS1 + TS1],
                                in0=xf[pb][:].rearrange("p (ch t) -> p ch t", ch=8)[:, chh, :],
                                in1=bc1[:, pb, 0:TS1]))

                @block.scalar
                def _(scalar):
                    for it in range(NT1):
                        pb = it % 2
                        scalar.wait_ge(p1cast.h, it + 1)
                        if it >= 1:
                            scalar.wait_ge(p1st.h, 2 * it)
                        p1sq.inc(scalar.activation(out=xsq[:], in_=xbf[:],
                                                   func=AF.Square))
                        r_ = rows1[:, pb, :]
                        scalar.wait_ge(p1var.h, it + 1)
                        scalar.activation(out=r_[:, RR:RR + TS1],
                                          in_=r_[:, RV:RV + TS1], func=AF.Ln)
                        p1rstd.inc(scalar.activation(out=r_[:, RR:RR + TS1],
                                                     in_=r_[:, RR:RR + TS1],
                                                     func=AF.Exp, scale=-0.5))
                        scalar.wait_ge(p1bc.h, it + 1)
                        if it >= 2:
                            scalar.wait_ge(p1xn.h, 8 * (it - 1))
                            scalar.wait_ge(p1tm.h, 9 * (it - 1))
                        p1evb.inc(scalar.copy(out=bc1[:, pb, 0:TS1], in_=psBc[pb][:, 0:TS1]))
                        p1evb.inc(scalar.copy(out=bc1[:, pb, TS1:2 * TS1], in_=psBc[pb][:, TS1:2 * TS1]))
                        for m in range(8):
                            scalar.wait_ge(p1xip.h, 8 * it + m + 1)
                            p1xie.inc(scalar.copy(
                                out=xiS3[:, m, 1 + it * TS1:1 + (it + 1) * TS1],
                                in_=psXi[m % 2][:, :]))

                @block.gpsimd
                def _(gpsimd):
                    for it in range(NT1):
                        pb = it % 2
                        for chh in range(8):
                            gpsimd.wait_ge(p1tm.h, 9 * it + chh + 2)
                            if it >= 2 and chh == 0:
                                gpsimd.wait_ge(p1xip.h, 8 * (it - 1))
                            p1xn.inc(gpsimd.tensor_sub(
                                out=xn1[:, pb, :].rearrange("p (ch t) -> p ch t", ch=8)[:, chh, :],
                                in0=tmv1[:, (chh % 2) * TS1:(chh % 2) * TS1 + TS1],
                                in1=bc1[:, pb, TS1:2 * TS1]))

                @block.tensor
                def _(tensor):
                    tensor.wait_ge(cW.h, CW_WI)
                    for it in range(NT1):
                        pb = it % 2
                        tensor.wait_ge(p1cast.h, it + 1)
                        if it >= 2:
                            tensor.wait_ge(p1var.h, it - 1)
                        for chh in range(8):
                            ins = tensor.matmul(
                                psStA[pb][0:1, :], onesColB[:],
                                xbf[:].rearrange("p (ch t) -> p ch t", ch=8)[:, chh, :],
                                start=(chh == 0), stop=(chh == 7))
                            if chh == 7:
                                p1st.inc(ins)
                        tensor.wait_ge(p1sq.h, it + 1)
                        for chh in range(8):
                            ins = tensor.matmul(
                                psStB[pb][0:1, :], onesColB[:],
                                xsq[:].rearrange("p (ch t) -> p ch t", ch=8)[:, chh, :],
                                start=(chh == 0), stop=(chh == 7))
                            if chh == 7:
                                p1st.inc(ins)
                        tensor.wait_ge(p1tm.h, 9 * it + 1)
                        if it >= 2:
                            tensor.wait_ge(p1evb.h, 2 * (it - 1))
                        tensor.matmul(psBc[pb][:, 0:TS1], onesRowF[:],
                                      rows1[0:1, pb, RR:RR + TS1], start=True, stop=True)
                        p1bc.inc(tensor.matmul(psBc[pb][:, TS1:2 * TS1], onesRowF[:],
                                               rows1[0:1, pb, RMR:RMR + TS1],
                                               start=True, stop=True))
                        tensor.wait_ge(p1xn.h, 8 * (it + 1))
                        for m in range(8):
                            if it * 8 + m >= 2:
                                tensor.wait_ge(p1xie.h, 8 * it + m - 1)
                            for dch in range(8):
                                ins = tensor.matmul(
                                    psXi[m % 2][:, :],
                                    WiTb[:].rearrange("p (a b) -> p a b", a=8)[:, dch, m * 128:(m + 1) * 128],
                                    xn1[:, pb, :].rearrange("p (ch t) -> p ch t", ch=8)[:, dch, :],
                                    start=(dch == 0), stop=(dch == 7))
                                if dch == 7:
                                    p1xip.inc(ins)
            P1_XIE = 8 * NT1

            s_ps, s_z, s_h, s_hz = sem("s_ps"), sem("s_z"), sem("s_h"), sem("s_hz")
            with nc.sbuf_tensor("stg2", [128, 2048], F32) as stg2, \
                 nc.sbuf_tensor("WhTb", [128, 64 * 128], BF16) as WhTb, \
                 nc.sbuf_tensor("wb0", [128, 8 * 128], BF16) as wb0, \
                 nc.sbuf_tensor("wb1", [128, 8 * 128], BF16) as wb1, \
                 nc.sbuf_tensor("zb", [128, 8 * 128], F32) as zb:
                p2ctx = ExitStack()
                ps2 = [p2ctx.enter_context(nc.psum_tensor(f"ps2_{j}", [128, 128], F32))
                       for j in range(8)]
                wb = [wb0, wb1]

                @block.sync
                def _(sync):
                    sync.wait_ge(p1xie.h, P1_XIE)
                    for r in range(4):
                        if r > 0:
                            sync.wait_ge(cW.h, CW_WI + r)
                        dW.dma(sync.dma_start(
                            out=stg2[:].rearrange("p (a b) -> p a b", a=2),
                            in_=WhT.rearrange("(a p) b -> p a b", p=128)[:, 2 * r:2 * r + 2, :]))

                @block.gpsimd
                def _(gpsimd):
                    for r in range(4):
                        gpsimd.wait_ge(dW.h, 224 + 16 * (r + 1))
                        for a in range(2):
                            ins = gpsimd.tensor_copy(
                                out=WhTb[:].rearrange("p (a b) -> p a b", a=64)
                                    [:, (2 * r + a) * 8:(2 * r + a) * 8 + 8, :],
                                in_=stg2[:].rearrange("p (a b) -> p a b", a=2)[:, a, :]
                                    .rearrange("p (a b) -> p a b", a=8))
                            if a == 1:
                                cW.inc(ins)
                CW_WH = cW.n

                @block.vector
                def _(vector):
                    vector.wait_ge(p1xf.h, 16 * NT1)
                    vector.wait_ge(p1xie.h, P1_XIE)
                    s_h.inc(vector.memset(wb0[:, 0:512], 0.0))
                    s_h.inc(vector.memset(wb0[:, 512:1024], 0.0))
                    vector.memset(xiS3[:, :, 0:1], 0.0)
                    for s in range(STEPS):
                        xio = (s + C - W) * 128 if s < W else 1 + (s - W) * 128
                        for j in range(8):
                            vector.wait_ge(s_ps.h, s * 8 + j + 1)
                            if s >= 1:
                                vector.wait_ge(s_h.h, 2 * s + (1 if j < 4 else 2))
                            s_z.inc(vector.scalar_tensor_tensor(
                                out=zb[:, j * 128:(j + 1) * 128],
                                in0=ps2[j][:, :], scalar=biS[:, j:j + 1],
                                in1=xiS3[:, j, xio:xio + 128],
                                op0=ALU.add, op1=ALU.add))
                        if s == W - 1:
                            vector.wait_ge(s_h.h, 2 + 2 * W)
                            s_hz.inc(vector.memset(
                                wb[0][:].rearrange("p (ch c) -> p ch c", ch=8)[:, :, 0:1], 0.0))

                @block.scalar
                def _(scalar):
                    for s in range(STEPS):
                        dst = wb[(s + 1) % 2]
                        for half in range(2):
                            scalar.wait_ge(s_z.h, s * 8 + 4 * (half + 1))
                            if half == 0 and s >= W + 2:
                                scalar.wait_ge(dOut.h, 16 * (s - W - 1))
                            s_h.inc(scalar.activation(
                                out=dst[:, half * 512:(half + 1) * 512],
                                in_=zb[:, half * 512:(half + 1) * 512], func=AF.Tanh))

                @block.tensor
                def _(tensor):
                    tensor.wait_ge(cW.h, CW_WH)
                    for s in range(STEPS):
                        src = wb[s % 2]
                        for j in range(8):
                            if s > 0:
                                tensor.wait_ge(s_z.h, (s - 1) * 8 + j + 1)
                            if j == 0:
                                tensor.wait_ge(s_h.h, 2 * s + 1)
                                if s == W:
                                    tensor.wait_ge(s_hz.h, 1)
                            for i in range(8):
                                if j == 0 and i == 4:
                                    tensor.wait_ge(s_h.h, 2 * s + 2)
                                ins = tensor.matmul(
                                    ps2[j][:, :],
                                    WhTb[:, (i * 8 + j) * 128:(i * 8 + j) * 128 + 128],
                                    src[:, i * 128:(i + 1) * 128],
                                    start=(i == 0), stop=(i == 7))
                                if i == 7:
                                    s_ps.inc(ins)

                @block.gpsimd
                def _(gpsimd):
                    for s in range(W, STEPS):
                        gpsimd.wait_ge(s_h.h, 2 + 2 * s + 2)
                        dOut.dma(gpsimd.dma_start(
                            out=hsD3[:, :, (s - W) * 128:(s - W + 1) * 128],
                            in_=wb[(s + 1) % 2][:].rearrange("p (ch c) -> p ch c", ch=8)))
                p2ctx.close()
        SC_Z = STEPS * 8
        SC_H = 2 + 2 * STEPS
        DOUT_P2 = dOut.n

        dK, dV, dF = sem("dK"), sem("dV"), sem("dF")
        p3w = sem("p3w")
        with nc.sbuf_tensor("x2own", [128, 8 * REG], F32) as x2own, \
             nc.sbuf_tensor("qT", [128, 4 * REG], BF16) as qT_s, \
             nc.sbuf_tensor("Wo3", [128, 3 * 1024], BF16) as Wo3:

            p3x2 = sem("p3x2")
            with nc.sbuf_tensor("stW", [128, 2048], F32) as stW, \
                 nc.sbuf_tensor("Wqk", [128, 8 * 1024], BF16) as Wqk, \
                 nc.sbuf_tensor("Wvsd", [128, 8 * 400], BF16) as Wvsd:

                p3s = ExitStack()
                _S3 = lambda n, s, d: p3s.enter_context(nc.sbuf_tensor(n, s, d))
                _P3 = lambda n, s, d: p3s.enter_context(nc.psum_tensor(n, s, d))
                if True:
                    hsob = _S3("hsob", [128, 8 * 512], BF16)
                    xf20 = _S3("xf20", [128, 8 * TS3], F32)
                    xf21 = _S3("xf21", [128, 8 * TS3], F32)
                    hsf = _S3("hsf", [128, 2, 8 * TS3], BF16)
                    x2bf = _S3("x2bf", [128, 8 * TS3], BF16)
                    x2sq = _S3("x2sq", [128, 8 * TS3], BF16)
                    xn2 = _S3("xn2", [128, 2, 8 * TS3], BF16)
                    rows3 = _S3("rows3", [1, 4 * TS3], F32)
                    bc3 = _S3("bc3", [128, 2 * TS3], F32)
                    tmv3 = _S3("tmv3", [128, 2 * TS3], F32)
                    kst = _S3("kst", [128, 2 * TS3], BF16)
                    xvb = _S3("xvb", [128, 8 * 320], BF16)
                    glf = _S3("glf", [128, 8 * 40], F32)
                    gsc = _S3("gsc", [128, 12 * 64], F32)
                    wvS = _S3("wvS", [128, 8 * 40], F32)
                    wvT = _S3("wvT", [128, 8 * 40], F32)
                    vst = _S3("vst", [128, 8 * 512], BF16)
                    psSt3A = _P3("psSt3A", [128, TS3], F32)
                    psSt3B = _P3("psSt3B", [128, TS3], F32)
                    psBc3 = _P3("psBc3", [128, 2 * TS3], F32)
                    psK0 = _P3("psK0", [128, TS3], F32)
                    psK1 = _P3("psK1", [128, TS3], F32)
                    psVG0 = _P3("psVG0", [128, 368], F32)
                    psVG1 = _P3("psVG1", [128, 368], F32)

                    psK = [psK0, psK1]
                    psVG = [psVG0, psVG1]
                    xf2 = [xf20, xf21]
                    p3xf, p3hs, p3cast, p3sq = sem("p3xf"), sem("p3hs"), sem("p3cast"), sem("p3sq")
                    p3st, p3var, p3rstd = sem("p3st"), sem("p3var"), sem("p3rstd")
                    p3bc, p3evb, p3tm, p3xn = sem("p3bc"), sem("p3evb"), sem("p3tm"), sem("p3xn")
                    p3kp, p3ke = sem("p3kp"), sem("p3ke")
                    p3vg, p3vgc, p3xv = sem("p3vg"), sem("p3vgc"), sem("p3xv")
                    p3sig, p3wv, p3vm = sem("p3sig"), sem("p3wv"), sem("p3vm")
                    p3own = sem("p3own")
                    RM, RV, RR, RMR = 0, TS3, 2 * TS3, 3 * TS3

                    @block.sync
                    def _(sync):
                        sync.wait_ge(s_z.h, SC_Z)
                        sync.wait_ge(s_h.h, SC_H)
                        sync.wait_ge(dOut.h, DOUT_P2)
                        wsrc = [("wq0", Wq, 0), ("wq1", Wq, 1), ("wk0", Wk, 0), ("wk1", Wk, 1),
                                ("wv0", Wvf, 0), ("wv1", Wvf, 1), ("ws", Ws, None), ("wd", Wd, None)]
                        nw = 0
                        for nm, W_, half in wsrc:
                            if nw > 0:
                                sync.wait_ge(cW.h, CW_WH + nw)
                            if half is None:
                                p3w.dma(sync.dma_start(
                                    out=stW[:].rearrange("p (a b) -> p a b", a=8)[:, :, 0:40],
                                    in_=W_.rearrange("(a p) b -> p a b", p=128)))
                            else:
                                cols = int(W_.ap[-1][1]) if False else (512 if nm[1] in "qk" else 320)
                                p3w.dma(sync.dma_start(
                                    out=stW[:].rearrange("p (a b) -> p a b", a=4)[:, :, 0:cols],
                                    in_=W_.rearrange("(a p) b -> p a b", p=128)[:, 4 * half:4 * half + 4, :]))
                            nw += 1
                        for kc in range(3):
                            rows = 128 if kc < 2 else 64
                            sync.wait_ge(cW.h, CW_WH + nw)
                            nw += 1
                            p3w.dma(sync.dma_start(
                                out=stW[0:rows, 0:1024],
                                in_=WoS[kc * 128:kc * 128 + rows, :]))
                        p3own.dma(sync.dma_start(
                            out=x2own[:].rearrange("p (ch t) -> p ch t", ch=8),
                            in_=xTown3))
                        with sync.register("rq") as rq:
                            sync.reg_load(rq, qoff)
                            q0 = sync.snap(rq, min_val=0, max_val=3 * REG)
                            for qc in range(4):
                                with sync.register(f"rq2_{qc}") as rq2:
                                    sync.reg_add(rq2, q0, qc * 512)
                                    q2 = sync.snap(rq2, min_val=0, max_val=4 * REG - 512)
                                    if qc >= 1:
                                        sync.wait_ge(p3x2.h, 8 * qc)
                                    p3own.dma(sync.dma_start(
                                        out=hsob[:].rearrange("p (ch t) -> p ch t", ch=8),
                                        in_=hsD3[:, :, bass.ds(q2, 512)]))
                        for it in range(NT3):
                            if it >= 2:
                                sync.wait_ge(p3tm.h, 9 * (it - 1))
                            p3xf.dma(sync.dma_start(
                                out=xf2[it % 2][:].rearrange("p (ch t) -> p ch t", ch=8),
                                in_=xT3[:, :, it * TS3:(it + 1) * TS3]))
                            if it >= 2:
                                sync.wait_ge(p3cast.h, 2 * (it - 1) - 1)
                            p3hs.dma(sync.dma_start(
                                out=hsf[:, it % 2, :].rearrange("p (ch t) -> p ch t", ch=8),
                                in_=hsD3[:, :, it * TS3:(it + 1) * TS3]))

                    @block.gpsimd
                    def _(gpsimd):
                        Wqk8 = Wqk[:].rearrange("p (a b) -> p a b", a=8)
                        Wvsd8 = Wvsd[:].rearrange("p (a b) -> p a b", a=8)
                        stW4 = stW[:].rearrange("p (a b) -> p a b", a=4)
                        stW8v = stW[:].rearrange("p (a b) -> p a b", a=8)
                        casts = [
                            lambda: gpsimd.tensor_copy(out=Wqk8[:, 0:4, 0:512], in_=stW4),
                            lambda: gpsimd.tensor_copy(out=Wqk8[:, 4:8, 0:512], in_=stW4),
                            lambda: gpsimd.tensor_copy(out=Wqk8[:, 0:4, 512:1024], in_=stW4),
                            lambda: gpsimd.tensor_copy(out=Wqk8[:, 4:8, 512:1024], in_=stW4),
                            lambda: gpsimd.tensor_copy(out=Wvsd8[:, 0:4, 0:320],
                                                       in_=stW4[:, :, 0:320]),
                            lambda: gpsimd.tensor_copy(out=Wvsd8[:, 4:8, 0:320],
                                                       in_=stW4[:, :, 0:320]),
                            lambda: gpsimd.tensor_copy(out=Wvsd8[:, :, 320:360], in_=stW8v[:, :, 0:40]),
                            lambda: gpsimd.tensor_copy(out=Wvsd8[:, :, 360:400], in_=stW8v[:, :, 0:40]),
                        ]
                        for ci, fn in enumerate(casts):
                            gpsimd.wait_ge(p3w.h, 16 * (ci + 1))
                            cW.inc(fn())
                        gpsimd.wait_ge(p3w.h, 16 * 9)
                        gpsimd.memset(Wo3[64:128, 2 * 1024:3 * 1024], 0.0)
                        for kc in range(3):
                            rows = 128 if kc < 2 else 64
                            gpsimd.wait_ge(p3w.h, 16 * (9 + kc))
                            cW.inc(gpsimd.tensor_copy(
                                out=Wo3[0:rows, kc * 1024:(kc + 1) * 1024],
                                in_=stW[0:rows, 0:1024]))
                        for it in range(NT3):
                            pb = it % 2
                            for chh in range(8):
                                gpsimd.wait_ge(p3tm.h, 9 * it + chh + 2)
                                if it >= 2 and chh == 0:
                                    gpsimd.wait_ge(p3kp.h, 4 * (it - 1))
                                    gpsimd.wait_ge(p3vg.h, 2 * (it - 1))
                                p3xn.inc(gpsimd.tensor_sub(
                                    out=xn2[:, pb, :].rearrange("p (ch t) -> p ch t", ch=8)[:, chh, :],
                                    in0=tmv3[:, (chh % 2) * TS3:(chh % 2) * TS3 + TS3],
                                    in1=bc3[:, TS3:2 * TS3]))
                            for mt in range(4):
                                gpsimd.wait_ge(p3ke.h, 4 * it + mt + 1)
                                dK.dma(gpsimd.dma_start(
                                    out=kTD[mt * 128:(mt + 1) * 128, it * TS3:(it + 1) * TS3],
                                    in_=kst[:, (mt % 2) * TS3:(mt % 2) * TS3 + TS3]))
                            if it % 4 == 3:
                                idx0 = (it // 4) * 8
                                for sl in range(8):
                                    gpsimd.wait_ge(p3vm.h, idx0 + sl + 1)
                                    dV.dma(gpsimd.dma_start(
                                        out=vD[(idx0 + sl) * 128:(idx0 + sl + 1) * 128, :],
                                        in_=vst[:, sl * 512:(sl + 1) * 512]))
                    CW_ATT = cW.n

                    @block.vector
                    def _(vector):
                        for qc in range(4):
                            vector.wait_ge(p3own.h, 16 + 16 * (qc + 1))
                            for chh in range(8):
                                p3x2.inc(vector.tensor_add(
                                    out=x2own[:].rearrange("p (ch t) -> p ch t", ch=8)
                                        [:, chh, qc * 512:(qc + 1) * 512],
                                    in0=x2own[:].rearrange("p (ch t) -> p ch t", ch=8)
                                        [:, chh, qc * 512:(qc + 1) * 512],
                                    in1=hsob[:].rearrange("p (ch t) -> p ch t", ch=8)[:, chh, :]))
                        for it in range(NT3):
                            pb = it % 2
                            vector.wait_ge(p3xf.h, 16 * (it + 1))
                            vector.wait_ge(p3hs.h, 16 * (it + 1))
                            if it >= 1:
                                vector.wait_ge(p3st.h, 2 * it)
                            p3cast.inc(vector.tensor_add(out=xf2[pb][:], in0=xf2[pb][:],
                                                         in1=hsf[:, pb, :]))
                            p3cast.inc(vector.tensor_copy(out=x2bf[:], in_=xf2[pb][:]))
                            vector.wait_ge(p3st.h, 2 * it + 1)
                            if it >= 1:
                                vector.wait_ge(p3bc.h, it)
                            vector.tensor_scalar_mul(out=rows3[:, RM:RM + TS3],
                                                     in0=psSt3A[0:1, :], scalar1=1.0 / 1024)
                            vector.wait_ge(p3st.h, 2 * it + 2)
                            vector.tensor_scalar(out=rows3[:, RV:RV + TS3],
                                                     in0=psSt3B[0:1, :], scalar1=1.0 / 1024,
                                                     scalar2=EPS, op0=ALU.mult, op1=ALU.add)
                            vector.tensor_mul(out=rows3[:, RMR:RMR + TS3],
                                              in0=rows3[:, RM:RM + TS3], in1=rows3[:, RM:RM + TS3])
                            p3var.inc(vector.tensor_sub(out=rows3[:, RV:RV + TS3],
                                                        in0=rows3[:, RV:RV + TS3],
                                                        in1=rows3[:, RMR:RMR + TS3]))
                            vector.wait_ge(p3rstd.h, it + 1)
                            p3tm.inc(vector.tensor_mul(out=rows3[:, RMR:RMR + TS3],
                                                       in0=rows3[:, RM:RM + TS3],
                                                       in1=rows3[:, RR:RR + TS3]))
                            vector.wait_ge(p3evb.h, 2 * (it + 1))
                            for chh in range(8):
                                if it >= 1 or chh >= 2:
                                    vector.wait_ge(p3xn.h, 8 * it + chh - 1)
                                p3tm.inc(vector.tensor_mul(
                                    out=tmv3[:, (chh % 2) * TS3:(chh % 2) * TS3 + TS3],
                                    in0=xf2[pb][:].rearrange("p (ch t) -> p ch t", ch=8)[:, chh, :],
                                    in1=bc3[:, 0:TS3]))
                            for sub in range(2):
                                idx = 2 * it + sub
                                sl = idx % 8
                                vector.wait_ge(p3vg.h, idx + 1)
                                if idx >= 8:
                                    vector.wait_ge(p3vm.h, idx - 7)
                                p3vgc.inc(vector.tensor_copy(out=glf[:, sl * 40:(sl + 1) * 40],
                                                             in_=psVG[sub][:, 320:360]))
                            if it % 4 == 3:
                                g4 = it // 4
                                idx0 = g4 * 8
                                vector.wait_ge(p3sig.h, idx0 + 8)
                                vector.wait_ge(p3xv.h, idx0 + 8)
                                if g4 >= 1:
                                    vector.wait_ge(dV.h, 16 * idx0)
                                vector.tensor_scalar(out=wvS[:], in0=wvT[:],
                                                     scalar1=0.5, scalar2=0.5,
                                                     op0=ALU.mult, op1=ALU.add)
                                gv = glf[:].rearrange("p (s h e) -> p e s h", e=5, h=8)
                                wv_ = wvS[:].rearrange("p (s h e) -> p e s h", e=5, h=8)
                                t_ = gsc[:].rearrange("p (a sh) -> p a sh", a=12)
                                g = [gv[:, e, :, :].rearrange("p s h -> p (s h)") for e in range(5)]
                                def _tt3(o, a, b, op):
                                    vector.tensor_tensor(out=o, in0=a, in1=b, op=op)
                                _tt3(t_[:, 0, :], g[0], g[1], ALU.max)
                                _tt3(t_[:, 3, :], g[4], g[3], ALU.max)
                                _tt3(t_[:, 1, :], t_[:, 0, :], g[2], ALU.max)
                                _tt3(t_[:, 4, :], t_[:, 3, :], g[2], ALU.max)
                                _tt3(t_[:, 2, :], t_[:, 1, :], g[3], ALU.max)
                                _tt3(t_[:, 5, :], t_[:, 4, :], g[1], ALU.max)
                                _tt3(t_[:, 6, :], g[0], t_[:, 4, :], ALU.max)
                                _tt3(t_[:, 7, :], t_[:, 0, :], t_[:, 3, :], ALU.max)
                                _tt3(t_[:, 8, :], t_[:, 1, :], g[4], ALU.max)
                                _tt3(t_[:, 5, :], g[0], t_[:, 5, :], ALU.min)
                                _tt3(t_[:, 6, :], g[1], t_[:, 6, :], ALU.min)
                                _tt3(t_[:, 7, :], g[2], t_[:, 7, :], ALU.min)
                                _tt3(t_[:, 8, :], g[3], t_[:, 8, :], ALU.min)
                                _tt3(t_[:, 9, :], g[4], t_[:, 2, :], ALU.min)
                                _tt3(t_[:, 5, :], t_[:, 5, :], t_[:, 6, :], ALU.max)
                                _tt3(t_[:, 7, :], t_[:, 7, :], t_[:, 8, :], ALU.max)
                                _tt3(t_[:, 5, :], t_[:, 5, :], t_[:, 7, :], ALU.max)
                                _tt3(t_[:, 5, :], t_[:, 5, :], t_[:, 9, :], ALU.max)
                                for e in range(5):
                                    _tt3(t_[:, 10 + (e % 2), :], g[e], t_[:, 5, :], ALU.is_ge)
                                    if e >= 1:
                                        we = e - 1
                                        vector.tensor_mul(
                                            out=wv_[:, we, :, :].rearrange("p s h -> p (s h)"),
                                            in0=t_[:, 10 + (we % 2), :],
                                            in1=wv_[:, we, :, :].rearrange("p s h -> p (s h)"))
                                vector.tensor_mul(
                                    out=wv_[:, 4, :, :].rearrange("p s h -> p (s h)"),
                                    in0=t_[:, 10, :],
                                    in1=wv_[:, 4, :, :].rearrange("p s h -> p (s h)"))
                                for sl in range(8):
                                    for hh in range(8):
                                        for e in range(5):
                                            if e == 0:
                                                ins = vector.tensor_scalar_mul(
                                                    out=vst[:, sl * 512 + hh * 64:sl * 512 + (hh + 1) * 64],
                                                    in0=xvb[:, sl * 320 + e * 64:sl * 320 + (e + 1) * 64],
                                                    scalar1=wvS[:, sl * 40 + hh * 5 + e:sl * 40 + hh * 5 + e + 1])
                                            else:
                                                ins = vector.scalar_tensor_tensor(
                                                    out=vst[:, sl * 512 + hh * 64:sl * 512 + (hh + 1) * 64],
                                                    in0=xvb[:, sl * 320 + e * 64:sl * 320 + (e + 1) * 64],
                                                    scalar=wvS[:, sl * 40 + hh * 5 + e:sl * 40 + hh * 5 + e + 1],
                                                    in1=vst[:, sl * 512 + hh * 64:sl * 512 + (hh + 1) * 64],
                                                    op0=ALU.mult, op1=ALU.add)
                                    p3vm.inc(ins)

                    @block.scalar
                    def _(scalar):
                        for it in range(NT3):
                            pb = it % 2
                            scalar.wait_ge(p3cast.h, 2 * (it + 1))
                            p3sq.inc(scalar.activation(out=x2sq[:], in_=x2bf[:],
                                                       func=AF.Square))
                            scalar.wait_ge(p3var.h, it + 1)
                            scalar.activation(out=rows3[:, RR:RR + TS3],
                                              in_=rows3[:, RV:RV + TS3], func=AF.Ln)
                            p3rstd.inc(scalar.activation(out=rows3[:, RR:RR + TS3],
                                                         in_=rows3[:, RR:RR + TS3],
                                                         func=AF.Exp, scale=-0.5))
                            scalar.wait_ge(p3bc.h, it + 1)
                            if it >= 1:
                                scalar.wait_ge(p3xn.h, 8 * it)
                                scalar.wait_ge(p3tm.h, 9 * it)
                            p3evb.inc(scalar.copy(out=bc3[:, 0:TS3], in_=psBc3[:, 0:TS3]))
                            p3evb.inc(scalar.copy(out=bc3[:, TS3:2 * TS3],
                                                  in_=psBc3[:, TS3:2 * TS3]))
                            for mt in range(4):
                                scalar.wait_ge(p3kp.h, 4 * it + mt + 1)
                                if 4 * it + mt >= 2:
                                    scalar.wait_ge(dK.h, 16 * (4 * it + mt - 1))
                                p3ke.inc(scalar.copy(
                                    out=kst[:, (mt % 2) * TS3:(mt % 2) * TS3 + TS3],
                                    in_=psK[mt % 2][:, :]))
                            for sub in range(2):
                                idx = 2 * it + sub
                                sl = idx % 8
                                scalar.wait_ge(p3vg.h, idx + 1)
                                if idx >= 8:
                                    scalar.wait_ge(p3vm.h, idx - 7)
                                p3xv.inc(scalar.copy(out=xvb[:, sl * 320:(sl + 1) * 320],
                                                     in_=psVG[sub][:, 0:320]))
                                ins = scalar.activation(out=wvT[:, sl * 40:(sl + 1) * 40],
                                                        in_=psVG[sub][:, 320:360],
                                                        func=AF.Tanh, scale=0.5)
                                p3sig.inc(ins)
                                p3vgc.inc(scalar.nop())

                    @block.tensor
                    def _(tensor):
                        tensor.wait_ge(cW.h, CW_ATT)
                        for it in range(NT3):
                            pb = it % 2
                            tensor.wait_ge(p3cast.h, 2 * (it + 1))
                            if it >= 1:
                                tensor.wait_ge(p3var.h, it)
                            for chh in range(8):
                                ins = tensor.matmul(
                                    psSt3A[0:1, :], onesColB[:],
                                    x2bf[:].rearrange("p (ch t) -> p ch t", ch=8)[:, chh, :],
                                    start=(chh == 0), stop=(chh == 7))
                                if chh == 7:
                                    p3st.inc(ins)
                            tensor.wait_ge(p3sq.h, it + 1)
                            for chh in range(8):
                                ins = tensor.matmul(
                                    psSt3B[0:1, :], onesColB[:],
                                    x2sq[:].rearrange("p (ch t) -> p ch t", ch=8)[:, chh, :],
                                    start=(chh == 0), stop=(chh == 7))
                                if chh == 7:
                                    p3st.inc(ins)
                            tensor.wait_ge(p3tm.h, 9 * it + 1)
                            if it >= 1:
                                tensor.wait_ge(p3evb.h, 2 * it)
                            tensor.matmul(psBc3[:, 0:TS3], onesRowF[:],
                                          rows3[0:1, RR:RR + TS3], start=True, stop=True)
                            p3bc.inc(tensor.matmul(psBc3[:, TS3:2 * TS3], onesRowF[:],
                                                   rows3[0:1, RMR:RMR + TS3],
                                                   start=True, stop=True))
                            tensor.wait_ge(p3xn.h, 8 * (it + 1))
                            for mt in range(4):
                                if it * 4 + mt >= 2:
                                    tensor.wait_ge(p3ke.h, 4 * it + mt - 1)
                                for dch in range(8):
                                    ins = tensor.matmul(
                                        psK[mt % 2][:, :],
                                        Wqk[:].rearrange("p (a b) -> p a b", a=8)
                                            [:, dch, 512 + mt * 128:512 + (mt + 1) * 128],
                                        xn2[:, pb, :].rearrange("p (ch t) -> p ch t", ch=8)[:, dch, :],
                                        start=(dch == 0), stop=(dch == 7))
                                    if dch == 7:
                                        p3kp.inc(ins)
                            for sub in range(2):
                                idx = 2 * it + sub
                                if idx >= 2:
                                    tensor.wait_ge(p3vgc.h, 2 * (idx - 1))
                                for dch in range(8):
                                    ins = tensor.matmul(
                                        psVG[sub][:, :],
                                        xn2[:, pb, :].rearrange("p (ch t) -> p ch t", ch=8)
                                            [:, dch, sub * 128:(sub + 1) * 128],
                                        Wvsd[:].rearrange("p (a b) -> p a b", a=8)[:, dch, 0:368],
                                        start=(dch == 0), stop=(dch == 7))
                                    if dch == 7:
                                        p3vg.inc(ins)
                    P3T = dict(ke=p3ke.n, xn=p3xn.n, vm=p3vm.n, vg=p3vg.n,
                               dK=dK.n, dV=dV.n, tm=p3tm.n, st=p3st.n)
                p3s.close()

                p4cast, p4sq, p4st = sem("p4cast"), sem("p4sq"), sem("p4st")
                p4var, p4rstd, p4bc = sem("p4var"), sem("p4rstd"), sem("p4bc")
                p4evb, p4tm, p4xn = sem("p4evb"), sem("p4tm"), sem("p4xn")
                p4qp, p4qe = sem("p4qp"), sem("p4qe")
                p4gp, p4gv, p4mk, p4mt, p4mte = sem("p4gp"), sem("p4gv"), sem("p4mk"), sem("p4mt"), sem("p4mte")
                NIC4 = REG // TS3

                p4s = ExitStack()
                _S4 = lambda n, s, d: p4s.enter_context(nc.sbuf_tensor(n, s, d))
                _P4 = lambda n, s, d: p4s.enter_context(nc.psum_tensor(n, s, d))
                if True:
                    xn2o = _S4("xn2o", [128, 8 * REG], BF16)
                    xo_bf = _S4("xo_bf", [128, 2, 8 * TS3], BF16)
                    xo_sq = _S4("xo_sq", [128, 8 * TS3], BF16)
                    rows4 = _S4("rows4", [1, 4 * TS3], F32)
                    bc4 = _S4("bc4", [128, 2 * TS3], F32)
                    tmv4 = _S4("tmv4", [128, 2 * TS3], F32)
                    glo4 = _S4("glo4", [128, 16 * 40], F32)
                    gsc4 = _S4("gsc4", [128, 12 * 128], F32)
                    mko = _S4("mko", [128, 16 * 40], BF16)
                    psSt4A = _P4("psSt4A", [128, TS3], F32)
                    psSt4B = _P4("psSt4B", [128, TS3], F32)
                    psBc4 = _P4("psBc4", [128, 2 * TS3], F32)
                    psQ0 = _P4("psQ0", [128, TS3], F32)
                    psQ1 = _P4("psQ1", [128, TS3], F32)
                    psGO = _P4("psGO", [128, 40], F32)
                    psMT = _P4("psMT", [128, 128], BF16)

                    psQ = [psQ0, psQ1]
                    RM, RV, RR, RMR = 0, TS3, 2 * TS3, 3 * TS3
                    x2o3 = x2own[:].rearrange("p (ch t) -> p ch t", ch=8)
                    xn2o3 = xn2o[:].rearrange("p (ch t) -> p ch t", ch=8)

                    @block.gpsimd
                    def _(gpsimd):
                        gpsimd.wait_ge(p3ke.h, P3T["ke"])
                        gpsimd.wait_ge(p3vm.h, P3T["vm"])
                        gpsimd.wait_ge(p3sig.h, 2 * NT3)
                        gpsimd.wait_ge(dK.h, P3T["dK"])
                        gpsimd.wait_ge(dV.h, P3T["dV"])
                        gpsimd.wait_ge(p3x2.h, 32)
                        for ic in range(NIC4):
                            if ic >= 2:
                                gpsimd.wait_ge(p4st.h, 2 * (ic - 1))
                            for chh in range(8):
                                ins = gpsimd.tensor_copy(
                                    out=xo_bf[:, ic % 2, :].rearrange("p (ch t) -> p ch t", ch=8)[:, chh, :],
                                    in_=x2o3[:, chh, ic * TS3:(ic + 1) * TS3])
                                if chh == 7:
                                    p4cast.inc(ins)
                            for chh in range(8):
                                gpsimd.wait_ge(p4tm.h, 9 * ic + chh + 2)
                                p4xn.inc(gpsimd.tensor_sub(
                                    out=xn2o3[:, chh, ic * TS3:(ic + 1) * TS3],
                                    in0=tmv4[:, (chh % 2) * TS3:(chh % 2) * TS3 + TS3],
                                    in1=bc4[:, TS3:2 * TS3]))

                    @block.vector
                    def _(vector):
                        vector.wait_ge(p3xn.h, P3T["xn"])
                        vector.wait_ge(p3vg.h, P3T["vg"])
                        for ic in range(NIC4):
                            vector.wait_ge(p4st.h, 2 * ic + 1)
                            if ic >= 1:
                                vector.wait_ge(p4bc.h, ic)
                            vector.tensor_scalar_mul(out=rows4[:, RM:RM + TS3],
                                                     in0=psSt4A[0:1, :], scalar1=1.0 / 1024)
                            vector.wait_ge(p4st.h, 2 * ic + 2)
                            vector.tensor_scalar(out=rows4[:, RV:RV + TS3],
                                                     in0=psSt4B[0:1, :], scalar1=1.0 / 1024,
                                                     scalar2=EPS, op0=ALU.mult, op1=ALU.add)
                            vector.tensor_mul(out=rows4[:, RMR:RMR + TS3],
                                              in0=rows4[:, RM:RM + TS3], in1=rows4[:, RM:RM + TS3])
                            p4var.inc(vector.tensor_sub(out=rows4[:, RV:RV + TS3],
                                                        in0=rows4[:, RV:RV + TS3],
                                                        in1=rows4[:, RMR:RMR + TS3]))
                            vector.wait_ge(p4rstd.h, ic + 1)
                            p4tm.inc(vector.tensor_mul(out=rows4[:, RMR:RMR + TS3],
                                                       in0=rows4[:, RM:RM + TS3],
                                                       in1=rows4[:, RR:RR + TS3]))
                            vector.wait_ge(p4evb.h, 2 * (ic + 1))
                            for chh in range(8):
                                if ic >= 1 or chh >= 2:
                                    vector.wait_ge(p4xn.h, 8 * ic + chh - 1)
                                p4tm.inc(vector.tensor_mul(
                                    out=tmv4[:, (chh % 2) * TS3:(chh % 2) * TS3 + TS3],
                                    in0=x2o3[:, chh, ic * TS3:(ic + 1) * TS3],
                                    in1=bc4[:, 0:TS3]))
                        for st4 in range(16):
                            vector.wait_ge(p4gp.h, st4 + 1)
                            p4gv.inc(vector.tensor_copy(
                                out=glo4[:, st4 * 40:(st4 + 1) * 40], in_=psGO[:, 0:40]))
                        gv = glo4[:].rearrange("p (s h e) -> p e s h", e=5, h=8)
                        mk_ = mko[:].rearrange("p (s h e) -> p e s h", e=5, h=8)
                        t_ = gsc4[:].rearrange("p (a sh) -> p a sh", a=12)
                        g = [gv[:, e, :, :].rearrange("p s h -> p (s h)") for e in range(5)]
                        def _tt4(o, a, b, op):
                            vector.tensor_tensor(out=o, in0=a, in1=b, op=op)
                        _tt4(t_[:, 0, :], g[0], g[1], ALU.max)
                        _tt4(t_[:, 3, :], g[4], g[3], ALU.max)
                        _tt4(t_[:, 1, :], t_[:, 0, :], g[2], ALU.max)
                        _tt4(t_[:, 4, :], t_[:, 3, :], g[2], ALU.max)
                        _tt4(t_[:, 2, :], t_[:, 1, :], g[3], ALU.max)
                        _tt4(t_[:, 5, :], t_[:, 4, :], g[1], ALU.max)
                        _tt4(t_[:, 6, :], g[0], t_[:, 4, :], ALU.max)
                        _tt4(t_[:, 7, :], t_[:, 0, :], t_[:, 3, :], ALU.max)
                        _tt4(t_[:, 8, :], t_[:, 1, :], g[4], ALU.max)
                        _tt4(t_[:, 5, :], g[0], t_[:, 5, :], ALU.min)
                        _tt4(t_[:, 6, :], g[1], t_[:, 6, :], ALU.min)
                        _tt4(t_[:, 7, :], g[2], t_[:, 7, :], ALU.min)
                        _tt4(t_[:, 8, :], g[3], t_[:, 8, :], ALU.min)
                        _tt4(t_[:, 9, :], g[4], t_[:, 2, :], ALU.min)
                        _tt4(t_[:, 5, :], t_[:, 5, :], t_[:, 6, :], ALU.max)
                        _tt4(t_[:, 7, :], t_[:, 7, :], t_[:, 8, :], ALU.max)
                        _tt4(t_[:, 5, :], t_[:, 5, :], t_[:, 7, :], ALU.max)
                        _tt4(t_[:, 5, :], t_[:, 5, :], t_[:, 9, :], ALU.max)
                        for e in range(5):
                            ins = vector.tensor_tensor(
                                out=mk_[:, e, :, :].rearrange("p s h -> p (s h)"),
                                in0=g[e], in1=t_[:, 5, :], op=ALU.is_ge)
                        p4mk.inc(ins)
                        for st4 in range(16):
                            vector.wait_ge(p4mt.h, st4 + 1)
                            p4mte.inc(vector.tensor_copy(
                                out=maskT[:, st4 * 128:(st4 + 1) * 128], in_=psMT[0:40, :]))

                    @block.scalar
                    def _(scalar):
                        scalar.wait_ge(p3sig.h, 2 * NT3)
                        for ic in range(NIC4):
                            scalar.wait_ge(p4cast.h, ic + 1)
                            if ic >= 1:
                                scalar.wait_ge(p4st.h, 2 * ic)
                            p4sq.inc(scalar.activation(out=xo_sq[:], in_=xo_bf[:, ic % 2, :],
                                                       func=AF.Square))
                            scalar.wait_ge(p4var.h, ic + 1)
                            scalar.activation(out=rows4[:, RR:RR + TS3],
                                              in_=rows4[:, RV:RV + TS3], func=AF.Ln)
                            p4rstd.inc(scalar.activation(out=rows4[:, RR:RR + TS3],
                                                         in_=rows4[:, RR:RR + TS3],
                                                         func=AF.Exp, scale=-0.5))
                            scalar.wait_ge(p4bc.h, ic + 1)
                            if ic >= 1:
                                scalar.wait_ge(p4xn.h, 8 * ic)
                                scalar.wait_ge(p4tm.h, 9 * ic)
                            p4evb.inc(scalar.copy(out=bc4[:, 0:TS3], in_=psBc4[:, 0:TS3]))
                            p4evb.inc(scalar.copy(out=bc4[:, TS3:2 * TS3],
                                                  in_=psBc4[:, TS3:2 * TS3]))
                            for mt in range(4):
                                scalar.wait_ge(p4qp.h, 4 * ic + mt + 1)
                                p4qe.inc(scalar.activation(
                                    out=qT_s[:].rearrange("p (a t) -> p a t", a=4)
                                        [:, mt, ic * TS3:(ic + 1) * TS3],
                                    in_=psQ[mt % 2][:, :], func=AF.Copy, scale=0.125))

                    @block.tensor
                    def _(tensor):
                        tensor.wait_ge(p3st.h, P3T["st"])
                        for ic in range(NIC4):
                            tensor.wait_ge(p4cast.h, ic + 1)
                            if ic >= 1:
                                tensor.wait_ge(p4var.h, ic)
                            for chh in range(8):
                                ins = tensor.matmul(
                                    psSt4A[0:1, :], onesColB[:],
                                    xo_bf[:, ic % 2, :].rearrange("p (ch t) -> p ch t", ch=8)[:, chh, :],
                                    start=(chh == 0), stop=(chh == 7))
                                if chh == 7:
                                    p4st.inc(ins)
                            tensor.wait_ge(p4sq.h, ic + 1)
                            for chh in range(8):
                                ins = tensor.matmul(
                                    psSt4B[0:1, :], onesColB[:],
                                    xo_sq[:].rearrange("p (ch t) -> p ch t", ch=8)[:, chh, :],
                                    start=(chh == 0), stop=(chh == 7))
                                if chh == 7:
                                    p4st.inc(ins)
                            tensor.wait_ge(p4tm.h, 9 * ic + 1)
                            if ic >= 1:
                                tensor.wait_ge(p4evb.h, 2 * ic)
                            tensor.matmul(psBc4[:, 0:TS3], onesRowF[:],
                                          rows4[0:1, RR:RR + TS3], start=True, stop=True)
                            p4bc.inc(tensor.matmul(psBc4[:, TS3:2 * TS3], onesRowF[:],
                                                   rows4[0:1, RMR:RMR + TS3],
                                                   start=True, stop=True))
                            tensor.wait_ge(p4xn.h, 8 * (ic + 1))
                            for mt in range(4):
                                if ic * 4 + mt >= 2:
                                    tensor.wait_ge(p4qe.h, 4 * ic + mt - 1)
                                for dch in range(8):
                                    ins = tensor.matmul(
                                        psQ[mt % 2][:, :],
                                        Wqk[:].rearrange("p (a b) -> p a b", a=8)
                                            [:, dch, mt * 128:(mt + 1) * 128],
                                        xn2o3[:, dch, ic * TS3:(ic + 1) * TS3],
                                        start=(dch == 0), stop=(dch == 7))
                                    if dch == 7:
                                        p4qp.inc(ins)
                        for st4 in range(16):
                            if st4 >= 1:
                                tensor.wait_ge(p4gv.h, st4)
                            tensor.wait_ge(p4xn.h, 8 * (st4 // 2 + 1))
                            for dch in range(8):
                                ins = tensor.matmul(
                                    psGO[:, 0:40],
                                    xn2o3[:, dch, st4 * 128:(st4 + 1) * 128],
                                    Wvsd[:].rearrange("p (a b) -> p a b", a=8)[:, dch, 360:400],
                                    start=(dch == 0), stop=(dch == 7))
                                if dch == 7:
                                    p4gp.inc(ins)
                        tensor.wait_ge(p4mk.h, 1)
                        for st4 in range(16):
                            if st4 >= 1:
                                tensor.wait_ge(p4mte.h, st4)
                            p4mt.inc(tensor.transpose(psMT[0:40, :],
                                                      mko[:, st4 * 40:(st4 + 1) * 40], IDb[:]))
                P4_QE, P4_MTE, P4_XN = p4qe.n, p4mte.n, p4xn.n
                p4s.close()

            a_kk, a_v1, a_qq = sem("a_kk"), sem("a_v1"), sem("a_qq")
            a_qk, a_exp, a_pv = sem("a_qk"), sem("a_exp"), sem("a_pv")
            a_rec, a_bcd, a_anf, a_ann = sem("a_rec"), sem("a_bcd"), sem("a_anf"), sem("a_ann")
            a_sel, a_re, a_rs = sem("a_sel"), sem("a_re"), sem("a_rs")
            a_rsb, a_wo, a_fin = sem("a_rsb"), sem("a_wo"), sem("a_fin")

            p5s = ExitStack()
            _S5 = lambda n, s, d: p5s.enter_context(nc.sbuf_tensor(n, s, d))
            _P5 = lambda n, s, d: p5s.enter_context(nc.psum_tensor(n, s, d))
            if True:
                kk0 = _S5("kk0", [128, T], BF16)
                kk1 = _S5("kk1", [128, T], BF16)
                V650 = _S5("V650", [128, 64 * 66], BF16)
                V651 = _S5("V651", [128, 64 * 66], BF16)
                qq = _S5("qq", [128, REG], BF16)
                pT0 = _S5("pT0", [128, 1024], BF16)
                pT1 = _S5("pT1", [128, 1024], BF16)
                recrow = _S5("recrow", [1, 512], F32)
                attf = _S5("attf", [64, 512], F32)
                attn_n = _S5("attn_n", [64, 512], BF16)
                Rst = _S5("Rst", [128, 2 * 512], BF16)
                RSum = _S5("RSum", [128, 3 * REG], F32)
                RSb = _S5("RSb", [128, 3 * 512], BF16)
                ost = _S5("ost", [128, 2 * 512], F32)
                psS0 = _P5("psS0", [128, 1024], F32)
                psS1 = _P5("psS1", [128, 1024], F32)
                psATT0 = _P5("psATT0", [128, 512], F32)
                psATT1 = _P5("psATT1", [128, 512], F32)
                psBR = _P5("psBR", [128, 512], F32)

                kk = [kk0, kk1]
                V65 = [V650, V651]
                pT = [pT0, pT1]
                psS = [psS0, psS1]
                psATT = [psATT0, psATT1]
                RSum3 = RSum[:].rearrange("p (kc t) -> p kc t", kc=3)

                GS = [(2 * g, 2) for g in range(32)]

                @block.sync
                def _(sync):
                    sync.wait_ge(p4qe.h, P4_QE)
                    sync.wait_ge(p4mte.h, P4_MTE)
                    for h in range(8):
                        if h >= 2:
                            sync.wait_ge(a_qk.h, 256 * (h - 1))
                        a_kk.dma(sync.dma_start(out=kk[h % 2][0:64, :],
                                                in_=kTD[h * 64:(h + 1) * 64, :]))
                        a_kk.dma(sync.dma_start(out=kk[h % 2][64:128, :],
                                                in_=kTD[h * 64:(h + 1) * 64, :]))
                        if h >= 2:
                            sync.wait_ge(a_pv.h, NG * 4 * (h - 1))
                        a_kk.dma(sync.dma_start(
                            out=V65[h % 2][:].rearrange("p (t c) -> p t c", c=66)[:, :, 0:64],
                            in_=vD.rearrange("(t p) d -> p t d", p=128)[:, :, h * 64:(h + 1) * 64]))

                @block.vector
                def _(vector):
                    vector.wait_ge(p4mte.h, P4_MTE)
                    vector.wait_ge(p4qe.h, P4_QE)
                    a_v1.inc(vector.memset(
                        V650[:].rearrange("p (t c) -> p t c", c=66)[:, :, 64:65], 1.0))
                    a_v1.inc(vector.memset(
                        V651[:].rearrange("p (t c) -> p t c", c=66)[:, :, 64:65], 1.0))
                    for h in range(8):
                        vector.wait_ge(a_qk.h, 256 * h)
                        a_qq.inc(vector.tensor_copy(
                            out=qq[0:64, :],
                            in_=qT_s[(h % 2) * 64:(h % 2) * 64 + 64,
                                     (h // 2) * REG:(h // 2 + 1) * REG]))
                        a_qq.inc(vector.tensor_copy(
                            out=qq[64:128, :],
                            in_=qT_s[(h % 2) * 64:(h % 2) * 64 + 64,
                                     (h // 2) * REG:(h // 2 + 1) * REG]))
                        for ic in range(4):
                            hic = h * 4 + ic
                            vector.wait_ge(a_pv.h, NG * (hic + 1))
                            if hic >= 1:
                                vector.wait_ge(a_bcd.h, hic)
                            a_rec.inc(vector.reciprocal(out=recrow[:], in_=psATT[hic % 2][64:65, :]))
                            vector.wait_ge(a_anf.h, hic + 1)
                            vector.wait_ge(a_bcd.h, hic + 1)
                            a_ann.inc(vector.tensor_mul(out=attn_n[:], in0=psBR[0:64, :],
                                                        in1=attf[:]))
                            for e in range(5):
                                vector.wait_ge(a_sel.h, 5 * hic + e + 1)
                                gp = 3 * hic + e // 2
                                if gp >= 2:
                                    vector.wait_ge(a_rs.h, gp + 2)
                                sl = ((e // 2) % 2) * 512
                                rb = 64 * (e % 2)
                                a_re.inc(vector.tensor_mul(
                                    out=Rst[rb:rb + 64, sl:sl + 512],
                                    in0=psBR[64:128, :], in1=attn_n[:]))

                @block.scalar
                def _(scalar):
                    for h in range(8):
                        for ic in range(4):
                            hic = h * 4 + ic
                            for g, (jp0, ng) in enumerate(GS):
                                gidx = hic * NG + g
                                scalar.wait_ge(a_qk.h, hic * 64 + jp0 + ng)
                                if gidx >= 2:
                                    scalar.wait_ge(a_pv.h, gidx - 1)
                                a_exp.inc(scalar.activation(
                                    out=pT[gidx % 2][:, 0:ng * 512],
                                    in_=psS[gidx % 2][:, 0:ng * 512], func=AF.Exp))
                            scalar.wait_ge(a_pv.h, NG * (hic + 1))
                            if hic >= 1:
                                scalar.wait_ge(a_ann.h, hic)
                            a_anf.inc(scalar.copy(out=attf[:], in_=psATT[hic % 2][0:64, :]))

                @block.tensor
                def _(tensor):
                    tensor.wait_ge(a_v1.h, 2)
                    for h in range(8):
                        tensor.wait_ge(a_kk.h, 48 * h + 32)
                        tensor.wait_ge(a_qq.h, 2 * (h + 1))
                        for ic in range(4):
                            hic = h * 4 + ic
                            for g, (jp0, ng) in enumerate(GS):
                                gidx = hic * NG + g
                                if gidx >= 2:
                                    tensor.wait_ge(a_exp.h, gidx - 1)
                                for jj in range(ng):
                                    jp = jp0 + jj
                                    jb = jp * 128
                                    tensor.matmul(
                                        psS[gidx % 2][0:64, jj * 512:(jj + 1) * 512],
                                        kk[h % 2][0:64, jb:jb + 64],
                                        qq[0:64, ic * 512:(ic + 1) * 512],
                                        start=True, stop=True, tile_position=(0, 0))
                                    ins = tensor.matmul(
                                        psS[gidx % 2][64:128, jj * 512:(jj + 1) * 512],
                                        kk[h % 2][64:128, jb + 64:jb + 128],
                                        qq[64:128, ic * 512:(ic + 1) * 512],
                                        start=True, stop=True, tile_position=(64, 64))
                                    a_qk.inc(ins)
                                if g >= 1:
                                    gprev = gidx - 1
                                    jpp, ngp = GS[g - 1]
                                    tensor.wait_ge(a_exp.h, gprev + 1)
                                    if g == 1:
                                        tensor.wait_ge(a_kk.h, 48 * (h + 1))
                                        if hic >= 2:
                                            tensor.wait_ge(a_rec.h, hic - 1)
                                            tensor.wait_ge(a_anf.h, hic - 1)
                                    for jj in range(ngp):
                                        jp = jpp + jj
                                        ins = tensor.matmul(
                                            psATT[hic % 2][0:65, :],
                                            V65[h % 2][:, jp * 66:jp * 66 + 65],
                                            pT[gprev % 2][:, jj * 512:(jj + 1) * 512],
                                            start=(jp == 0), stop=(jp == 63))
                                        if jj == ngp - 1:
                                            a_pv.inc(ins)
                            gidx = hic * NG + NG - 1
                            jpp, ngp = GS[NG - 1]
                            tensor.wait_ge(a_exp.h, gidx + 1)
                            for jj in range(ngp):
                                jp = jpp + jj
                                ins = tensor.matmul(
                                    psATT[hic % 2][0:65, :],
                                    V65[h % 2][:, jp * 66:jp * 66 + 65],
                                    pT[gidx % 2][:, jj * 512:(jj + 1) * 512],
                                    start=(jp == 0), stop=(jp == 63))
                                if jj == ngp - 1:
                                    a_pv.inc(ins)
                            tensor.wait_ge(a_rec.h, hic + 1)
                            if hic >= 1:
                                tensor.wait_ge(a_ann.h, hic)
                            a_bcd.inc(tensor.matmul(psBR[0:64, :], ones64F[:],
                                                    recrow[:], start=True, stop=True))
                            for e in range(5):
                                tensor.wait_ge(a_re.h, 5 * hic + e)
                                if e == 0:
                                    tensor.wait_ge(a_ann.h, hic + 1)
                                a_sel.inc(tensor.matmul(
                                    psBR[64:128, :],
                                    SELb[0:40, (h * 5 + e) * 64:(h * 5 + e) * 64 + 64],
                                    maskT[0:40, ic * 512:(ic + 1) * 512],
                                    start=True, stop=True, tile_position=(0, 64)))
                    for ic in range(4):
                        tensor.wait_ge(a_rsb.h, 3 * (ic + 1))
                        for dch in range(8):
                            gi = ic * 8 + dch
                            tensor.wait_ge(a_fin.h, gi)
                            for kc in range(3):
                                ins = tensor.matmul(
                                    psATT[0][:, :],
                                    Wo3[:, kc * 1024 + dch * 128:kc * 1024 + (dch + 1) * 128],
                                    RSb[:, kc * 512:(kc + 1) * 512],
                                    start=(kc == 0), stop=(kc == 2))
                                if kc == 2:
                                    a_wo.inc(ins)

                @block.gpsimd
                def _(gpsimd):
                    gpsimd.wait_ge(p4mte.h, P4_MTE)
                    for kc in range(3):
                        ins = gpsimd.memset(RSum3[:, kc, :], 0.0)
                        a_rs.inc(ins)
                    for h in range(8):
                        for ic in range(4):
                            hic = h * 4 + ic
                            for pi in range(3):
                                last_e = (pi * 2 + 1) if pi < 2 else 4
                                rows = 128 if pi < 2 else 64
                                gpsimd.wait_ge(a_re.h, 5 * hic + last_e + 1)
                                sl = (pi % 2) * 512
                                a_rs.inc(gpsimd.tensor_add(
                                    out=RSum3[0:rows, pi, ic * 512:(ic + 1) * 512],
                                    in0=RSum3[0:rows, pi, ic * 512:(ic + 1) * 512],
                                    in1=Rst[0:rows, sl:sl + 512]))
                    for ic in range(4):
                        for dch in range(8):
                            gi = ic * 8 + dch
                            gpsimd.wait_ge(a_fin.h, gi + 1)
                            dF.dma(gpsimd.dma_start(
                                out=outT3[:, dch, ic * 512:(ic + 1) * 512],
                                in_=ost[:, (gi % 2) * 512:(gi % 2) * 512 + 512]))
                    gpsimd.wait_ge(dF.h, 16 * 32)

                @block.scalar
                def _(scalar):
                    scalar.wait_ge(a_rs.h, 3 + 96)
                    for ic in range(4):
                        if ic >= 1:
                            scalar.wait_ge(a_wo.h, 8 * ic)
                        for kc in range(3):
                            a_rsb.inc(scalar.copy(out=RSb[:, kc * 512:(kc + 1) * 512],
                                                  in_=RSum3[:, kc, ic * 512:(ic + 1) * 512]))

                @block.vector
                def _(vector):
                    for ic in range(4):
                        for dch in range(8):
                            gi = ic * 8 + dch
                            vector.wait_ge(a_wo.h, gi + 1)
                            if gi >= 2:
                                vector.wait_ge(dF.h, 16 * (gi - 1))
                            a_fin.inc(vector.tensor_add(
                                out=ost[:, (gi % 2) * 512:(gi % 2) * 512 + 512],
                                in0=psATT[0][:, :],
                                in1=x2own[:].rearrange("p (ch t) -> p ch t", ch=8)
                                    [:, dch, ic * 512:(ic + 1) * 512]))
            p5s.close()
    return nc


_PCOL_TOK = None


def _tok_of_col():
    global _PCOL_TOK
    if _PCOL_TOK is None:
        j = np.arange(T)
        sp, c = j // 128, j % 128
        _PCOL_TOK = c * C + sp
    return _PCOL_TOK


def kernel(L0_attn_x, L0_attn_Wi, L0_attn_bi, L0_attn_Wh, L0_attn_bh,
           L0_ffn_Wq, L0_ffn_Wk, L0_ffn_Ws, L0_ffn_Wd, L0_ffn_Wv, L0_ffn_Wo):
    x = np.asarray(L0_attn_x, dtype=np.float32)
    Wi = np.asarray(L0_attn_Wi, dtype=np.float32)
    Wh = np.asarray(L0_attn_Wh, dtype=np.float32)
    biv = np.ascontiguousarray(np.asarray(L0_attn_bi, dtype=np.float32).reshape(8, 128).T)
    bhv = np.ascontiguousarray(np.asarray(L0_attn_bh, dtype=np.float32).reshape(8, 128).T)
    Wq = np.ascontiguousarray(np.asarray(L0_ffn_Wq, dtype=np.float32))
    Wk = np.ascontiguousarray(np.asarray(L0_ffn_Wk, dtype=np.float32))
    Ws = np.ascontiguousarray(np.asarray(L0_ffn_Ws, dtype=np.float32))
    Wd = np.ascontiguousarray(np.asarray(L0_ffn_Wd, dtype=np.float32))
    Wv = np.asarray(L0_ffn_Wv, dtype=np.float32)
    Wo = np.asarray(L0_ffn_Wo, dtype=np.float32)

    tok = _tok_of_col()
    xP = [np.ascontiguousarray(x[b].T[:, tok]) for b in range(B)]
    WiT = np.ascontiguousarray(Wi.T)
    WhT = np.ascontiguousarray(Wh.T)
    Wvf = np.ascontiguousarray(Wv.transpose(1, 0, 2).reshape(D, E * DH))
    WoS = np.ascontiguousarray(Wo.reshape(E * DH, D))
    selc = np.zeros((40, 2560), np.float32)
    for r in range(40):
        selc[r, r * 64:(r + 1) * 64] = 1.0
    idt = np.eye(128, dtype=np.float32)

    nc = build_nc()
    in_maps = []
    for k in range(8):
        b, q = k // 4, k % 4
        in_maps.append({
            "xT": xP[b],
            "xTown": np.ascontiguousarray(xP[b][:, q * REG:(q + 1) * REG]),
            "qoff": np.array([[q * REG]], np.int32),
            "WiT": WiT, "WhT": WhT, "Wq": Wq, "Wk": Wk, "Ws": Ws, "Wd": Wd,
            "Wvf": Wvf, "WoS": WoS, "bi": biv, "bh": bhv,
            "selc": selc, "idt": idt,
        })
    res = run_bass_kernel_spmd(nc, in_maps, core_ids=list(range(8)))
    out = np.empty((B, T, D), np.float32)
    for k in range(8):
        b, q = k // 4, k % 4
        cols = tok[q * REG:(q + 1) * REG]
        out[b, cols, :] = res.results[k]["outT"].T
    return out
